# version 22
# baseline (speedup 1.0000x reference)
import numpy as np
from contextlib import ExitStack

import concourse.bass as bass
import concourse.mybir as mybir
from concourse.bass_utils import run_bass_kernel_spmd

F32 = mybir.dt.float32
BF16 = mybir.dt.bfloat16
AF = mybir.ActivationFunctionType
ALU = mybir.AluOpType
AX = mybir.AxisListType

S = 4096
C = 1024
D = 64
TOWN = 16
TT = 32
CT = 8
KT = 32
EPS = 1e-5
RC = 1.0 / C

_PROGRAM_CACHE = {}


def build_program(debug=False):
    nc = bass.Bass("TRN2", target_bir_lowering=False)

    x_in = nc.dram_tensor("x_in", [S, C], F32, kind="ExternalInput").ap()
    wqk_in = nc.dram_tensor("wqk_in", [C, 128], F32, kind="ExternalInput").ap()
    wkq_in = nc.dram_tensor("wkq_in", [C, 128], F32, kind="ExternalInput").ap()
    wvv_in = nc.dram_tensor("wvv_in", [C, 128], F32, kind="ExternalInput").ap()
    wrT_in = nc.dram_tensor("wrT_in", [C, C], F32, kind="ExternalInput").ap()
    wkT_in = nc.dram_tensor("wkT_in", [C, C], F32, kind="ExternalInput").ap()
    wvT_in = nc.dram_tensor("wvT_in", [C, C], F32, kind="ExternalInput").ap()
    mu_in = nc.dram_tensor("mu_in", [128, 16], F32, kind="ExternalInput").ap()
    hmask_in = nc.dram_tensor("hmask_in", [1, 1], F32, kind="ExternalInput").ap()
    ident_in = nc.dram_tensor("ident_in", [128, 128], F32, kind="ExternalInput").ap()
    y_out = nc.dram_tensor("y_out", [2048, C], F32, kind="ExternalOutput").ap()

    if debug:
        d_z = nc.dram_tensor("d_z", [S, C], F32, kind="ExternalOutput").ap()
        d_qk = nc.dram_tensor("d_qk", [256, S], F32, kind="ExternalOutput").ap()
        d_v = nc.dram_tensor("d_v", [128, KT * D], F32, kind="ExternalOutput").ap()
        d_out = nc.dram_tensor("d_out", [64, 2176], F32, kind="ExternalOutput").ap()
        d_ell = nc.dram_tensor("d_ell", [128, 2 * KT], F32, kind="ExternalOutput").ap()
        d_tok = nc.dram_tensor("d_tok", [128, TOWN * D + D], F32, kind="ExternalOutput").ap()
        d_z1 = nc.dram_tensor("d_z1", [2080, C], F32, kind="ExternalOutput").ap()
        d_x1 = nc.dram_tensor("d_x1", [2048, C], F32, kind="ExternalOutput").ap()
        d_r = nc.dram_tensor("d_r", [128, TOWN * C], F32, kind="ExternalOutput").ap()
        d_ka = nc.dram_tensor("d_ka", [128, CT * 2048], F32, kind="ExternalOutput").ap()
    zdram = nc.dram_tensor("zdram", [S, C], BF16).ap()
    z1dram = nc.dram_tensor("z1dram", [2080, C], BF16).ap()
    x1dram = nc.dram_tensor("x1dram", [2048, C], F32).ap()
    xrow_d = nc.dram_tensor("xrow_d", [1, D], F32).ap()
    vdram = nc.dram_tensor("vdram", [64, S], BF16).ap()

    ctx = ExitStack()
    e = ctx.enter_context
    with ctx:
        wqk = e(nc.sbuf_tensor("wqk", [128, 8 * 128], BF16))
        wkq = e(nc.sbuf_tensor("wkq", [128, 8 * 128], BF16))
        wvv = e(nc.sbuf_tensor("wvv", [128, 8 * 128], BF16))
        wrT = e(nc.sbuf_tensor("wrT", [128, 8 * C], BF16))
        wkT = e(nc.sbuf_tensor("wkT", [128, 8 * C], BF16))
        wvT = e(nc.sbuf_tensor("wvT", [128, 8 * C], BF16))
        mu = e(nc.sbuf_tensor("mu", [128, 16], F32))
        mu1m = e(nc.sbuf_tensor("mu1m", [128, 16], F32))
        onec = e(nc.sbuf_tensor("onec", [128, 16], F32))
        hmask = e(nc.sbuf_tensor("hmask", [1, 1], F32))
        ident = e(nc.sbuf_tensor("ident", [128, 128], F32))

        cconst = e(nc.sbuf_tensor("cconst", [128, 4], F32))
        xst = e(nc.sbuf_tensor("xst", [128, 8 * C], F32))
        zst = e(nc.sbuf_tensor("zst", [128, 8 * C], BF16))
        sums = e(nc.sbuf_tensor("sums", [128, 8], F32))
        sumsq = e(nc.sbuf_tensor("sumsq", [128, 8], F32))
        smean = e(nc.sbuf_tensor("smean", [128, 8], F32))
        svar = e(nc.sbuf_tensor("svar", [128, 8], F32))
        sstd = e(nc.sbuf_tensor("sstd", [128, 8], F32))
        srstd = e(nc.sbuf_tensor("srstd", [128, 8], F32))
        snmr = e(nc.sbuf_tensor("snmr", [128, 8], F32))

        rowf = e(nc.sbuf_tensor("rowf", [128, C], F32))
        xlast = rowf[0:1, :]
        x1x = rowf[32:33, :]
        z1x = zst[0:1, C:2 * C]
        z1xm = zst[32:33, C:2 * C]
        jrow = zst[64:65, C:2 * C]
        outx_row = e(nc.sbuf_tensor("outx_row", [1, D], F32))
        rs = e(nc.sbuf_tensor("rs", [1, 12], F32))

        def opens(cms):
            return [cm.__enter__() for cm in cms]

        def closes(cms):
            for cm in reversed(cms):
                cm.__exit__(None, None, None)

        zT_cm = [nc.sbuf_tensor(f"zT{i}", [128, S], BF16, side="right") for i in range(CT)]
        zT = opens(zT_cm)
        attn_cm = [
            nc.sbuf_tensor("qrep", [128, S], BF16),
            nc.sbuf_tensor("krep", [128, S], BF16),
            nc.sbuf_tensor("v_sb", [128, KT * D], BF16),
            nc.sbuf_tensor("p_sb0", [128, S], BF16),
            nc.sbuf_tensor("negm", [128, KT], F32),
            nc.sbuf_tensor("ellp", [128, 16], F32),
            nc.sbuf_tensor("ell", [128, KT], F32),
            nc.sbuf_tensor("rell", [128, KT], F32),
        ]
        (qrep, krep, v_sb, p_sb0, negm, ellp, ell, rell) = opens(attn_cm)
        vs_sb = v_sb
        vT_sb = p_sb0
        p_sb = [p_sb0]

        SW = e(nc.semaphore("SW"))
        SX = e(nc.semaphore("SX"))
        SD = e(nc.semaphore("SD"))
        DSs = [e(nc.semaphore(f"DS{i}")) for i in range(8)]
        ZSs = [e(nc.semaphore(f"ZS{i}")) for i in range(8)]
        XSs = [e(nc.semaphore(f"XS{i}")) for i in range(8)]
        YSs = [e(nc.semaphore(f"YS{i}")) for i in range(2)]
        SVT = e(nc.semaphore("SVT"))
        SH = e(nc.semaphore("SH"))
        SWB = e(nc.semaphore("SWB"))
        FSs = [e(nc.semaphore(f"FS{i}")) for i in range(8)]
        ZAs = [e(nc.semaphore(f"ZA{i}")) for i in range(8)]
        SZQ = [e(nc.semaphore(f"SZQ{i}")) for i in range(4)]
        SG_LO = e(nc.semaphore("SG_LO"))
        SG_HI = e(nc.semaphore("SG_HI"))
        SM = e(nc.semaphore("SM"))
        SV = e(nc.semaphore("SV"))
        SS = e(nc.semaphore("SS"))
        SY = e(nc.semaphore("SY"))

        block = e(nc.Block())

        fs_f = lambda t: 16 * (t // 8 + 1)
        fs_k = lambda t: 32 + 16 * (t // 8 + 1)
        zs_f = lambda t: 16 * (t // 8 + 1)
        xs_f = lambda t: 16 * (t // 8 + 1)
        ys_k = lambda t: 16 * (t // 2 + 1)
        def plan_ln(n_tiles, sv0, extra_per_tile):
            ng = n_tiles // 4
            cnt = {}
            sv = sv0
            for g in range(ng):
                for s4 in range(4):
                    i = 4 * g + s4
                    for op in extra_per_tile:
                        sv += 1
                        cnt[(op, i)] = sv
                    sv += 1
                    cnt[("sum", i)] = sv
                if g >= 1:
                    for op in ("smean", "svar", "msq", "varsub", "recip", "nmr"):
                        sv += 1
                        cnt[(op, g - 1)] = sv
                    for s4 in range(4):
                        sv += 1
                        cnt[("apply", 4 * (g - 1) + s4)] = sv
            for op in ("smean", "svar", "msq", "varsub", "recip", "nmr"):
                sv += 1
                cnt[(op, ng - 1)] = sv
            for s4 in range(4):
                sv += 1
                cnt[("apply", 4 * (ng - 1) + s4)] = sv
            return cnt, sv

        def plan_ln_af(n_tiles, sv0):
            ng = n_tiles // 4
            cnt = {}
            sv = sv0
            for s4 in range(4):
                sv += 1
                cnt[("sum", s4)] = sv
            for g in range(1, ng + 1):
                gm = g - 1
                for op in ("smean", "svar", "msq", "varsub", "recip", "nmr"):
                    sv += 1
                    cnt[(op, gm)] = sv
                for s4 in range(4):
                    sv += 1
                    cnt[("apply", 4 * gm + s4)] = sv
                if g < ng:
                    for s4 in range(4):
                        sv += 1
                        cnt[("sum", 4 * g + s4)] = sv
            return cnt, sv

        A_CNT, SV_A = plan_ln_af(TT, 0)
        a_square = lambda i: 5 * (i // 4) + (i % 4) + 1
        a_sqrt = lambda g: 5 * g + 5
        SS_A = 5 * 8
        a_ds = lambda i: 16 * (i // 8 + 1)
        a_za = lambda i: 16 * (i // 8 + 1)
        SM_C0, SV_C0 = 0, SV_A
        c_qkmm = lambda n: SM_C0 + n + 1
        c_vmm = lambda n: SM_C0 + 8 + n + 1
        c_qkev = lambda n: SV_C0 + 4 * n + 4
        c_vev = lambda n: SV_C0 + 32 + n + 1
        SM_C = SM_C0 + 16
        SV_C = SV_C0 + 40
        SM_D0, SS_D0, SV_D0 = SM_C, SS_A, SV_C
        d_P = lambda q: 0 if q == 0 else 24 * q - 8
        d_smm = lambda kt, c: (SM_D0 + d_P(kt // 2) + 4 * (c // 2) + 2 * (kt % 2)
                               + (c % 2) + 1)
        d_eidx = lambda kt, s: 2 * (4 * (kt // 2) + s) + (kt % 2)

        def d_einv(e):
            sig, a = e // 2, e % 2
            return 2 * (sig // 4) + a, sig % 4

        def d_outmm(t):
            q = t // 2
            if q <= 13:
                return SM_D0 + d_P(q + 2) - 4 * (1 - (t % 2))
            return SM_D0 + 368 + 8 * (q - 14) + 4 * (t % 2 + 1)
        d_exp = lambda kt, s: SS_D0 + 8 * (kt // 2) + 2 * s + (kt % 2) + 1
        D_CNT = {}
        _sv = SV_D0
        for _kt in range(4):
            _sv += 1
            D_CNT[("max", _kt)] = _sv
        for _q in range(16):
            for _a in range(2):
                _kt = 2 * _q + _a
                for _op in ("ellsum", "recip", "vsc", "xp"):
                    _sv += 1
                    D_CNT[(_op, _kt)] = _sv
            if _q + 2 < 16:
                for _a in range(2):
                    _sv += 1
                    D_CNT[("max", 2 * (_q + 2) + _a)] = _sv
        d_max = lambda kt: D_CNT[("max", kt)]
        d_ellsum = lambda kt: D_CNT[("ellsum", kt)]
        d_recip = lambda kt: D_CNT[("recip", kt)]
        d_vsc = lambda kt: D_CNT[("vsc", kt)]
        d_xp = lambda kt: D_CNT[("xp", kt)]
        SM_D = SM_D0 + 384
        SS_D = SS_D0 + 128
        SV_D = _sv
        assert SV_D == SV_D0 + 160
        e_evac = SV_D + 2
        e_tpmm = lambda b: SM_D + b + 1
        e_oxmm = SM_D + 16 + KT
        e_tpev = lambda b: e_evac + b + 1
        e_oxev = e_evac + 17
        SM_E = e_oxmm
        SV_E = e_oxev
        SV_F0, SS_F0 = SV_E, SS_D
        SV_TAIL = SV_F0 + 11
        SS_TAIL = SS_F0 + 2

        def plan_ln_af2(n_tiles, sv0):
            ng = n_tiles // 4
            cnt = {}
            sv = sv0
            for s4 in range(4):
                sv += 1
                cnt[("x1", s4)] = sv
                sv += 1
                cnt[("sum", s4)] = sv
            for g in range(1, ng + 1):
                gm = g - 1
                for op in ("smean", "svar", "msq", "varsub", "recip", "nmr"):
                    sv += 1
                    cnt[(op, gm)] = sv
                for s4 in range(4):
                    sv += 1
                    cnt[("apply", 4 * gm + s4)] = sv
                if g < ng:
                    for s4 in range(4):
                        i = 4 * g + s4
                        sv += 1
                        cnt[("x1", i)] = sv
                        sv += 1
                        cnt[("sum", i)] = sv
            return cnt, sv

        F_CNT, SV_Floop = plan_ln_af2(TOWN, SV_TAIL)
        f_square = lambda t: SS_TAIL + 5 * (t // 4) + (t % 4) + 1
        f_sqrt = lambda g: SS_TAIL + 5 * g + 5
        SV_F = SV_Floop
        SS_F = SS_TAIL + 20
        SD_E0 = 0
        sd_xrowd = SD_E0 + 16
        sd_outxrow = SD_E0 + 32
        sd_xlast = SD_E0 + 48
        sd_row63 = SD_E0 + 64
        sd_zero1 = SD_E0 + 80
        sd_zero2 = SD_E0 + 96
        SD_F = sd_zero2
        szt_g_hi = lambda ct: 16 * (ct + 1)
        SS_H0 = SS_F
        h_tsr = lambda ct: SS_H0 + ct + 1
        h_tsk = lambda ct: SS_H0 + 8 + ct + 1
        SS_H = SS_H0 + 16
        SV_H0 = SV_F + 1
        h_xr_all = SV_H0 + 8
        SV_H = SV_H0 + 16
        SH_ALL = 0
        SM_I0, SS_I0 = SM_E, SS_H
        i_mm = lambda t: SM_I0 + t + 1
        i_sig = lambda t: SS_I0 + t + 1
        SM_I = SM_I0 + 16
        SS_I = SS_I0 + 16
        SM_J0, SS_J0, SV_J0 = SM_I, SS_I, SV_H
        j_mm = lambda ix: SM_J0 + ix + 1
        j_relu = lambda ix: SS_J0 + ix + 1
        j_sq = lambda ix: SV_J0 + ix + 1
        SM_J = SM_J0 + 32
        SS_J = SS_J0 + 32
        SV_J = SV_J0 + 32
        SM_K0, SV_K0 = SM_J, SV_J
        k_mm = lambda t: SM_K0 + t + 1
        k_tmp = lambda t: SV_K0 + 3 * t + 1
        k_mid = lambda t: SV_K0 + 3 * t + 2
        k_x2 = lambda t: SV_K0 + 3 * t + 3
        k_ydma = lambda t: 16 * (t + 1)

        @block.gpsimd
        def _(g):
            g.dma_start(out=wqk[:, :].rearrange("p (ct d) -> p ct d", d=128),
                        in_=wqk_in.rearrange("(ct p) d -> p ct d", p=128)).then_inc(SW, 16)
            g.dma_start(out=wkq[:, :].rearrange("p (ct d) -> p ct d", d=128),
                        in_=wkq_in.rearrange("(ct p) d -> p ct d", p=128)).then_inc(SW, 16)
            g.dma_start(out=wvv[:, :].rearrange("p (ct d) -> p ct d", d=128),
                        in_=wvv_in.rearrange("(ct p) d -> p ct d", p=128)).then_inc(SW, 16)
            g.dma_start(out=mu[:, :], in_=mu_in).then_inc(SW, 16)
            g.dma_start(out=hmask[:, :], in_=hmask_in).then_inc(SW, 16)
            g.dma_start(out=ident[:, :], in_=ident_in).then_inc(SW, 16)
        SW_SMALL = 96
        SWB_ALL = 48

        @block.sync
        def _(sy):
            for i in range(TT):
                s = i % 8
                if i >= 8:
                    sy.wait_ge(SV, A_CNT[("apply", i - 8)])
                    sy.wait_ge(SS, a_square(i - 8))
                sy.dma_start(out=xst[:, s * C:(s + 1) * C],
                             in_=x_in[i * 128:(i + 1) * 128, :]).then_inc(DSs[s], 16)
                if i >= 8:
                    j = i - 8
                    sy.dma_start(out=zdram[j * 128:(j + 1) * 128, :],
                                 in_=zst[:, (j % 8) * C:(j % 8 + 1) * C]).then_inc(ZAs[j % 8], 16)
            for j in range(TT - 8, TT):
                sy.wait_ge(SV, A_CNT[("apply", j)])
                sy.dma_start(out=zdram[j * 128:(j + 1) * 128, :],
                             in_=zst[:, (j % 8) * C:(j % 8 + 1) * C]).then_inc(ZAs[j % 8], 16)

        @block.scalar
        def _(sc):
            for g in range(8):
                gb = (g % 2) * 4
                for s in range(4):
                    i = g * 4 + s
                    sc.wait_ge(DSs[i % 8], a_ds(i))
                    sc.wait_ge(SS, a_square(i) - 1)
                    if i >= 8:
                        sc.wait_ge(ZAs[i % 8], a_za(i - 8))
                    if g >= 2:
                        sc.wait_ge(SV, A_CNT[("svar", g - 2)])
                    sc.activation(zst[:, (i % 8) * C:(i % 8 + 1) * C],
                                  xst[:, (i % 8) * C:(i % 8 + 1) * C],
                                  AF.Square,
                                  accum_out=sumsq[:, i % 8:i % 8 + 1]).then_inc(SS, 1)
                sc.wait_ge(SV, A_CNT[("varsub", g)])
                if g >= 2:
                    sc.wait_ge(SV, A_CNT[("recip", g - 2)])
                sc.activation(sstd[:, gb:gb + 4], svar[:, gb:gb + 4], AF.Sqrt,
                              bias=cconst[:, 0:1]).then_inc(SS, 1)
                if g % 2 == 1:
                    qtr = (g - 1) // 2
                    for s in range(8):
                        sc.wait_ge(ZAs[s], 16 * (qtr + 1))
                    for ctn in range(CT):
                        sc.dma_start(out=zT[ctn][:, qtr * 1024:(qtr + 1) * 1024],
                                     in_=zdram[qtr * 1024:(qtr + 1) * 1024,
                                               ctn * 128:(ctn + 1) * 128],
                                     transpose=True).then_inc(SZQ[qtr], 16)

        def emit_ln_fin(v, g, cnt, sq_last, sq_grp):
            gb = (g % 2) * 4
            cb = slice(4 * g % 8, 4 * g % 8 + 4)
            v.wait_ge(SS, sq_last)
            v.wait_ge(SV, cnt[("sum", 4 * g + 3)])
            v.tensor_scalar(smean[:, gb:gb + 4], sums[:, cb], cconst[:, 2:3],
                            None, ALU.mult).then_inc(SV, 1)
            v.tensor_scalar(svar[:, gb:gb + 4], sumsq[:, cb], cconst[:, 2:3],
                            None, ALU.mult).then_inc(SV, 1)
            v.wait_ge(SV, cnt[("smean", g)])
            v.tensor_tensor(sstd[:, gb:gb + 4], smean[:, gb:gb + 4],
                            smean[:, gb:gb + 4], ALU.mult).then_inc(SV, 1)
            v.wait_ge(SV, cnt[("msq", g)])
            v.tensor_tensor(svar[:, gb:gb + 4], svar[:, gb:gb + 4],
                            sstd[:, gb:gb + 4], ALU.subtract).then_inc(SV, 1)
            v.wait_ge(SS, sq_grp)
            v.reciprocal(srstd[:, gb:gb + 4], sstd[:, gb:gb + 4]).then_inc(SV, 1)
            v.wait_ge(SV, cnt[("recip", g)])
            v.scalar_tensor_tensor(snmr[:, gb:gb + 4], smean[:, gb:gb + 4],
                                   cconst[:, 1:2], srstd[:, gb:gb + 4],
                                   ALU.mult, ALU.mult).then_inc(SV, 1)

        @block.vector
        def _(v):
            v.memset(cconst[:, 0:1], EPS)
            v.memset(cconst[:, 1:2], -1.0)
            v.memset(cconst[:, 2:3], RC)
            v.memset(onec[:, :], 1.0)

            def a_apply_emit(v, g):
                gb = (g % 2) * 4
                for s in range(4):
                    i = g * 4 + s
                    if i >= 8:
                        v.wait_ge(ZAs[i % 8], a_za(i - 8))
                    v.wait_ge(SV, A_CNT[("nmr", g)])
                    v.tensor_scalar(zst[:, (i % 8) * C:(i % 8 + 1) * C],
                                    xst[:, (i % 8) * C:(i % 8 + 1) * C],
                                    srstd[:, gb + s:gb + s + 1],
                                    snmr[:, gb + s:gb + s + 1],
                                    ALU.mult, ALU.add).then_inc(SV, 1)

            def a_sums_emit(v, g):
                for s in range(4):
                    i = g * 4 + s
                    v.wait_ge(DSs[i % 8], a_ds(i))
                    if g >= 2:
                        v.wait_ge(SV, A_CNT[("smean", g - 2)])
                    v.reduce_sum(sums[:, i % 8:i % 8 + 1],
                                 xst[:, (i % 8) * C:(i % 8 + 1) * C],
                                 axis=AX.X).then_inc(SV, 1)

            a_sums_emit(v, 0)
            for g in range(1, 8):
                emit_ln_fin(v, g - 1, A_CNT, a_square(4 * (g - 1) + 3),
                            a_sqrt(g - 1))
                a_apply_emit(v, g - 1)
                a_sums_emit(v, g)
            emit_ln_fin(v, 7, A_CNT, a_square(31), a_sqrt(7))
            a_apply_emit(v, 7)


        cm_qk = [nc.psum_tensor(f"qk{i}", [128, 512], F32) for i in range(4)]
        qk = opens(cm_qk)
        cm_pvv = [nc.psum_tensor(f"pvv{i}", [128, 512], F32) for i in range(2)]
        pvv = opens(cm_pvv)

        @block.tensor
        def _(t):
            t.wait_ge(SW, SW_SMALL)
            for nch in range(8):
                buf = nch % 2
                t.wait_ge(SZQ[nch // 2], 128)
                if nch >= 2:
                    t.wait_ge(SV, c_qkev(nch - 2))
                for ct in range(CT):
                    t.matmul(qk[buf][:, :], wqk[:, ct * 128:(ct + 1) * 128],
                             zT[ct][:, nch * 512:(nch + 1) * 512],
                             start=(ct == 0), stop=(ct == CT - 1),
                             skip_group_check=True)
                for ct in range(CT):
                    ins = t.matmul(qk[2 + buf][:, :], wkq[:, ct * 128:(ct + 1) * 128],
                                   zT[ct][:, nch * 512:(nch + 1) * 512],
                                   start=(ct == 0), stop=(ct == CT - 1),
                                   skip_group_check=True)
                ins.then_inc(SM, 1)
            for nch in range(8):
                t.wait_ge(SZQ[nch // 2], 128)
                if nch >= 2:
                    t.wait_ge(SV, c_vev(nch - 2))
                for ct in range(CT):
                    ins = t.matmul(pvv[nch % 2][:, :],
                                   wvv[:, ct * 128:(ct + 1) * 128],
                                   zT[ct][:, nch * 512:(nch + 1) * 512],
                                   start=(ct == 0), stop=(ct == CT - 1),
                                   skip_group_check=True)
                ins.then_inc(SM, 1)

        @block.vector
        def _(v):
            for nch in range(8):
                sl = slice(nch * 512, (nch + 1) * 512)
                buf = nch % 2
                v.wait_ge(SM, c_qkmm(nch))
                v.tensor_copy(qrep[0:64, sl], qk[buf][0:64, :]).then_inc(SV, 1)
                v.tensor_copy(krep[64:128, sl], qk[buf][64:128, :]).then_inc(SV, 1)
                v.tensor_copy(krep[0:64, sl], qk[2 + buf][0:64, :]).then_inc(SV, 1)
                v.tensor_copy(qrep[64:128, sl], qk[2 + buf][64:128, :]).then_inc(SV, 1)
            for nch in range(8):
                sl = slice(nch * 512, (nch + 1) * 512)
                v.wait_ge(SM, c_vmm(nch))
                v.tensor_copy(vT_sb[0:64, sl], pvv[nch % 2][0:64, :]).then_inc(SV, 1)

        @block.sync
        def _(sy):
            sy.wait_ge(SV, SV_C)
            sy.dma_start(out=vdram, in_=vT_sb[0:64, :]).then_inc(SVT, 16)
            sy.wait_ge(SVT, 16)
            for kt in range(KT):
                sy.wait_ge(SVT, 16 * (kt + 1))
                sy.dma_start(out=v_sb[:, kt * D:(kt + 1) * D],
                             in_=vdram[:, kt * 128:(kt + 1) * 128],
                             transpose=True).then_inc(SVT, 16)

        closes(zT_cm)

        closes(cm_pvv)
        closes(cm_qk)
        out_cm = [
            nc.sbuf_tensor("out_tok", [128, TOWN * D], F32, side="right"),
        ]
        out_tok, = opens(out_cm)
        z1T_cm = [nc.sbuf_tensor(f"z1T{i}", [128, 2080], BF16, side="right") for i in range(CT)]
        z1T = opens(z1T_cm)
        out2_cm = [
            nc.sbuf_tensor("out_sb", [128, 1024], F32, side="right"),
            nc.sbuf_tensor("outx_sb", [64, 1], F32, side="right"),
            nc.sbuf_tensor("xp_sb", [128, KT], BF16, side="right"),
        ]
        out_sb, outx_sb, xp_sb = opens(out2_cm)
        cm_p23 = [nc.sbuf_tensor("p_sb1", [128, S], BF16, side="right"),
                  nc.sbuf_tensor("p_sb2", [128, S], BF16, side="right"),
                  nc.sbuf_tensor("p_sb3", [128, S], BF16, side="right")]
        p_sb.extend(opens(cm_p23))
        cm_dp = nc.psum_tensor("dpool", [128, 4096], F32)
        dpool = cm_dp.__enter__()
        units = [dpool[:, 1024 + u * 1024:2048 + u * 1024] for u in range(3)]

        @block.tensor
        def _(t):
            t.wait_ge(SV, SV_C)
            for q in range(KT // 2):
                if q >= 2:
                    for tt_ in (2 * (q - 2), 2 * (q - 2) + 1):
                        t.wait_ge(SV, d_vsc(tt_))
                        t.wait_ge(SS, d_exp(tt_, 3))
                        for w in range(4):
                            half = w // 2
                            t.matmul(dpool[64 * half:64 * half + 64,
                                           (w % 2) * 512:(w % 2) * 512 + 512],
                                     vs_sb[:, tt_ * D:(tt_ + 1) * D],
                                     p_sb[tt_ % 4][:, w * 512:(w + 1) * 512],
                                     start=(tt_ == 0), stop=(tt_ == KT - 1),
                                     skip_group_check=True,
                                     tile_position=(0, 64 * half)).then_inc(SM, 1)
                for s in range(4):
                    for a in range(2):
                        kt = 2 * q + a
                        e = d_eidx(kt, s)
                        u = e % 3
                        if e >= 3:
                            ktp_, sp_ = d_einv(e - 3)
                            t.wait_ge(SS, d_exp(ktp_, sp_))
                            if sp_ == 0:
                                t.wait_ge(SV, d_max(ktp_))
                        for cc in range(2):
                            c = 2 * s + cc
                            t.matmul(units[u][:, cc * 512:(cc + 1) * 512],
                                     krep[64 * a:64 * (a + 1), kt * 128:(kt + 1) * 128],
                                     qrep[64 * a:64 * (a + 1), c * 512:(c + 1) * 512],
                                     start=True, stop=True, skip_group_check=True,
                                     tile_position=(64 * a, 0)).then_inc(SM, 1)
            for tt_ in range(KT - 4, KT):
                t.wait_ge(SV, d_vsc(tt_))
                t.wait_ge(SS, d_exp(tt_, 3))
                for w in range(4):
                    half = w // 2
                    t.matmul(dpool[64 * half:64 * half + 64,
                                   (w % 2) * 512:(w % 2) * 512 + 512],
                             vs_sb[:, tt_ * D:(tt_ + 1) * D],
                             p_sb[tt_ % 4][:, w * 512:(w + 1) * 512],
                             start=(tt_ == 0), stop=(tt_ == KT - 1),
                             skip_group_check=True,
                             tile_position=(0, 64 * half)).then_inc(SM, 1)

        @block.vector
        def _(v):
            def d_maxes(v, q):
                for a in range(2):
                    kt = 2 * q + a
                    u0 = d_eidx(kt, 0) % 3
                    v.wait_ge(SM, d_smm(kt, 1))
                    v.reduce_max(negm[:, kt:kt + 1],
                                 units[u0][:, 0:512:4],
                                 axis=AX.X, negate=True).then_inc(SV, 1)

            d_maxes(v, 0)
            d_maxes(v, 1)
            for q in range(KT // 2):
                for a in range(2):
                    kt = 2 * q + a
                    v.wait_ge(SS, d_exp(kt, 3))
                    v.reduce_sum(ell[:, kt:kt + 1],
                                 ellp[:, 4 * (kt % 2):4 * (kt % 2) + 4],
                                 axis=AX.X).then_inc(SV, 1)
                    v.wait_ge(SV, d_ellsum(kt))
                    v.reciprocal(rell[:, kt:kt + 1], ell[:, kt:kt + 1]).then_inc(SV, 1)
                    v.wait_ge(SV, d_recip(kt))
                    v.wait_ge(SVT, 16 * (kt + 2))
                    v.tensor_scalar(vs_sb[:, kt * D:(kt + 1) * D],
                                    v_sb[:, kt * D:(kt + 1) * D],
                                    rell[:, kt:kt + 1], None, ALU.mult).then_inc(SV, 1)
                    v.tensor_copy(xp_sb[:, kt:kt + 1],
                                  p_sb[kt % 4][:, 4095:4096]).then_inc(SV, 1)
                if q + 2 < KT // 2:
                    d_maxes(v, q + 2)

        @block.scalar
        def _(sc):
            for q in range(KT // 2):
                for s in range(4):
                    for a in range(2):
                        kt = 2 * q + a
                        u = d_eidx(kt, s) % 3
                        if s == 0:
                            if kt == 0:
                                sc.wait_ge(SVT, 16)
                            sc.wait_ge(SV, d_max(kt))
                            if kt >= 2:
                                sc.wait_ge(SV, d_ellsum(kt - 2))
                            if kt >= 4:
                                sc.wait_ge(SM, d_outmm(kt - 4))
                        sc.wait_ge(SM, d_smm(kt, 2 * s + 1))
                        sc.activation(p_sb[kt % 4][:, s * 1024:(s + 1) * 1024],
                                      units[u][:, :], AF.Exp,
                                      bias=negm[:, kt:kt + 1],
                                      accum_out=ellp[:, 4 * (kt % 2) + s:
                                                     4 * (kt % 2) + s + 1]).then_inc(SS, 1)

        @block.gpsimd
        def _(g):
            g.wait_ge(SS, d_exp(7, 3))
            g.dma_start(out=wrT[:, :].rearrange("p (ct d) -> p ct d", d=C),
                        in_=wrT_in.rearrange("(ct p) d -> p ct d", p=128)).then_inc(SWB, 16)
            g.dma_start(out=wkT[:, :].rearrange("p (ct d) -> p ct d", d=C),
                        in_=wkT_in.rearrange("(ct p) d -> p ct d", p=128)).then_inc(SWB, 16)
            g.dma_start(out=wvT[:, :].rearrange("p (ct d) -> p ct d", d=C),
                        in_=wvT_in.rearrange("(ct p) d -> p ct d", p=128)).then_inc(SWB, 16)

        @block.vector
        def _(v):
            v.wait_ge(SM, d_outmm(KT - 1))
            v.tensor_copy(out_sb[0:64, 0:1024], dpool[0:64, 0:1024]).then_inc(SV, 1)
            v.tensor_copy(out_sb[64:128, 0:1024], dpool[64:128, 0:1024]).then_inc(SV, 1)

        if debug:
            @block.gpsimd
            def _(g):
                g.wait_ge(SV, e_evac)
                g.dma_start(out=d_ell[:, 0:KT], in_=ell[:, :]).then_inc(SW, 16)
                g.dma_start(out=d_ell[:, KT:2 * KT], in_=negm[:, :]).then_inc(SW, 16)
                g.dma_start(out=d_out[:, 0:1024], in_=out_sb[0:64, 0:1024]).then_inc(SW, 16)
                g.wait_ge(SW, SW_SMALL + 112)

        cm_dp.__exit__(None, None, None)
        cm_tp = [nc.psum_tensor(f"tp{i}", [128, 64], F32) for i in range(2)]
        tp_ps = opens(cm_tp)
        cm_oxe = nc.psum_tensor("oxe", [64, 64], F32)
        ox_e = cm_oxe.__enter__()

        @block.tensor
        def _(t):
            t.wait_ge(SV, e_evac)
            t.wait_ge(SS, SS_D)
            for b in range(16):
                if b >= 2:
                    t.wait_ge(SV, e_tpev(b - 2))
                hb = b // 8
                t.transpose(tp_ps[b % 2][:, :],
                            out_sb[64 * hb:64 * hb + 64,
                                   (b % 8) * 128:(b % 8 + 1) * 128],
                            ident[64 * hb:64 * hb + 64,
                                  64 * hb:64 * hb + 64]).then_inc(SM, 1)
            for kt in range(KT):
                t.matmul(ox_e[:, 0:1],
                         vs_sb[:, kt * D:(kt + 1) * D],
                         xp_sb[:, kt:kt + 1],
                         start=(kt == 0), stop=(kt == KT - 1),
                         skip_group_check=True).then_inc(SM, 1)

        @block.vector
        def _(v):
            for b in range(16):
                v.wait_ge(SM, e_tpmm(b))
                v.tensor_copy(out_tok[:, b * D:(b + 1) * D],
                              tp_ps[b % 2][:, :]).then_inc(SV, 1)
            v.wait_ge(SM, e_oxmm)
            v.tensor_copy(outx_sb[:, :], ox_e[:, 0:1]).then_inc(SV, 1)

        @block.sync
        def _(sy):
            sy.wait_ge(SV, e_oxev)
            sy.wait_ge(SD, SD_E0)
            sy.dma_start(out=xrow_d.rearrange("a (p o) -> p (a o)", p=64),
                         in_=outx_sb[0:64, 0:1]).then_inc(SD, 16)
            sy.wait_ge(SD, sd_xrowd)
            sy.dma_start(out=outx_row[:, :], in_=xrow_d).then_inc(SD, 16)
            sy.wait_ge(SD, sd_outxrow)
            sy.dma_start(out=xlast, in_=x_in[4095:4096, :]).then_inc(SD, 16)

        if debug:
            @block.gpsimd
            def _(g):
                g.wait_ge(SV, SV_E)
                g.dma_start(out=d_tok[:, 0:TOWN * D], in_=out_tok[:, :]).then_inc(SW, 16)
                g.dma_start(out=d_tok[:, TOWN * D:], in_=out_tok[:, 0:D]).then_inc(SW, 16)
                g.wait_ge(SW, SW_SMALL + 144)

        cm_oxe.__exit__(None, None, None)
        closes(cm_tp)
        closes(cm_p23)
        closes(attn_cm)
        x1f_cm = [nc.sbuf_tensor("x1b", [128, 8 * C], F32),
                  nc.sbuf_tensor("z1b", [128, 8 * C], BF16)]
        x1b, z1b = opens(x1f_cm)

        @block.gpsimd
        def _(g_):
            for t_ in range(TOWN):
                if t_ >= 8:
                    g_.wait_ge(SV, F_CNT[("apply", t_ - 8)])
                    g_.wait_ge(SS, f_square(t_ - 8))
                else:
                    g_.wait_ge(SV, A_CNT[("apply", 24 + t_)])
                    g_.wait_ge(SS, a_square(24 + t_))
                g_.dma_start(out=xst[:, (t_ % 8) * C:(t_ % 8 + 1) * C],
                             in_=x_in[t_ * 128:(t_ + 1) * 128, :]).then_inc(FSs[t_ % 8], 16)

        @block.sync
        def _(sy):
            sy.wait_ge(SV, SV_F0 + 10)
            sy.wait_ge(SD, sd_row63 - 16)
            sy.dma_start(out=z1dram[15:16, :], in_=z1xm).then_inc(SD, 16)
            sy.wait_ge(SV, SV_F0 + 11)
            sy.wait_ge(SD, sd_zero1 - 16)
            sy.dma_start(out=z1dram[0:15, :],
                         in_=bass.AP(zst, 0, [[8 * C, 15], [1, C]])).then_inc(SD, 16)
            sy.wait_ge(SD, sd_zero2 - 16)
            sy.dma_start(out=z1dram[2064:2080, :],
                         in_=bass.AP(zst, 0, [[8 * C, 16], [1, C]])).then_inc(SD, 16)
            for g in range(4):
                for s in range(4):
                    t_ = g * 4 + s
                    sy.wait_ge(SV, F_CNT[("apply", t_)])
                    sy.dma_start(out=z1dram[16 + t_ * 128:16 + (t_ + 1) * 128, :],
                                 in_=z1b[:, (t_ % 8) * C:(t_ % 8 + 1) * C]).then_inc(ZSs[t_ % 8], 16)

        @block.vector
        def _(v):
            def f_x1_sum(v, t_):
                s8 = t_ % 8
                v.wait_ge(FSs[s8], fs_f(t_))
                if t_ >= 8:
                    v.wait_ge(SS, f_square(t_ - 8))
                v.tensor_tensor(
                    x1b[:, s8 * C:(s8 + 1) * C],
                    xst[:, s8 * C:(s8 + 1) * C],
                    bass.AP(out_tok, t_ * D, [[TOWN * D, 128], [0, 16], [1, D]]),
                    ALU.add).then_inc(SV, 1)
                v.wait_ge(SV, F_CNT[("x1", t_)])
                if t_ >= 8:
                    v.wait_ge(SV, F_CNT[("smean", (t_ - 8) // 4)])
                v.reduce_sum(sums[:, s8:s8 + 1], x1b[:, s8 * C:(s8 + 1) * C],
                             axis=AX.X).then_inc(SV, 1)

            def f_apply_emit(v, g):
                gb = (g % 2) * 4
                for s in range(4):
                    t_ = g * 4 + s
                    if t_ >= 8:
                        v.wait_ge(ZSs[t_ % 8], zs_f(t_ - 8))
                    v.wait_ge(SV, F_CNT[("nmr", g)])
                    v.tensor_scalar(z1b[:, (t_ % 8) * C:(t_ % 8 + 1) * C],
                                    x1b[:, (t_ % 8) * C:(t_ % 8 + 1) * C],
                                    srstd[:, gb + s:gb + s + 1],
                                    snmr[:, gb + s:gb + s + 1],
                                    ALU.mult, ALU.add).then_inc(SV, 1)

            v.wait_ge(SD, sd_xlast)
            v.tensor_tensor(x1x, xlast,
                            bass.AP(outx_row, 0, [[D, 1], [0, 16], [1, D]]),
                            ALU.add).then_inc(SV, 1)
            v.wait_ge(SV, SV_F0 + 1)
            v.reduce_sum(rs[:, 0:1], x1x, axis=AX.X).then_inc(SV, 1)
            v.wait_ge(SS, SS_F0 + 1)
            v.wait_ge(SV, SV_F0 + 2)
            v.tensor_scalar(rs[:, 2:3], rs[:, 0:1], cconst[0:1, 2:3], None, ALU.mult).then_inc(SV, 1)
            v.tensor_scalar(rs[:, 3:4], rs[:, 1:2], cconst[0:1, 2:3], None, ALU.mult).then_inc(SV, 1)
            v.wait_ge(SV, SV_F0 + 3)
            v.tensor_tensor(rs[:, 4:5], rs[:, 2:3], rs[:, 2:3], ALU.mult).then_inc(SV, 1)
            v.wait_ge(SV, SV_F0 + 5)
            v.tensor_tensor(rs[:, 5:6], rs[:, 3:4], rs[:, 4:5], ALU.subtract).then_inc(SV, 1)
            v.wait_ge(SS, SS_TAIL)
            v.reciprocal(rs[:, 7:8], rs[:, 6:7]).then_inc(SV, 1)
            v.wait_ge(SV, SV_F0 + 7)
            v.scalar_tensor_tensor(rs[:, 8:9], rs[:, 2:3], cconst[0:1, 1:2], rs[:, 7:8],
                                   ALU.mult, ALU.mult).then_inc(SV, 1)
            v.wait_ge(SV, SV_F0 + 8)
            v.tensor_scalar(z1x, x1x, rs[:, 7:8], rs[:, 8:9],
                            ALU.mult, ALU.add).then_inc(SV, 1)
            v.wait_ge(SW, SW_SMALL)
            v.wait_ge(SV, SV_F0 + 9)
            v.tensor_scalar(z1xm, z1x, hmask[0:1, 0:1], None,
                            ALU.mult).then_inc(SV, 1)
            v.memset(zst[:, 0:C], 0.0).then_inc(SV, 1)
            for s in range(4):
                f_x1_sum(v, s)
            for g in range(1, 4):
                emit_ln_fin(v, g - 1, F_CNT, f_square(4 * (g - 1) + 3),
                            f_sqrt(g - 1))
                f_apply_emit(v, g - 1)
                for s in range(4):
                    f_x1_sum(v, g * 4 + s)
            emit_ln_fin(v, 3, F_CNT, f_square(15), f_sqrt(3))
            f_apply_emit(v, 3)

        @block.scalar
        def _(sc):
            sc.wait_ge(SV, SV_F0 + 1)
            sc.activation(jrow, x1x, AF.Square,
                          accum_out=rs[:, 1:2]).then_inc(SS, 1)
            sc.wait_ge(SV, SV_F0 + 6)
            sc.activation(rs[:, 6:7], rs[:, 5:6], AF.Sqrt, bias=cconst[0:1, 0:1]).then_inc(SS, 1)
            for g in range(4):
                gb = (g % 2) * 4
                for s in range(4):
                    t_ = g * 4 + s
                    sc.wait_ge(SV, F_CNT[("x1", t_)])
                    sc.wait_ge(SS, f_square(t_) - 1)
                    if g >= 2:
                        sc.wait_ge(SV, F_CNT[("svar", g - 2)])
                    sc.activation(zst[:, 2 * C:3 * C], x1b[:, (t_ % 8) * C:(t_ % 8 + 1) * C],
                                  AF.Square,
                                  accum_out=sumsq[:, t_ % 8:t_ % 8 + 1]).then_inc(SS, 1)
                sc.wait_ge(SV, F_CNT[("varsub", g)])
                if g >= 2:
                    sc.wait_ge(SV, F_CNT[("recip", g - 2)])
                sc.activation(sstd[:, gb:gb + 4], svar[:, gb:gb + 4], AF.Sqrt,
                              bias=cconst[:, 0:1]).then_inc(SS, 1)

        @block.scalar
        def _(sc):
            sc.wait_ge(SD, sd_zero1)
            for s in range(8):
                sc.wait_ge(ZSs[s], 16)
            for ctn in range(CT):
                sc.dma_start(out=z1T[ctn][:, 0:1040],
                             in_=z1dram[0:1040, ctn * 128:(ctn + 1) * 128],
                             transpose=True).then_inc(SG_LO, 16)
        @block.sync
        def _(sy):
            sy.wait_ge(SD, sd_zero2)
            for s in range(8):
                sy.wait_ge(ZSs[s], 32)
            sy.wait_ge(SG_LO, 128)
            for ctn in range(CT):
                sy.wait_ge(SG_HI, 16 * ctn)
                sy.dma_start(out=z1T[ctn][:, 1040:2080],
                             in_=z1dram[1040:2080, ctn * 128:(ctn + 1) * 128],
                             transpose=True).then_inc(SG_HI, 16)

        if debug:
            @block.gpsimd
            def _(g):
                g.wait_ge(SG_HI, 128)
                g.dma_start(out=d_z1, in_=z1dram).then_inc(SW, 16)
                g.wait_ge(SW, SW_SMALL + 160)

        closes(x1f_cm)
        closes(out2_cm)
        xk_cm = [nc.sbuf_tensor(f"xk{i}", [128, 2048], BF16) for i in range(CT)]
        xk = opens(xk_cm)
        xr_cm = [nc.sbuf_tensor(f"xr{i}", [128, 2048], BF16) for i in range(CT)]
        xr = opens(xr_cm)
        dbuf = zst[:, 0:2048]

        @block.scalar
        def _(sc):
            sc.wait_ge(SV, F_CNT[("apply", 15)])
            for ct in range(CT):
                sc.wait_ge(SG_HI, szt_g_hi(ct))
                sc.activation(xr[ct][:, :], z1T[ct][:, 16:2064], AF.Identity,
                              bias=0.0, scale=mu[:, ct:ct + 1]).then_inc(SS, 1)
            for ct in range(CT):
                sc.activation(xk[ct][:, :], z1T[ct][:, 16:2064], AF.Identity,
                              bias=0.0, scale=mu[:, 8 + ct:9 + ct]).then_inc(SS, 1)

        @block.vector
        def _(v):
            v.wait_ge(SS, f_square(15))
            v.scalar_tensor_tensor(mu1m[:, :], mu[:, :], -1.0, onec[:, :],
                                   ALU.mult, ALU.add).then_inc(SV, 1)
            v.wait_ge(SV, SV_H0)
            for ct in range(CT):
                v.wait_ge(SS, h_tsr(ct))
                v.wait_ge(SV, SV_H0 + ct)
                v.scalar_tensor_tensor(xr[ct][:, :], z1T[ct][:, 15:2063],
                                       mu1m[:, ct:ct + 1],
                                       xr[ct][:, :], ALU.mult,
                                       ALU.add).then_inc(SV, 1)
            for ct in range(CT):
                v.wait_ge(SS, h_tsk(ct))
                v.wait_ge(SV, SV_H0 + 8 + ct)
                v.scalar_tensor_tensor(xk[ct][:, :], z1T[ct][:, 15:2063],
                                       mu1m[:, 8 + ct:9 + ct],
                                       xk[ct][:, :], ALU.mult,
                                       ALU.add).then_inc(SV, 1)

        closes(z1T_cm)
        r_sb_cm = [nc.sbuf_tensor("r_sb", [128, TOWN * C], BF16, side="right")]
        r_sb, = opens(r_sb_cm)
        cm_rps = [nc.psum_tensor(f"rps{i}", [128, C], F32) for i in range(3)]
        r_ps = opens(cm_rps)

        @block.tensor
        def _(t):
            t.wait_ge(SV, h_xr_all)
            t.wait_ge(SS, SS_F)
            t.wait_ge(SWB, SWB_ALL)
            for t_ in range(TOWN):
                if t_ >= 3:
                    t.wait_ge(SS, i_sig(t_ - 3))
                for ct in range(CT):
                    for hh in range(2):
                        ins = t.matmul(
                            r_ps[t_ % 3][:, hh * 512:(hh + 1) * 512],
                            xr[ct][:, t_ * 128:(t_ + 1) * 128],
                            wrT[:, ct * C + hh * 512:ct * C + (hh + 1) * 512],
                            start=(ct == 0), stop=(ct == CT - 1),
                            skip_group_check=True)
                ins.then_inc(SM, 1)

        @block.scalar
        def _(sc):
            sc.wait_ge(SV, SV_H)
            for t_ in range(TOWN):
                sc.wait_ge(SM, i_mm(t_))
                sc.activation(r_sb[:, t_ * C:(t_ + 1) * C], r_ps[t_ % 3][:, :],
                              AF.Sigmoid).then_inc(SS, 1)

        closes(cm_rps)
        closes(xr_cm)

        if debug:
            @block.gpsimd
            def _(g):
                g.wait_ge(SS, SS_I)
                g.dma_start(out=d_r, in_=r_sb[:, :]).then_inc(SW, 16)
                g.wait_ge(SW, SW_SMALL + 192)

        kaT_cm = [nc.sbuf_tensor(f"kaT{i}", [128, 2048], BF16, side="right") for i in range(CT)]
        kaT = opens(kaT_cm)
        cm_ktp = [nc.psum_tensor(f"ktp{i}", [128, 512], F32) for i in range(3)]
        kt_ps = opens(cm_ktp)

        @block.tensor
        def _(t):
            t.wait_ge(SS, SS_I)
            for dt in range(CT):
                for tc in range(4):
                    ix = dt * 4 + tc
                    if ix >= 3:
                        t.wait_ge(SS, j_relu(ix - 3))
                    for ct in range(CT):
                        ins = t.matmul(
                            kt_ps[ix % 3][:, :],
                            wkT[:, ct * C + dt * 128:ct * C + (dt + 1) * 128],
                            xk[ct][:, tc * 512:(tc + 1) * 512],
                            start=(ct == 0), stop=(ct == CT - 1),
                            skip_group_check=True)
                    ins.then_inc(SM, 1)

        @block.scalar
        def _(sc):
            for dt in range(CT):
                for tc in range(4):
                    ix = dt * 4 + tc
                    sc.wait_ge(SM, j_mm(ix))
                    if ix >= 4:
                        sc.wait_ge(SV, j_sq(ix - 4))
                    sc.activation(kaT[dt][:, tc * 512:(tc + 1) * 512],
                                  kt_ps[ix % 3][:, :], AF.Relu).then_inc(SS, 1)

        @block.vector
        def _(v):
            for dt in range(CT):
                for tc in range(4):
                    ix = dt * 4 + tc
                    v.wait_ge(SS, j_relu(ix))
                    v.tensor_tensor(kaT[dt][:, tc * 512:(tc + 1) * 512],
                                    kaT[dt][:, tc * 512:(tc + 1) * 512],
                                    kaT[dt][:, tc * 512:(tc + 1) * 512],
                                    ALU.mult).then_inc(SV, 1)

        closes(cm_ktp)
        closes(xk_cm)

        if debug:
            @block.gpsimd
            def _(g):
                g.wait_ge(SV, SV_J)
                for dt in range(CT):
                    g.dma_start(out=d_ka[:, dt * 2048:(dt + 1) * 2048],
                                in_=kaT[dt][:, :]).then_inc(SW, 16)
                g.wait_ge(SW, SW_SMALL + 192 + 128)

        k32_cm = [nc.sbuf_tensor("tmp32", [128, 2 * C], F32, side="right"),
                  nc.sbuf_tensor("x2b", [128, 2 * C], F32, side="right")]
        tmp32, x2b = opens(k32_cm)
        cm_ops = [nc.psum_tensor(f"ops{i}", [128, C], F32) for i in range(3)]
        o_ps = opens(cm_ops)

        @block.tensor
        def _(t):
            t.wait_ge(SV, SV_J)
            t.wait_ge(SS, SS_J)
            for t_ in range(TOWN):
                if t_ >= 3:
                    t.wait_ge(SV, k_tmp(t_ - 3))
                for dt in range(CT):
                    for hh in range(2):
                        ins = t.matmul(
                            o_ps[t_ % 3][:, hh * 512:(hh + 1) * 512],
                            kaT[dt][:, t_ * 128:(t_ + 1) * 128],
                            wvT[:, dt * C + hh * 512:dt * C + (hh + 1) * 512],
                            start=(dt == 0), stop=(dt == CT - 1),
                            skip_group_check=True)
                ins.then_inc(SM, 1)

        @block.gpsimd
        def _(g):
            for t_ in range(TOWN):
                s = t_ % 4
                if t_ >= 4:
                    g.wait_ge(SV, k_x2(t_ - 4))
                else:
                    g.wait_ge(SV, SV_J)
                if t_ >= 8:
                    g.wait_ge(FSs[t_ % 8], fs_k(t_ - 8))
                g.dma_start(out=xst[:, (t_ % 8) * C:(t_ % 8 + 1) * C],
                            in_=x_in[t_ * 128:(t_ + 1) * 128, :]).then_inc(FSs[t_ % 8], 16)

        def k_ydma_emit(sy, t_):
            sy.wait_ge(SV, k_x2(t_))
            sy.dma_start(out=y_out[t_ * 128:(t_ + 1) * 128, :],
                         in_=x2b[:, (t_ % 2) * C:(t_ % 2 + 1) * C]).then_inc(YSs[t_ % 2], 16)

        @block.sync
        def _(sy):
            for t_ in range(TOWN):
                if t_ >= 2:
                    k_ydma_emit(sy, t_ - 2)
            k_ydma_emit(sy, TOWN - 2)
            k_ydma_emit(sy, TOWN - 1)
            sy.wait_ge(YSs[0], 16 * (TOWN // 2))
            sy.wait_ge(YSs[1], 16 * (TOWN // 2))

        @block.vector
        def _(v):
            for t_ in range(TOWN):
                s = t_ % 4
                v.wait_ge(SM, k_mm(t_))
                v.wait_ge(SS, i_sig(t_))
                if t_ >= 2:
                    v.wait_ge(SV, k_x2(t_ - 2))
                    v.wait_ge(YSs[t_ % 2], ys_k(t_ - 2))
                v.tensor_tensor(tmp32[:, (t_ % 2) * C:(t_ % 2 + 1) * C],
                                o_ps[t_ % 3][:, :],
                                r_sb[:, t_ * C:(t_ + 1) * C], ALU.mult).then_inc(SV, 1)
                v.wait_ge(SV, k_tmp(t_))
                v.tensor_tensor(tmp32[:, (t_ % 2) * C:(t_ % 2 + 1) * C],
                                tmp32[:, (t_ % 2) * C:(t_ % 2 + 1) * C],
                                bass.AP(out_tok, t_ * D, [[TOWN * D, 128], [0, 16], [1, D]]),
                                ALU.add).then_inc(SV, 1)
                v.wait_ge(FSs[t_ % 8], fs_k(t_))
                v.wait_ge(SV, k_mid(t_))
                v.tensor_tensor(x2b[:, (t_ % 2) * C:(t_ % 2 + 1) * C],
                                tmp32[:, (t_ % 2) * C:(t_ % 2 + 1) * C],
                                xst[:, (t_ % 8) * C:(t_ % 8 + 1) * C], ALU.add).then_inc(SV, 1)


        closes(cm_ops)
        closes(k32_cm)
        closes(kaT_cm)
        closes(r_sb_cm)
        closes(out_cm)

    return nc


def kernel(L0_attn_x, L0_attn_Wq, L0_attn_Wk, L0_attn_Wv,
           L0_ffn_W_r, L0_ffn_W_k, L0_ffn_W_v, L0_ffn_mu_r, L0_ffn_mu_k):
    x = np.ascontiguousarray(np.asarray(L0_attn_x, dtype=np.float32))
    Wq = np.asarray(L0_attn_Wq, dtype=np.float32)
    Wk = np.asarray(L0_attn_Wk, dtype=np.float32)
    Wv = np.asarray(L0_attn_Wv, dtype=np.float32)
    Wr = np.asarray(L0_ffn_W_r, dtype=np.float32)
    Wkf = np.asarray(L0_ffn_W_k, dtype=np.float32)
    Wvf = np.asarray(L0_ffn_W_v, dtype=np.float32)
    mur = np.asarray(L0_ffn_mu_r, dtype=np.float32)
    muk = np.asarray(L0_ffn_mu_k, dtype=np.float32)

    wqk = np.ascontiguousarray(np.concatenate([Wq, Wk], axis=1))
    wkq = np.ascontiguousarray(np.concatenate([Wk, Wq], axis=1))
    wvv = np.ascontiguousarray(np.concatenate([Wv, Wv], axis=1))
    wrT = np.ascontiguousarray(Wr.T)
    wkT = np.ascontiguousarray(Wkf.T)
    wvT = np.ascontiguousarray(Wvf.T)
    mu_sb = np.ascontiguousarray(
        np.concatenate([mur.reshape(8, 128).T, muk.reshape(8, 128).T], axis=1))
    ident = np.ascontiguousarray(np.eye(128, dtype=np.float32))

    if "nc" not in _PROGRAM_CACHE:
        _PROGRAM_CACHE["nc"] = build_program()
    nc = _PROGRAM_CACHE["nc"]

    in_maps = []
    for c in range(8):
        b, h = c // 2, c % 2
        own = x[b, h * 2048:(h + 1) * 2048]
        other = x[b, (1 - h) * 2048:(2 - h) * 2048]
        xp = np.ascontiguousarray(np.concatenate([own, other], axis=0))
        in_maps.append({
            "x_in": xp,
            "wqk_in": wqk, "wkq_in": wkq, "wvv_in": wvv,
            "wrT_in": wrT, "wkT_in": wkT, "wvT_in": wvT,
            "mu_in": mu_sb,
            "hmask_in": np.array([[float(h)]], dtype=np.float32),
            "ident_in": ident,
        })

    res = run_bass_kernel_spmd(nc, in_maps, core_ids=list(range(8)))

    y = np.empty((4, S, C), dtype=np.float32)
    for c in range(8):
        b, h = c // 2, c % 2
        y[b, h * 2048:(h + 1) * 2048] = np.asarray(res.results[c]["y_out"])
    return y


# version 23
# speedup vs baseline: 1.1357x; 1.1357x over previous
import numpy as np
from contextlib import ExitStack

import concourse.bass as bass
import concourse.mybir as mybir
from concourse.bass_utils import run_bass_kernel_spmd

F32 = mybir.dt.float32
BF16 = mybir.dt.bfloat16
AF = mybir.ActivationFunctionType
ALU = mybir.AluOpType
AX = mybir.AxisListType

S = 4096
C = 1024
D = 64
TOWN = 16
TT = 32
CT = 8
KT = 32
EPS = 1e-5
RC = 1.0 / C

_PROGRAM_CACHE = {}


def build_program(debug=False):
    nc = bass.Bass("TRN2", target_bir_lowering=False)

    x_in = nc.dram_tensor("x_in", [S, C], F32, kind="ExternalInput").ap()
    wqk_in = nc.dram_tensor("wqk_in", [C, 128], F32, kind="ExternalInput").ap()
    wkq_in = nc.dram_tensor("wkq_in", [C, 128], F32, kind="ExternalInput").ap()
    wvv_in = nc.dram_tensor("wvv_in", [C, 128], F32, kind="ExternalInput").ap()
    wrT_in = nc.dram_tensor("wrT_in", [C, C], F32, kind="ExternalInput").ap()
    wkT_in = nc.dram_tensor("wkT_in", [C, C], F32, kind="ExternalInput").ap()
    wvT_in = nc.dram_tensor("wvT_in", [C, C], F32, kind="ExternalInput").ap()
    mu_in = nc.dram_tensor("mu_in", [128, 16], F32, kind="ExternalInput").ap()
    hmask_in = nc.dram_tensor("hmask_in", [1, 1], F32, kind="ExternalInput").ap()
    ident_in = nc.dram_tensor("ident_in", [128, 128], F32, kind="ExternalInput").ap()
    y_out = nc.dram_tensor("y_out", [2048, C], F32, kind="ExternalOutput").ap()

    if debug:
        d_z = nc.dram_tensor("d_z", [S, C], F32, kind="ExternalOutput").ap()
        d_qk = nc.dram_tensor("d_qk", [256, S], F32, kind="ExternalOutput").ap()
        d_v = nc.dram_tensor("d_v", [128, KT * D], F32, kind="ExternalOutput").ap()
        d_out = nc.dram_tensor("d_out", [64, 2176], F32, kind="ExternalOutput").ap()
        d_ell = nc.dram_tensor("d_ell", [128, 2 * KT], F32, kind="ExternalOutput").ap()
        d_tok = nc.dram_tensor("d_tok", [128, TOWN * D + D], F32, kind="ExternalOutput").ap()
        d_z1 = nc.dram_tensor("d_z1", [2080, C], F32, kind="ExternalOutput").ap()
        d_x1 = nc.dram_tensor("d_x1", [2048, C], F32, kind="ExternalOutput").ap()
        d_r = nc.dram_tensor("d_r", [128, TOWN * C], F32, kind="ExternalOutput").ap()
        d_ka = nc.dram_tensor("d_ka", [128, CT * 2048], F32, kind="ExternalOutput").ap()
    zdram = nc.dram_tensor("zdram", [S, C], BF16).ap()
    z1dram = nc.dram_tensor("z1dram", [2080, C], BF16).ap()
    x1dram = nc.dram_tensor("x1dram", [2048, C], F32).ap()
    xrow_d = nc.dram_tensor("xrow_d", [1, D], F32).ap()
    vdram = nc.dram_tensor("vdram", [64, S], BF16).ap()

    ctx = ExitStack()
    e = ctx.enter_context
    with ctx:
        wqk = e(nc.sbuf_tensor("wqk", [128, 8 * 128], BF16))
        wkq = e(nc.sbuf_tensor("wkq", [128, 8 * 128], BF16))
        wvv = e(nc.sbuf_tensor("wvv", [128, 8 * 128], BF16))
        wrT = e(nc.sbuf_tensor("wrT", [128, 8 * C], BF16))
        wkT = e(nc.sbuf_tensor("wkT", [128, 8 * C], BF16))
        wvT = e(nc.sbuf_tensor("wvT", [128, 8 * C], BF16))
        mu = e(nc.sbuf_tensor("mu", [128, 16], F32))
        mu1m = e(nc.sbuf_tensor("mu1m", [128, 16], F32))
        onec = e(nc.sbuf_tensor("onec", [128, 16], F32))
        hmask = e(nc.sbuf_tensor("hmask", [1, 1], F32))
        ident = e(nc.sbuf_tensor("ident", [128, 128], F32))

        cconst = e(nc.sbuf_tensor("cconst", [128, 4], F32))
        xst = e(nc.sbuf_tensor("xst", [128, 8 * C], F32))
        zst = e(nc.sbuf_tensor("zst", [128, 8 * C], BF16))
        sums = e(nc.sbuf_tensor("sums", [128, 8], F32))
        sumsq = e(nc.sbuf_tensor("sumsq", [128, 8], F32))
        smean = e(nc.sbuf_tensor("smean", [128, 8], F32))
        svar = e(nc.sbuf_tensor("svar", [128, 8], F32))
        sstd = e(nc.sbuf_tensor("sstd", [128, 8], F32))
        srstd = e(nc.sbuf_tensor("srstd", [128, 8], F32))
        snmr = e(nc.sbuf_tensor("snmr", [128, 8], F32))

        rowf = e(nc.sbuf_tensor("rowf", [128, C], F32))
        xlast = rowf[0:1, :]
        x1x = rowf[32:33, :]
        z1x = zst[0:1, C:2 * C]
        z1xm = zst[32:33, C:2 * C]
        jrow = zst[64:65, C:2 * C]
        outx_row = e(nc.sbuf_tensor("outx_row", [1, D], F32))
        rs = e(nc.sbuf_tensor("rs", [1, 12], F32))

        def opens(cms):
            return [cm.__enter__() for cm in cms]

        def closes(cms):
            for cm in reversed(cms):
                cm.__exit__(None, None, None)

        zT_cm = [nc.sbuf_tensor(f"zT{i}", [128, S], BF16, side="right") for i in range(CT)]
        zT = opens(zT_cm)
        attn_cm = [
            nc.sbuf_tensor("qrep", [128, S], BF16),
            nc.sbuf_tensor("krep", [128, S], BF16),
            nc.sbuf_tensor("v_sb", [128, KT * D], BF16),
            nc.sbuf_tensor("p_sb0", [128, S], BF16),
            nc.sbuf_tensor("negm", [128, KT], F32),
            nc.sbuf_tensor("ellp", [128, 16], F32),
            nc.sbuf_tensor("ell", [128, KT], F32),
            nc.sbuf_tensor("rell", [128, KT], F32),
        ]
        (qrep, krep, v_sb, p_sb0, negm, ellp, ell, rell) = opens(attn_cm)
        vs_sb = v_sb
        vT_sb = p_sb0
        p_sb = [p_sb0]

        SW = e(nc.semaphore("SW"))
        SX = e(nc.semaphore("SX"))
        SD = e(nc.semaphore("SD"))
        DSs = [e(nc.semaphore(f"DS{i}")) for i in range(8)]
        ZSs = [e(nc.semaphore(f"ZS{i}")) for i in range(8)]
        XSs = [e(nc.semaphore(f"XS{i}")) for i in range(8)]
        YSs = [e(nc.semaphore(f"YS{i}")) for i in range(2)]
        SVT = e(nc.semaphore("SVT"))
        SH = e(nc.semaphore("SH"))
        SWB = e(nc.semaphore("SWB"))
        FSs = [e(nc.semaphore(f"FS{i}")) for i in range(8)]
        ZAs = [e(nc.semaphore(f"ZA{i}")) for i in range(8)]
        SZQ = [e(nc.semaphore(f"SZQ{i}")) for i in range(4)]
        SG_LO = e(nc.semaphore("SG_LO"))
        SG_HI = e(nc.semaphore("SG_HI"))
        SM = e(nc.semaphore("SM"))
        SV = e(nc.semaphore("SV"))
        SS = e(nc.semaphore("SS"))
        SY = e(nc.semaphore("SY"))

        block = e(nc.Block())

        fs_f = lambda t: 16 * (t // 8 + 1)
        fs_k = lambda t: 32 + 16 * (t // 8 + 1)
        zs_f = lambda t: 16 * (t // 8 + 1)
        xs_f = lambda t: 16 * (t // 8 + 1)
        ys_k = lambda t: 16 * (t // 2 + 1)
        def plan_ln(n_tiles, sv0, extra_per_tile):
            ng = n_tiles // 4
            cnt = {}
            sv = sv0
            for g in range(ng):
                for s4 in range(4):
                    i = 4 * g + s4
                    for op in extra_per_tile:
                        sv += 1
                        cnt[(op, i)] = sv
                    sv += 1
                    cnt[("sum", i)] = sv
                if g >= 1:
                    for op in ("smean", "svar", "msq", "varsub", "recip", "nmr"):
                        sv += 1
                        cnt[(op, g - 1)] = sv
                    for s4 in range(4):
                        sv += 1
                        cnt[("apply", 4 * (g - 1) + s4)] = sv
            for op in ("smean", "svar", "msq", "varsub", "recip", "nmr"):
                sv += 1
                cnt[(op, ng - 1)] = sv
            for s4 in range(4):
                sv += 1
                cnt[("apply", 4 * (ng - 1) + s4)] = sv
            return cnt, sv

        def plan_ln_af(n_tiles, sv0):
            ng = n_tiles // 4
            cnt = {}
            sv = sv0
            for s4 in range(4):
                sv += 1
                cnt[("sum", s4)] = sv
            for g in range(1, ng + 1):
                gm = g - 1
                for op in ("smean", "svar", "msq", "varsub", "recip", "nmr"):
                    sv += 1
                    cnt[(op, gm)] = sv
                for s4 in range(4):
                    sv += 1
                    cnt[("apply", 4 * gm + s4)] = sv
                if g < ng:
                    for s4 in range(4):
                        sv += 1
                        cnt[("sum", 4 * g + s4)] = sv
            return cnt, sv

        A_CNT, SV_A = plan_ln_af(TT, 0)
        a_square = lambda i: 5 * (i // 4) + (i % 4) + 1
        a_sqrt = lambda g: 5 * g + 5
        SS_A = 5 * 8
        a_ds = lambda i: 16 * (i // 8 + 1)
        a_za = lambda i: 16 * (i // 8 + 1)
        SM_C0, SV_C0 = 0, SV_A
        c_qkmm = lambda n: SM_C0 + n + 1
        c_vmm = lambda n: SM_C0 + 8 + n + 1
        c_qkev = lambda n: SV_C0 + 4 * n + 4
        c_vev = lambda n: SV_C0 + 32 + n + 1
        SM_C = SM_C0 + 16
        SV_C = SV_C0 + 40
        SM_D0, SS_D0, SV_D0 = SM_C, SS_A, SV_C
        d_P = lambda q: 0 if q == 0 else 24 * q - 8
        d_smm = lambda kt, c: (SM_D0 + d_P(kt // 2) + 4 * (c // 2) + 2 * (kt % 2)
                               + (c % 2) + 1)
        d_eidx = lambda kt, s: 2 * (4 * (kt // 2) + s) + (kt % 2)

        def d_einv(e):
            sig, a = e // 2, e % 2
            return 2 * (sig // 4) + a, sig % 4

        def d_outmm(t):
            q = t // 2
            if q <= 13:
                return SM_D0 + d_P(q + 2) - 4 * (1 - (t % 2))
            return SM_D0 + 368 + 8 * (q - 14) + 4 * (t % 2 + 1)
        d_exp = lambda kt, s: SS_D0 + 8 * (kt // 2) + 2 * s + (kt % 2) + 1
        D_CNT = {}
        _sv = SV_D0
        for _kt in range(4):
            _sv += 1
            D_CNT[("max", _kt)] = _sv
        for _q in range(16):
            for _a in range(2):
                _kt = 2 * _q + _a
                for _op in ("ellsum", "recip", "vsc", "xp"):
                    _sv += 1
                    D_CNT[(_op, _kt)] = _sv
            if _q + 2 < 16:
                for _a in range(2):
                    _sv += 1
                    D_CNT[("max", 2 * (_q + 2) + _a)] = _sv
        d_max = lambda kt: D_CNT[("max", kt)]
        d_ellsum = lambda kt: D_CNT[("ellsum", kt)]
        d_recip = lambda kt: D_CNT[("recip", kt)]
        d_vsc = lambda kt: D_CNT[("vsc", kt)]
        d_xp = lambda kt: D_CNT[("xp", kt)]
        SM_D = SM_D0 + 384
        SS_D = SS_D0 + 128
        SV_D = _sv
        assert SV_D == SV_D0 + 160
        e_evac = SV_D + 2
        e_tpmm = lambda b: SM_D + b + 1
        e_oxmm = SM_D + 16 + KT
        e_tpev = lambda b: e_evac + b + 1
        e_oxev = e_evac + 17
        SM_E = e_oxmm
        SV_E = e_oxev
        SV_F0, SS_F0 = SV_E, SS_D
        SV_TAIL = SV_F0 + 11
        SS_TAIL = SS_F0 + 2

        def plan_ln_af2(n_tiles, sv0):
            ng = n_tiles // 4
            cnt = {}
            sv = sv0
            for s4 in range(4):
                sv += 1
                cnt[("x1", s4)] = sv
            for g in range(1, ng + 1):
                gm = g - 1
                for op in ("smean", "svar", "msq", "varsub", "recip", "nmr"):
                    sv += 1
                    cnt[(op, gm)] = sv
                for s4 in range(4):
                    sv += 1
                    cnt[("apply", 4 * gm + s4)] = sv
                if g < ng:
                    for s4 in range(4):
                        i = 4 * g + s4
                        sv += 1
                        cnt[("x1", i)] = sv
            return cnt, sv

        F_CNT, SV_Floop = plan_ln_af2(TOWN, SV_TAIL)
        f_sumss = lambda t: SS_TAIL + 9 * (t // 4) + 2 * (t % 4) + 1
        f_square = lambda t: SS_TAIL + 9 * (t // 4) + 2 * (t % 4) + 2
        f_sqrt = lambda g: SS_TAIL + 9 * g + 9
        SV_F = SV_Floop
        SS_F = SS_TAIL + 36
        SD_E0 = 0
        sd_xrowd = SD_E0 + 16
        sd_outxrow = SD_E0 + 32
        sd_xlast = SD_E0 + 48
        sd_row63 = SD_E0 + 64
        sd_zero1 = SD_E0 + 80
        sd_zero2 = SD_E0 + 96
        SD_F = sd_zero2
        szt_g_hi = lambda ct: 16 * (ct + 1)
        SS_H0 = SS_F
        h_tsr = lambda ct: SS_H0 + ct + 1
        h_tsk = lambda ct: SS_H0 + 8 + ct + 1
        SS_H = SS_H0 + 16
        SV_H0 = SV_F + 1
        h_xr_all = SV_H0 + 8
        SV_H = SV_H0 + 16
        SH_ALL = 0
        SM_I0, SS_I0 = SM_E, SS_H
        i_mm = lambda t: SM_I0 + t + 1
        i_sig = lambda t: SS_I0 + t + 1
        SM_I = SM_I0 + 16
        SS_I = SS_I0 + 16
        SM_J0, SS_J0, SV_J0 = SM_I, SS_I, SV_H
        j_mm = lambda ix: SM_J0 + ix + 1
        j_relu = lambda ix: SS_J0 + ix + 1
        j_sq = lambda ix: SV_J0 + ix + 1
        SM_J = SM_J0 + 32
        SS_J = SS_J0 + 32
        SV_J = SV_J0 + 32
        SM_K0, SV_K0 = SM_J, SV_J
        k_mm = lambda t: SM_K0 + t + 1
        k_tmp = lambda t: SV_K0 + 2 * t + 1
        k_x2 = lambda t: SV_K0 + 2 * t + 2
        k_ydma = lambda t: 16 * (t + 1)

        @block.gpsimd
        def _(g):
            g.dma_start(out=wqk[:, :].rearrange("p (ct d) -> p ct d", d=128),
                        in_=wqk_in.rearrange("(ct p) d -> p ct d", p=128)).then_inc(SW, 16)
            g.dma_start(out=wkq[:, :].rearrange("p (ct d) -> p ct d", d=128),
                        in_=wkq_in.rearrange("(ct p) d -> p ct d", p=128)).then_inc(SW, 16)
            g.dma_start(out=wvv[:, :].rearrange("p (ct d) -> p ct d", d=128),
                        in_=wvv_in.rearrange("(ct p) d -> p ct d", p=128)).then_inc(SW, 16)
            g.dma_start(out=mu[:, :], in_=mu_in).then_inc(SW, 16)
            g.dma_start(out=hmask[:, :], in_=hmask_in).then_inc(SW, 16)
            g.dma_start(out=ident[:, :], in_=ident_in).then_inc(SW, 16)
        SW_SMALL = 96
        SWB_ALL = 48

        @block.sync
        def _(sy):
            for i in range(TT):
                s = i % 8
                if i >= 8:
                    sy.wait_ge(SV, A_CNT[("apply", i - 8)])
                    sy.wait_ge(SS, a_square(i - 8))
                sy.dma_start(out=xst[:, s * C:(s + 1) * C],
                             in_=x_in[i * 128:(i + 1) * 128, :]).then_inc(DSs[s], 16)
                if i >= 8:
                    j = i - 8
                    sy.dma_start(out=zdram[j * 128:(j + 1) * 128, :],
                                 in_=zst[:, (j % 8) * C:(j % 8 + 1) * C]).then_inc(ZAs[j % 8], 16)
            for j in range(TT - 8, TT):
                sy.wait_ge(SV, A_CNT[("apply", j)])
                sy.dma_start(out=zdram[j * 128:(j + 1) * 128, :],
                             in_=zst[:, (j % 8) * C:(j % 8 + 1) * C]).then_inc(ZAs[j % 8], 16)

        @block.scalar
        def _(sc):
            for g in range(8):
                gb = (g % 2) * 4
                for s in range(4):
                    i = g * 4 + s
                    sc.wait_ge(DSs[i % 8], a_ds(i))
                    sc.wait_ge(SS, a_square(i) - 1)
                    if i >= 8:
                        sc.wait_ge(ZAs[i % 8], a_za(i - 8))
                    if g >= 2:
                        sc.wait_ge(SV, A_CNT[("svar", g - 2)])
                    sc.activation(zst[:, (i % 8) * C:(i % 8 + 1) * C],
                                  xst[:, (i % 8) * C:(i % 8 + 1) * C],
                                  AF.Square,
                                  accum_out=sumsq[:, i % 8:i % 8 + 1]).then_inc(SS, 1)
                sc.wait_ge(SV, A_CNT[("varsub", g)])
                if g >= 2:
                    sc.wait_ge(SV, A_CNT[("recip", g - 2)])
                sc.activation(sstd[:, gb:gb + 4], svar[:, gb:gb + 4], AF.Sqrt,
                              bias=cconst[:, 0:1]).then_inc(SS, 1)
                if g % 2 == 1:
                    qtr = (g - 1) // 2
                    for s in range(8):
                        sc.wait_ge(ZAs[s], 16 * (qtr + 1))
                    for ctn in range(CT):
                        sc.dma_start(out=zT[ctn][:, qtr * 1024:(qtr + 1) * 1024],
                                     in_=zdram[qtr * 1024:(qtr + 1) * 1024,
                                               ctn * 128:(ctn + 1) * 128],
                                     transpose=True).then_inc(SZQ[qtr], 16)

        def emit_ln_fin(v, g, cnt, sq_last, sq_grp):
            gb = (g % 2) * 4
            cb = slice(4 * g % 8, 4 * g % 8 + 4)
            v.wait_ge(SS, sq_last)
            v.wait_ge(SV, cnt.get(("sum", 4 * g + 3), cnt.get(("x1", 4 * g + 3), 0)))
            v.tensor_scalar(smean[:, gb:gb + 4], sums[:, cb], cconst[:, 2:3],
                            None, ALU.mult).then_inc(SV, 1)
            v.tensor_scalar(svar[:, gb:gb + 4], sumsq[:, cb], cconst[:, 2:3],
                            None, ALU.mult).then_inc(SV, 1)
            v.wait_ge(SV, cnt[("smean", g)])
            v.tensor_tensor(sstd[:, gb:gb + 4], smean[:, gb:gb + 4],
                            smean[:, gb:gb + 4], ALU.mult).then_inc(SV, 1)
            v.wait_ge(SV, cnt[("msq", g)])
            v.tensor_tensor(svar[:, gb:gb + 4], svar[:, gb:gb + 4],
                            sstd[:, gb:gb + 4], ALU.subtract).then_inc(SV, 1)
            v.wait_ge(SS, sq_grp)
            v.reciprocal(srstd[:, gb:gb + 4], sstd[:, gb:gb + 4]).then_inc(SV, 1)
            v.wait_ge(SV, cnt[("recip", g)])
            v.scalar_tensor_tensor(snmr[:, gb:gb + 4], smean[:, gb:gb + 4],
                                   cconst[:, 1:2], srstd[:, gb:gb + 4],
                                   ALU.mult, ALU.mult).then_inc(SV, 1)

        @block.vector
        def _(v):
            v.memset(cconst[:, 0:1], EPS)
            v.memset(cconst[:, 1:2], -1.0)
            v.memset(cconst[:, 2:3], RC)
            v.memset(onec[:, :], 1.0)

            def a_apply_emit(v, g):
                gb = (g % 2) * 4
                for s in range(4):
                    i = g * 4 + s
                    if i >= 8:
                        v.wait_ge(ZAs[i % 8], a_za(i - 8))
                    v.wait_ge(SV, A_CNT[("nmr", g)])
                    v.tensor_scalar(zst[:, (i % 8) * C:(i % 8 + 1) * C],
                                    xst[:, (i % 8) * C:(i % 8 + 1) * C],
                                    srstd[:, gb + s:gb + s + 1],
                                    snmr[:, gb + s:gb + s + 1],
                                    ALU.mult, ALU.add).then_inc(SV, 1)

            def a_sums_emit(v, g):
                for s in range(4):
                    i = g * 4 + s
                    v.wait_ge(DSs[i % 8], a_ds(i))
                    if g >= 2:
                        v.wait_ge(SV, A_CNT[("smean", g - 2)])
                    v.reduce_sum(sums[:, i % 8:i % 8 + 1],
                                 xst[:, (i % 8) * C:(i % 8 + 1) * C],
                                 axis=AX.X).then_inc(SV, 1)

            a_sums_emit(v, 0)
            for g in range(1, 8):
                emit_ln_fin(v, g - 1, A_CNT, a_square(4 * (g - 1) + 3),
                            a_sqrt(g - 1))
                a_apply_emit(v, g - 1)
                a_sums_emit(v, g)
            emit_ln_fin(v, 7, A_CNT, a_square(31), a_sqrt(7))
            a_apply_emit(v, 7)


        cm_qk = [nc.psum_tensor(f"qk{i}", [128, 512], F32) for i in range(4)]
        qk = opens(cm_qk)
        cm_pvv = [nc.psum_tensor(f"pvv{i}", [128, 512], F32) for i in range(2)]
        pvv = opens(cm_pvv)

        @block.tensor
        def _(t):
            t.wait_ge(SW, SW_SMALL)
            for nch in range(8):
                buf = nch % 2
                t.wait_ge(SZQ[nch // 2], 128)
                if nch >= 2:
                    t.wait_ge(SV, c_qkev(nch - 2))
                for ct in range(CT):
                    t.matmul(qk[buf][:, :], wqk[:, ct * 128:(ct + 1) * 128],
                             zT[ct][:, nch * 512:(nch + 1) * 512],
                             start=(ct == 0), stop=(ct == CT - 1),
                             skip_group_check=True)
                for ct in range(CT):
                    ins = t.matmul(qk[2 + buf][:, :], wkq[:, ct * 128:(ct + 1) * 128],
                                   zT[ct][:, nch * 512:(nch + 1) * 512],
                                   start=(ct == 0), stop=(ct == CT - 1),
                                   skip_group_check=True)
                ins.then_inc(SM, 1)
            for nch in range(8):
                t.wait_ge(SZQ[nch // 2], 128)
                if nch >= 2:
                    t.wait_ge(SV, c_vev(nch - 2))
                for ct in range(CT):
                    ins = t.matmul(pvv[nch % 2][:, :],
                                   wvv[:, ct * 128:(ct + 1) * 128],
                                   zT[ct][:, nch * 512:(nch + 1) * 512],
                                   start=(ct == 0), stop=(ct == CT - 1),
                                   skip_group_check=True)
                ins.then_inc(SM, 1)

        @block.vector
        def _(v):
            for nch in range(8):
                sl = slice(nch * 512, (nch + 1) * 512)
                buf = nch % 2
                v.wait_ge(SM, c_qkmm(nch))
                v.tensor_copy(qrep[0:64, sl], qk[buf][0:64, :]).then_inc(SV, 1)
                v.tensor_copy(krep[64:128, sl], qk[buf][64:128, :]).then_inc(SV, 1)
                v.tensor_copy(krep[0:64, sl], qk[2 + buf][0:64, :]).then_inc(SV, 1)
                v.tensor_copy(qrep[64:128, sl], qk[2 + buf][64:128, :]).then_inc(SV, 1)
            for nch in range(8):
                sl = slice(nch * 512, (nch + 1) * 512)
                v.wait_ge(SM, c_vmm(nch))
                v.tensor_copy(vT_sb[0:64, sl], pvv[nch % 2][0:64, :]).then_inc(SV, 1)

        @block.sync
        def _(sy):
            sy.wait_ge(SV, SV_C)
            sy.dma_start(out=vdram, in_=vT_sb[0:64, :]).then_inc(SVT, 16)
            sy.wait_ge(SVT, 16)
            for kt in range(KT):
                sy.wait_ge(SVT, 16 * (kt + 1))
                sy.dma_start(out=v_sb[:, kt * D:(kt + 1) * D],
                             in_=vdram[:, kt * 128:(kt + 1) * 128],
                             transpose=True).then_inc(SVT, 16)

        closes(zT_cm)

        closes(cm_pvv)
        closes(cm_qk)
        out_cm = [
            nc.sbuf_tensor("out_tok", [128, TOWN * D], F32, side="right"),
        ]
        out_tok, = opens(out_cm)
        z1T_cm = [nc.sbuf_tensor(f"z1T{i}", [128, 2080], BF16, side="right") for i in range(CT)]
        z1T = opens(z1T_cm)
        out2_cm = [
            nc.sbuf_tensor("out_sb", [128, 1024], F32, side="right"),
            nc.sbuf_tensor("outx_sb", [64, 1], F32, side="right"),
            nc.sbuf_tensor("xp_sb", [128, KT], BF16, side="right"),
        ]
        out_sb, outx_sb, xp_sb = opens(out2_cm)
        cm_p23 = [nc.sbuf_tensor("p_sb1", [128, S], BF16, side="right"),
                  nc.sbuf_tensor("p_sb2", [128, S], BF16, side="right"),
                  nc.sbuf_tensor("p_sb3", [128, S], BF16, side="right")]
        p_sb.extend(opens(cm_p23))
        cm_dp = nc.psum_tensor("dpool", [128, 4096], F32)
        dpool = cm_dp.__enter__()
        units = [dpool[:, 1024 + u * 1024:2048 + u * 1024] for u in range(3)]

        @block.tensor
        def _(t):
            t.wait_ge(SV, SV_C)
            for q in range(KT // 2):
                if q >= 2:
                    for tt_ in (2 * (q - 2), 2 * (q - 2) + 1):
                        t.wait_ge(SV, d_vsc(tt_))
                        t.wait_ge(SS, d_exp(tt_, 3))
                        for w in range(4):
                            half = w // 2
                            t.matmul(dpool[64 * half:64 * half + 64,
                                           (w % 2) * 512:(w % 2) * 512 + 512],
                                     vs_sb[:, tt_ * D:(tt_ + 1) * D],
                                     p_sb[tt_ % 4][:, w * 512:(w + 1) * 512],
                                     start=(tt_ == 0), stop=(tt_ == KT - 1),
                                     skip_group_check=True,
                                     tile_position=(0, 64 * half)).then_inc(SM, 1)
                for s in range(4):
                    for a in range(2):
                        kt = 2 * q + a
                        e = d_eidx(kt, s)
                        u = e % 3
                        if e >= 3:
                            ktp_, sp_ = d_einv(e - 3)
                            t.wait_ge(SS, d_exp(ktp_, sp_))
                            if sp_ == 0:
                                t.wait_ge(SV, d_max(ktp_))
                        for cc in range(2):
                            c = 2 * s + cc
                            t.matmul(units[u][:, cc * 512:(cc + 1) * 512],
                                     krep[64 * a:64 * (a + 1), kt * 128:(kt + 1) * 128],
                                     qrep[64 * a:64 * (a + 1), c * 512:(c + 1) * 512],
                                     start=True, stop=True, skip_group_check=True,
                                     tile_position=(64 * a, 0)).then_inc(SM, 1)
            for tt_ in range(KT - 4, KT):
                t.wait_ge(SV, d_vsc(tt_))
                t.wait_ge(SS, d_exp(tt_, 3))
                for w in range(4):
                    half = w // 2
                    t.matmul(dpool[64 * half:64 * half + 64,
                                   (w % 2) * 512:(w % 2) * 512 + 512],
                             vs_sb[:, tt_ * D:(tt_ + 1) * D],
                             p_sb[tt_ % 4][:, w * 512:(w + 1) * 512],
                             start=(tt_ == 0), stop=(tt_ == KT - 1),
                             skip_group_check=True,
                             tile_position=(0, 64 * half)).then_inc(SM, 1)

        @block.vector
        def _(v):
            def d_maxes(v, q):
                for a in range(2):
                    kt = 2 * q + a
                    u0 = d_eidx(kt, 0) % 3
                    v.wait_ge(SM, d_smm(kt, 1))
                    v.reduce_max(negm[:, kt:kt + 1],
                                 units[u0][:, 0:512:4],
                                 axis=AX.X, negate=True).then_inc(SV, 1)

            d_maxes(v, 0)
            d_maxes(v, 1)
            for q in range(KT // 2):
                for a in range(2):
                    kt = 2 * q + a
                    v.wait_ge(SS, d_exp(kt, 3))
                    v.reduce_sum(ell[:, kt:kt + 1],
                                 ellp[:, 4 * (kt % 2):4 * (kt % 2) + 4],
                                 axis=AX.X).then_inc(SV, 1)
                    v.wait_ge(SV, d_ellsum(kt))
                    v.reciprocal(rell[:, kt:kt + 1], ell[:, kt:kt + 1]).then_inc(SV, 1)
                    v.wait_ge(SV, d_recip(kt))
                    v.wait_ge(SVT, 16 * (kt + 2))
                    v.tensor_scalar(vs_sb[:, kt * D:(kt + 1) * D],
                                    v_sb[:, kt * D:(kt + 1) * D],
                                    rell[:, kt:kt + 1], None, ALU.mult).then_inc(SV, 1)
                    v.tensor_copy(xp_sb[:, kt:kt + 1],
                                  p_sb[kt % 4][:, 4095:4096]).then_inc(SV, 1)
                if q + 2 < KT // 2:
                    d_maxes(v, q + 2)

        @block.scalar
        def _(sc):
            for q in range(KT // 2):
                for s in range(4):
                    for a in range(2):
                        kt = 2 * q + a
                        u = d_eidx(kt, s) % 3
                        if s == 0:
                            if kt == 0:
                                sc.wait_ge(SVT, 16)
                            sc.wait_ge(SV, d_max(kt))
                            if kt >= 2:
                                sc.wait_ge(SV, d_ellsum(kt - 2))
                            if kt >= 4:
                                sc.wait_ge(SM, d_outmm(kt - 4))
                        sc.wait_ge(SM, d_smm(kt, 2 * s + 1))
                        sc.activation(p_sb[kt % 4][:, s * 1024:(s + 1) * 1024],
                                      units[u][:, :], AF.Exp,
                                      bias=negm[:, kt:kt + 1],
                                      accum_out=ellp[:, 4 * (kt % 2) + s:
                                                     4 * (kt % 2) + s + 1]).then_inc(SS, 1)

        @block.gpsimd
        def _(g):
            g.wait_ge(SS, d_exp(7, 3))
            g.dma_start(out=wrT[:, :].rearrange("p (ct d) -> p ct d", d=C),
                        in_=wrT_in.rearrange("(ct p) d -> p ct d", p=128)).then_inc(SWB, 16)
            g.dma_start(out=wkT[:, :].rearrange("p (ct d) -> p ct d", d=C),
                        in_=wkT_in.rearrange("(ct p) d -> p ct d", p=128)).then_inc(SWB, 16)
            g.dma_start(out=wvT[:, :].rearrange("p (ct d) -> p ct d", d=C),
                        in_=wvT_in.rearrange("(ct p) d -> p ct d", p=128)).then_inc(SWB, 16)

        @block.vector
        def _(v):
            v.wait_ge(SM, d_outmm(KT - 1))
            v.tensor_copy(out_sb[0:64, 0:1024], dpool[0:64, 0:1024]).then_inc(SV, 1)
            v.tensor_copy(out_sb[64:128, 0:1024], dpool[64:128, 0:1024]).then_inc(SV, 1)

        if debug:
            @block.gpsimd
            def _(g):
                g.wait_ge(SV, e_evac)
                g.dma_start(out=d_ell[:, 0:KT], in_=ell[:, :]).then_inc(SW, 16)
                g.dma_start(out=d_ell[:, KT:2 * KT], in_=negm[:, :]).then_inc(SW, 16)
                g.dma_start(out=d_out[:, 0:1024], in_=out_sb[0:64, 0:1024]).then_inc(SW, 16)
                g.wait_ge(SW, SW_SMALL + 112)

        cm_dp.__exit__(None, None, None)
        cm_tp = [nc.psum_tensor(f"tp{i}", [128, 64], F32) for i in range(2)]
        tp_ps = opens(cm_tp)
        cm_oxe = nc.psum_tensor("oxe", [64, 64], F32)
        ox_e = cm_oxe.__enter__()

        @block.tensor
        def _(t):
            t.wait_ge(SV, e_evac)
            t.wait_ge(SS, SS_D)
            for b in range(16):
                if b >= 2:
                    t.wait_ge(SV, e_tpev(b - 2))
                hb = b // 8
                t.transpose(tp_ps[b % 2][:, :],
                            out_sb[64 * hb:64 * hb + 64,
                                   (b % 8) * 128:(b % 8 + 1) * 128],
                            ident[64 * hb:64 * hb + 64,
                                  64 * hb:64 * hb + 64]).then_inc(SM, 1)
            for kt in range(KT):
                t.matmul(ox_e[:, 0:1],
                         vs_sb[:, kt * D:(kt + 1) * D],
                         xp_sb[:, kt:kt + 1],
                         start=(kt == 0), stop=(kt == KT - 1),
                         skip_group_check=True).then_inc(SM, 1)

        @block.vector
        def _(v):
            for b in range(16):
                v.wait_ge(SM, e_tpmm(b))
                v.tensor_copy(out_tok[:, b * D:(b + 1) * D],
                              tp_ps[b % 2][:, :]).then_inc(SV, 1)
            v.wait_ge(SM, e_oxmm)
            v.tensor_copy(outx_sb[:, :], ox_e[:, 0:1]).then_inc(SV, 1)

        @block.sync
        def _(sy):
            sy.wait_ge(SV, e_oxev)
            sy.wait_ge(SD, SD_E0)
            sy.dma_start(out=xrow_d.rearrange("a (p o) -> p (a o)", p=64),
                         in_=outx_sb[0:64, 0:1]).then_inc(SD, 16)
            sy.wait_ge(SD, sd_xrowd)
            sy.dma_start(out=outx_row[:, :], in_=xrow_d).then_inc(SD, 16)
            sy.wait_ge(SD, sd_outxrow)
            sy.dma_start(out=xlast, in_=x_in[4095:4096, :]).then_inc(SD, 16)

        if debug:
            @block.gpsimd
            def _(g):
                g.wait_ge(SV, SV_E)
                g.dma_start(out=d_tok[:, 0:TOWN * D], in_=out_tok[:, :]).then_inc(SW, 16)
                g.dma_start(out=d_tok[:, TOWN * D:], in_=out_tok[:, 0:D]).then_inc(SW, 16)
                g.wait_ge(SW, SW_SMALL + 144)

        cm_oxe.__exit__(None, None, None)
        closes(cm_tp)
        closes(cm_p23)
        closes(attn_cm)
        x1f_cm = [nc.sbuf_tensor("x1b", [128, 8 * C], F32),
                  nc.sbuf_tensor("z1b", [128, 8 * C], BF16)]
        x1b, z1b = opens(x1f_cm)

        @block.gpsimd
        def _(g_):
            for t_ in range(TOWN):
                if t_ >= 8:
                    g_.wait_ge(SV, F_CNT[("apply", t_ - 8)])
                    g_.wait_ge(SS, f_square(t_ - 8))
                else:
                    g_.wait_ge(SV, A_CNT[("apply", 24 + t_)])
                    g_.wait_ge(SS, a_square(24 + t_))
                g_.dma_start(out=xst[:, (t_ % 8) * C:(t_ % 8 + 1) * C],
                             in_=x_in[t_ * 128:(t_ + 1) * 128, :]).then_inc(FSs[t_ % 8], 16)

        @block.sync
        def _(sy):
            sy.wait_ge(SV, SV_F0 + 10)
            sy.wait_ge(SD, sd_row63 - 16)
            sy.dma_start(out=z1dram[15:16, :], in_=z1xm).then_inc(SD, 16)
            sy.wait_ge(SV, SV_F0 + 11)
            sy.wait_ge(SD, sd_zero1 - 16)
            sy.dma_start(out=z1dram[0:15, :],
                         in_=bass.AP(zst, 0, [[8 * C, 15], [1, C]])).then_inc(SD, 16)
            sy.wait_ge(SD, sd_zero2 - 16)
            sy.dma_start(out=z1dram[2064:2080, :],
                         in_=bass.AP(zst, 0, [[8 * C, 16], [1, C]])).then_inc(SD, 16)
            for g in range(4):
                for s in range(4):
                    t_ = g * 4 + s
                    sy.wait_ge(SV, F_CNT[("apply", t_)])
                    sy.dma_start(out=z1dram[16 + t_ * 128:16 + (t_ + 1) * 128, :],
                                 in_=z1b[:, (t_ % 8) * C:(t_ % 8 + 1) * C]).then_inc(ZSs[t_ % 8], 16)

        @block.vector
        def _(v):
            def f_x1_sum(v, t_):
                s8 = t_ % 8
                v.wait_ge(FSs[s8], fs_f(t_))
                if t_ >= 8:
                    v.wait_ge(SS, f_square(t_ - 8))
                v.tensor_tensor(
                    x1b[:, s8 * C:(s8 + 1) * C],
                    xst[:, s8 * C:(s8 + 1) * C],
                    bass.AP(out_tok, t_ * D, [[TOWN * D, 128], [0, 16], [1, D]]),
                    ALU.add).then_inc(SV, 1)

            def f_apply_emit(v, g):
                gb = (g % 2) * 4
                for s in range(4):
                    t_ = g * 4 + s
                    if t_ >= 8:
                        v.wait_ge(ZSs[t_ % 8], zs_f(t_ - 8))
                    v.wait_ge(SV, F_CNT[("nmr", g)])
                    v.tensor_scalar(z1b[:, (t_ % 8) * C:(t_ % 8 + 1) * C],
                                    x1b[:, (t_ % 8) * C:(t_ % 8 + 1) * C],
                                    srstd[:, gb + s:gb + s + 1],
                                    snmr[:, gb + s:gb + s + 1],
                                    ALU.mult, ALU.add).then_inc(SV, 1)

            v.wait_ge(SD, sd_xlast)
            v.tensor_tensor(x1x, xlast,
                            bass.AP(outx_row, 0, [[D, 1], [0, 16], [1, D]]),
                            ALU.add).then_inc(SV, 1)
            v.wait_ge(SV, SV_F0 + 1)
            v.reduce_sum(rs[:, 0:1], x1x, axis=AX.X).then_inc(SV, 1)
            v.wait_ge(SS, SS_F0 + 1)
            v.wait_ge(SV, SV_F0 + 2)
            v.tensor_scalar(rs[:, 2:3], rs[:, 0:1], cconst[0:1, 2:3], None, ALU.mult).then_inc(SV, 1)
            v.tensor_scalar(rs[:, 3:4], rs[:, 1:2], cconst[0:1, 2:3], None, ALU.mult).then_inc(SV, 1)
            v.wait_ge(SV, SV_F0 + 3)
            v.tensor_tensor(rs[:, 4:5], rs[:, 2:3], rs[:, 2:3], ALU.mult).then_inc(SV, 1)
            v.wait_ge(SV, SV_F0 + 5)
            v.tensor_tensor(rs[:, 5:6], rs[:, 3:4], rs[:, 4:5], ALU.subtract).then_inc(SV, 1)
            v.wait_ge(SS, SS_TAIL)
            v.reciprocal(rs[:, 7:8], rs[:, 6:7]).then_inc(SV, 1)
            v.wait_ge(SV, SV_F0 + 7)
            v.scalar_tensor_tensor(rs[:, 8:9], rs[:, 2:3], cconst[0:1, 1:2], rs[:, 7:8],
                                   ALU.mult, ALU.mult).then_inc(SV, 1)
            v.wait_ge(SV, SV_F0 + 8)
            v.tensor_scalar(z1x, x1x, rs[:, 7:8], rs[:, 8:9],
                            ALU.mult, ALU.add).then_inc(SV, 1)
            v.wait_ge(SW, SW_SMALL)
            v.wait_ge(SV, SV_F0 + 9)
            v.tensor_scalar(z1xm, z1x, hmask[0:1, 0:1], None,
                            ALU.mult).then_inc(SV, 1)
            v.memset(zst[:, 0:C], 0.0).then_inc(SV, 1)
            for s in range(4):
                f_x1_sum(v, s)
            for g in range(1, 4):
                emit_ln_fin(v, g - 1, F_CNT, f_square(4 * (g - 1) + 3),
                            f_sqrt(g - 1))
                f_apply_emit(v, g - 1)
                for s in range(4):
                    f_x1_sum(v, g * 4 + s)
            emit_ln_fin(v, 3, F_CNT, f_square(15), f_sqrt(3))
            f_apply_emit(v, 3)

        @block.scalar
        def _(sc):
            sc.wait_ge(SV, SV_F0 + 1)
            sc.activation(jrow, x1x, AF.Square,
                          accum_out=rs[:, 1:2]).then_inc(SS, 1)
            sc.wait_ge(SV, SV_F0 + 6)
            sc.activation(rs[:, 6:7], rs[:, 5:6], AF.Sqrt, bias=cconst[0:1, 0:1]).then_inc(SS, 1)
            for g in range(4):
                gb = (g % 2) * 4
                for s in range(4):
                    t_ = g * 4 + s
                    sc.wait_ge(SV, F_CNT[("x1", t_)])
                    sc.wait_ge(SS, f_sumss(t_) - 1)
                    if g >= 2:
                        sc.wait_ge(SV, F_CNT[("svar", g - 2)])
                        sc.wait_ge(SV, F_CNT[("smean", g - 2)])
                    sc.activation(zst[:, 2 * C:3 * C], x1b[:, (t_ % 8) * C:(t_ % 8 + 1) * C],
                                  AF.Copy,
                                  accum_out=sums[:, t_ % 8:t_ % 8 + 1]).then_inc(SS, 1)
                    sc.wait_ge(SS, f_sumss(t_))
                    sc.activation(zst[:, 2 * C:3 * C], x1b[:, (t_ % 8) * C:(t_ % 8 + 1) * C],
                                  AF.Square,
                                  accum_out=sumsq[:, t_ % 8:t_ % 8 + 1]).then_inc(SS, 1)
                sc.wait_ge(SV, F_CNT[("varsub", g)])
                if g >= 2:
                    sc.wait_ge(SV, F_CNT[("recip", g - 2)])
                sc.activation(sstd[:, gb:gb + 4], svar[:, gb:gb + 4], AF.Sqrt,
                              bias=cconst[:, 0:1]).then_inc(SS, 1)

        @block.scalar
        def _(sc):
            sc.wait_ge(SD, sd_zero1)
            for s in range(8):
                sc.wait_ge(ZSs[s], 16)
            for ctn in range(CT):
                sc.dma_start(out=z1T[ctn][:, 0:1040],
                             in_=z1dram[0:1040, ctn * 128:(ctn + 1) * 128],
                             transpose=True).then_inc(SG_LO, 16)
        @block.sync
        def _(sy):
            sy.wait_ge(SD, sd_zero2)
            for s in range(8):
                sy.wait_ge(ZSs[s], 32)
            sy.wait_ge(SG_LO, 128)
            for ctn in range(CT):
                sy.wait_ge(SG_HI, 16 * ctn)
                sy.dma_start(out=z1T[ctn][:, 1040:2080],
                             in_=z1dram[1040:2080, ctn * 128:(ctn + 1) * 128],
                             transpose=True).then_inc(SG_HI, 16)

        if debug:
            @block.gpsimd
            def _(g):
                g.wait_ge(SG_HI, 128)
                g.dma_start(out=d_z1, in_=z1dram).then_inc(SW, 16)
                g.wait_ge(SW, SW_SMALL + 160)

        closes(x1f_cm)
        closes(out2_cm)
        xk_cm = [nc.sbuf_tensor(f"xk{i}", [128, 2048], BF16) for i in range(CT)]
        xk = opens(xk_cm)
        xr_cm = [nc.sbuf_tensor(f"xr{i}", [128, 2048], BF16) for i in range(CT)]
        xr = opens(xr_cm)
        dbuf = zst[:, 0:2048]

        @block.scalar
        def _(sc):
            sc.wait_ge(SV, F_CNT[("apply", 15)])
            for ct in range(CT):
                sc.wait_ge(SG_HI, szt_g_hi(ct))
                sc.activation(xr[ct][:, :], z1T[ct][:, 16:2064], AF.Identity,
                              bias=0.0, scale=mu[:, ct:ct + 1]).then_inc(SS, 1)
            for ct in range(CT):
                sc.activation(xk[ct][:, :], z1T[ct][:, 16:2064], AF.Identity,
                              bias=0.0, scale=mu[:, 8 + ct:9 + ct]).then_inc(SS, 1)

        @block.vector
        def _(v):
            v.wait_ge(SS, f_square(15))
            v.scalar_tensor_tensor(mu1m[:, :], mu[:, :], -1.0, onec[:, :],
                                   ALU.mult, ALU.add).then_inc(SV, 1)
            v.wait_ge(SV, SV_H0)
            for ct in range(CT):
                v.wait_ge(SS, h_tsr(ct))
                v.wait_ge(SV, SV_H0 + ct)
                v.scalar_tensor_tensor(xr[ct][:, :], z1T[ct][:, 15:2063],
                                       mu1m[:, ct:ct + 1],
                                       xr[ct][:, :], ALU.mult,
                                       ALU.add).then_inc(SV, 1)
            for ct in range(CT):
                v.wait_ge(SS, h_tsk(ct))
                v.wait_ge(SV, SV_H0 + 8 + ct)
                v.scalar_tensor_tensor(xk[ct][:, :], z1T[ct][:, 15:2063],
                                       mu1m[:, 8 + ct:9 + ct],
                                       xk[ct][:, :], ALU.mult,
                                       ALU.add).then_inc(SV, 1)

        closes(z1T_cm)
        r_sb_cm = [nc.sbuf_tensor("r_sb", [128, TOWN * C], BF16, side="right")]
        r_sb, = opens(r_sb_cm)
        cm_rps = [nc.psum_tensor(f"rps{i}", [128, C], F32) for i in range(3)]
        r_ps = opens(cm_rps)

        @block.tensor
        def _(t):
            t.wait_ge(SV, h_xr_all)
            t.wait_ge(SS, SS_F)
            t.wait_ge(SWB, SWB_ALL)
            for t_ in range(TOWN):
                if t_ >= 3:
                    t.wait_ge(SS, i_sig(t_ - 3))
                for ct in range(CT):
                    for hh in range(2):
                        ins = t.matmul(
                            r_ps[t_ % 3][:, hh * 512:(hh + 1) * 512],
                            xr[ct][:, t_ * 128:(t_ + 1) * 128],
                            wrT[:, ct * C + hh * 512:ct * C + (hh + 1) * 512],
                            start=(ct == 0), stop=(ct == CT - 1),
                            skip_group_check=True)
                ins.then_inc(SM, 1)

        @block.scalar
        def _(sc):
            sc.wait_ge(SV, SV_H)
            for t_ in range(TOWN):
                sc.wait_ge(SM, i_mm(t_))
                sc.activation(r_sb[:, t_ * C:(t_ + 1) * C], r_ps[t_ % 3][:, :],
                              AF.Sigmoid).then_inc(SS, 1)

        closes(cm_rps)
        closes(xr_cm)

        if debug:
            @block.gpsimd
            def _(g):
                g.wait_ge(SS, SS_I)
                g.dma_start(out=d_r, in_=r_sb[:, :]).then_inc(SW, 16)
                g.wait_ge(SW, SW_SMALL + 192)

        kaT_cm = [nc.sbuf_tensor(f"kaT{i}", [128, 2048], BF16, side="right") for i in range(CT)]
        kaT = opens(kaT_cm)
        cm_ktp = [nc.psum_tensor(f"ktp{i}", [128, 512], F32) for i in range(3)]
        kt_ps = opens(cm_ktp)

        @block.tensor
        def _(t):
            t.wait_ge(SS, SS_I)
            for dt in range(CT):
                for tc in range(4):
                    ix = dt * 4 + tc
                    if ix >= 3:
                        t.wait_ge(SS, j_relu(ix - 3))
                    for ct in range(CT):
                        ins = t.matmul(
                            kt_ps[ix % 3][:, :],
                            wkT[:, ct * C + dt * 128:ct * C + (dt + 1) * 128],
                            xk[ct][:, tc * 512:(tc + 1) * 512],
                            start=(ct == 0), stop=(ct == CT - 1),
                            skip_group_check=True)
                    ins.then_inc(SM, 1)

        @block.scalar
        def _(sc):
            for dt in range(CT):
                for tc in range(4):
                    ix = dt * 4 + tc
                    sc.wait_ge(SM, j_mm(ix))
                    if ix >= 4:
                        sc.wait_ge(SV, j_sq(ix - 4))
                    sc.activation(kaT[dt][:, tc * 512:(tc + 1) * 512],
                                  kt_ps[ix % 3][:, :], AF.Relu).then_inc(SS, 1)

        @block.vector
        def _(v):
            for dt in range(CT):
                for tc in range(4):
                    ix = dt * 4 + tc
                    v.wait_ge(SS, j_relu(ix))
                    v.tensor_tensor(kaT[dt][:, tc * 512:(tc + 1) * 512],
                                    kaT[dt][:, tc * 512:(tc + 1) * 512],
                                    kaT[dt][:, tc * 512:(tc + 1) * 512],
                                    ALU.mult).then_inc(SV, 1)

        closes(cm_ktp)
        closes(xk_cm)

        if debug:
            @block.gpsimd
            def _(g):
                g.wait_ge(SV, SV_J)
                for dt in range(CT):
                    g.dma_start(out=d_ka[:, dt * 2048:(dt + 1) * 2048],
                                in_=kaT[dt][:, :]).then_inc(SW, 16)
                g.wait_ge(SW, SW_SMALL + 192 + 128)

        k32_cm = [nc.sbuf_tensor("tmp32", [128, 2 * C], F32, side="right"),
                  nc.sbuf_tensor("x2b", [128, 2 * C], F32, side="right")]
        tmp32, x2b = opens(k32_cm)
        cm_ops = [nc.psum_tensor(f"ops{i}", [128, C], F32) for i in range(3)]
        o_ps = opens(cm_ops)

        @block.tensor
        def _(t):
            t.wait_ge(SV, SV_J)
            t.wait_ge(SS, SS_J)
            for t_ in range(TOWN):
                if t_ >= 3:
                    t.wait_ge(SV, k_tmp(t_ - 3))
                for dt in range(CT):
                    for hh in range(2):
                        ins = t.matmul(
                            o_ps[t_ % 3][:, hh * 512:(hh + 1) * 512],
                            kaT[dt][:, t_ * 128:(t_ + 1) * 128],
                            wvT[:, dt * C + hh * 512:dt * C + (hh + 1) * 512],
                            start=(dt == 0), stop=(dt == CT - 1),
                            skip_group_check=True)
                ins.then_inc(SM, 1)

        @block.gpsimd
        def _(g):
            def k_xadd(g, t_):
                g.wait_ge(FSs[t_ % 8], fs_k(t_))
                g.tensor_tensor(
                    xst[:, (t_ % 8) * C:(t_ % 8 + 1) * C],
                    xst[:, (t_ % 8) * C:(t_ % 8 + 1) * C],
                    bass.AP(out_tok, t_ * D, [[TOWN * D, 128], [0, 16], [1, D]]),
                    ALU.add).then_inc(SH, 1)

            for t_ in range(TOWN):
                if t_ >= 4:
                    g.wait_ge(SV, k_x2(t_ - 4))
                else:
                    g.wait_ge(SV, SV_J)
                if t_ >= 8:
                    g.wait_ge(FSs[t_ % 8], fs_k(t_ - 8))
                g.dma_start(out=xst[:, (t_ % 8) * C:(t_ % 8 + 1) * C],
                            in_=x_in[t_ * 128:(t_ + 1) * 128, :]).then_inc(FSs[t_ % 8], 16)
                if t_ >= 1:
                    k_xadd(g, t_ - 1)
            k_xadd(g, TOWN - 1)

        def k_ydma_emit(sy, t_):
            sy.wait_ge(SV, k_x2(t_))
            sy.dma_start(out=y_out[t_ * 128:(t_ + 1) * 128, :],
                         in_=x2b[:, (t_ % 2) * C:(t_ % 2 + 1) * C]).then_inc(YSs[t_ % 2], 16)

        @block.sync
        def _(sy):
            for t_ in range(TOWN):
                if t_ >= 2:
                    k_ydma_emit(sy, t_ - 2)
            k_ydma_emit(sy, TOWN - 2)
            k_ydma_emit(sy, TOWN - 1)
            sy.wait_ge(YSs[0], 16 * (TOWN // 2))
            sy.wait_ge(YSs[1], 16 * (TOWN // 2))

        @block.vector
        def _(v):
            for t_ in range(TOWN):
                s = t_ % 4
                v.wait_ge(SM, k_mm(t_))
                v.wait_ge(SS, i_sig(t_))
                if t_ >= 2:
                    v.wait_ge(SV, k_x2(t_ - 2))
                    v.wait_ge(YSs[t_ % 2], ys_k(t_ - 2))
                v.tensor_tensor(tmp32[:, (t_ % 2) * C:(t_ % 2 + 1) * C],
                                o_ps[t_ % 3][:, :],
                                r_sb[:, t_ * C:(t_ + 1) * C], ALU.mult).then_inc(SV, 1)
                v.wait_ge(SH, t_ + 1)
                v.wait_ge(SV, k_tmp(t_))
                v.tensor_tensor(x2b[:, (t_ % 2) * C:(t_ % 2 + 1) * C],
                                tmp32[:, (t_ % 2) * C:(t_ % 2 + 1) * C],
                                xst[:, (t_ % 8) * C:(t_ % 8 + 1) * C], ALU.add).then_inc(SV, 1)


        closes(cm_ops)
        closes(k32_cm)
        closes(kaT_cm)
        closes(r_sb_cm)
        closes(out_cm)

    return nc


def kernel(L0_attn_x, L0_attn_Wq, L0_attn_Wk, L0_attn_Wv,
           L0_ffn_W_r, L0_ffn_W_k, L0_ffn_W_v, L0_ffn_mu_r, L0_ffn_mu_k):
    x = np.ascontiguousarray(np.asarray(L0_attn_x, dtype=np.float32))
    Wq = np.asarray(L0_attn_Wq, dtype=np.float32)
    Wk = np.asarray(L0_attn_Wk, dtype=np.float32)
    Wv = np.asarray(L0_attn_Wv, dtype=np.float32)
    Wr = np.asarray(L0_ffn_W_r, dtype=np.float32)
    Wkf = np.asarray(L0_ffn_W_k, dtype=np.float32)
    Wvf = np.asarray(L0_ffn_W_v, dtype=np.float32)
    mur = np.asarray(L0_ffn_mu_r, dtype=np.float32)
    muk = np.asarray(L0_ffn_mu_k, dtype=np.float32)

    wqk = np.ascontiguousarray(np.concatenate([Wq, Wk], axis=1))
    wkq = np.ascontiguousarray(np.concatenate([Wk, Wq], axis=1))
    wvv = np.ascontiguousarray(np.concatenate([Wv, Wv], axis=1))
    wrT = np.ascontiguousarray(Wr.T)
    wkT = np.ascontiguousarray(Wkf.T)
    wvT = np.ascontiguousarray(Wvf.T)
    mu_sb = np.ascontiguousarray(
        np.concatenate([mur.reshape(8, 128).T, muk.reshape(8, 128).T], axis=1))
    ident = np.ascontiguousarray(np.eye(128, dtype=np.float32))

    if "nc" not in _PROGRAM_CACHE:
        _PROGRAM_CACHE["nc"] = build_program()
    nc = _PROGRAM_CACHE["nc"]

    in_maps = []
    for c in range(8):
        b, h = c // 2, c % 2
        own = x[b, h * 2048:(h + 1) * 2048]
        other = x[b, (1 - h) * 2048:(2 - h) * 2048]
        xp = np.ascontiguousarray(np.concatenate([own, other], axis=0))
        in_maps.append({
            "x_in": xp,
            "wqk_in": wqk, "wkq_in": wkq, "wvv_in": wvv,
            "wrT_in": wrT, "wkT_in": wkT, "wvT_in": wvT,
            "mu_in": mu_sb,
            "hmask_in": np.array([[float(h)]], dtype=np.float32),
            "ident_in": ident,
        })

    res = run_bass_kernel_spmd(nc, in_maps, core_ids=list(range(8)))

    y = np.empty((4, S, C), dtype=np.float32)
    for c in range(8):
        b, h = c // 2, c % 2
        y[b, h * 2048:(h + 1) * 2048] = np.asarray(res.results[c]["y_out"])
    return y


# version 25
# speedup vs baseline: 1.1492x; 1.0118x over previous
import numpy as np
from contextlib import ExitStack

import concourse.bass as bass
import concourse.mybir as mybir
from concourse.bass_utils import run_bass_kernel_spmd

F32 = mybir.dt.float32
BF16 = mybir.dt.bfloat16
AF = mybir.ActivationFunctionType
ALU = mybir.AluOpType
AX = mybir.AxisListType

S = 4096
C = 1024
D = 64
TOWN = 16
TT = 32
CT = 8
KT = 32
EPS = 1e-5
RC = 1.0 / C

_PROGRAM_CACHE = {}


def build_program(debug=False):
    nc = bass.Bass("TRN2", target_bir_lowering=False)

    x_in = nc.dram_tensor("x_in", [S, C], F32, kind="ExternalInput").ap()
    wqk_in = nc.dram_tensor("wqk_in", [C, 128], F32, kind="ExternalInput").ap()
    wkq_in = nc.dram_tensor("wkq_in", [C, 128], F32, kind="ExternalInput").ap()
    wvv_in = nc.dram_tensor("wvv_in", [C, 128], F32, kind="ExternalInput").ap()
    wrT_in = nc.dram_tensor("wrT_in", [C, C], F32, kind="ExternalInput").ap()
    wkT_in = nc.dram_tensor("wkT_in", [C, C], F32, kind="ExternalInput").ap()
    wvT_in = nc.dram_tensor("wvT_in", [C, C], F32, kind="ExternalInput").ap()
    mu_in = nc.dram_tensor("mu_in", [128, 16], F32, kind="ExternalInput").ap()
    hmask_in = nc.dram_tensor("hmask_in", [1, 1], F32, kind="ExternalInput").ap()
    ident_in = nc.dram_tensor("ident_in", [128, 128], F32, kind="ExternalInput").ap()
    y_out = nc.dram_tensor("y_out", [2048, C], F32, kind="ExternalOutput").ap()

    if debug:
        d_z = nc.dram_tensor("d_z", [S, C], F32, kind="ExternalOutput").ap()
        d_qk = nc.dram_tensor("d_qk", [256, S], F32, kind="ExternalOutput").ap()
        d_v = nc.dram_tensor("d_v", [128, KT * D], F32, kind="ExternalOutput").ap()
        d_out = nc.dram_tensor("d_out", [64, 2176], F32, kind="ExternalOutput").ap()
        d_ell = nc.dram_tensor("d_ell", [128, 2 * KT], F32, kind="ExternalOutput").ap()
        d_tok = nc.dram_tensor("d_tok", [128, TOWN * D + D], F32, kind="ExternalOutput").ap()
        d_z1 = nc.dram_tensor("d_z1", [2080, C], F32, kind="ExternalOutput").ap()
        d_x1 = nc.dram_tensor("d_x1", [2048, C], F32, kind="ExternalOutput").ap()
        d_r = nc.dram_tensor("d_r", [128, TOWN * C], F32, kind="ExternalOutput").ap()
        d_ka = nc.dram_tensor("d_ka", [128, CT * 2048], F32, kind="ExternalOutput").ap()
    zdram = nc.dram_tensor("zdram", [S, C], BF16).ap()
    z1dram = nc.dram_tensor("z1dram", [2080, C], BF16).ap()
    x1dram = nc.dram_tensor("x1dram", [2048, C], F32).ap()
    xrow_d = nc.dram_tensor("xrow_d", [1, D], F32).ap()
    vdram = nc.dram_tensor("vdram", [64, S], BF16).ap()

    ctx = ExitStack()
    e = ctx.enter_context
    with ctx:
        wqk = e(nc.sbuf_tensor("wqk", [128, 8 * 128], BF16))
        wkq = e(nc.sbuf_tensor("wkq", [128, 8 * 128], BF16))
        wvv = e(nc.sbuf_tensor("wvv", [128, 8 * 128], BF16))
        wrT = e(nc.sbuf_tensor("wrT", [128, 8 * C], BF16))
        wkT = e(nc.sbuf_tensor("wkT", [128, 8 * C], BF16))
        wvT = e(nc.sbuf_tensor("wvT", [128, 8 * C], BF16))
        mu = e(nc.sbuf_tensor("mu", [128, 16], F32))
        mu1m = e(nc.sbuf_tensor("mu1m", [128, 16], F32))
        onec = e(nc.sbuf_tensor("onec", [128, 16], F32))
        hmask = e(nc.sbuf_tensor("hmask", [1, 1], F32))
        ident = e(nc.sbuf_tensor("ident", [128, 128], F32))

        cconst = e(nc.sbuf_tensor("cconst", [128, 4], F32))
        xst = e(nc.sbuf_tensor("xst", [128, 8 * C], F32))
        zst = e(nc.sbuf_tensor("zst", [128, 8 * C], BF16))
        sums = e(nc.sbuf_tensor("sums", [128, 8], F32))
        sumsq = e(nc.sbuf_tensor("sumsq", [128, 8], F32))
        smean = e(nc.sbuf_tensor("smean", [128, 8], F32))
        svar = e(nc.sbuf_tensor("svar", [128, 8], F32))
        sstd = e(nc.sbuf_tensor("sstd", [128, 8], F32))
        srstd = e(nc.sbuf_tensor("srstd", [128, 8], F32))
        snmr = e(nc.sbuf_tensor("snmr", [128, 8], F32))

        rowf = e(nc.sbuf_tensor("rowf", [128, C], F32))
        xlast = rowf[0:1, :]
        x1x = rowf[32:33, :]
        z1x = zst[0:1, C:2 * C]
        z1xm = zst[32:33, C:2 * C]
        jrow = zst[64:65, C:2 * C]
        outx_row = e(nc.sbuf_tensor("outx_row", [1, D], F32))
        rs = e(nc.sbuf_tensor("rs", [1, 12], F32))

        def opens(cms):
            return [cm.__enter__() for cm in cms]

        def closes(cms):
            for cm in reversed(cms):
                cm.__exit__(None, None, None)

        zT_cm = [nc.sbuf_tensor(f"zT{i}", [128, S], BF16, side="right") for i in range(CT)]
        zT = opens(zT_cm)
        attn_cm = [
            nc.sbuf_tensor("qrep", [128, S], BF16),
            nc.sbuf_tensor("krep", [128, S], BF16),
            nc.sbuf_tensor("v_sb", [128, KT * D], BF16),
            nc.sbuf_tensor("p_sb0", [128, S], BF16),
            nc.sbuf_tensor("negm", [128, KT], F32),
            nc.sbuf_tensor("ellp", [128, 16], F32),
            nc.sbuf_tensor("ell", [128, KT], F32),
            nc.sbuf_tensor("rell", [128, KT], F32),
        ]
        (qrep, krep, v_sb, p_sb0, negm, ellp, ell, rell) = opens(attn_cm)
        vs_sb = v_sb
        vT_sb = p_sb0
        p_sb = [p_sb0]

        SW = e(nc.semaphore("SW"))
        SX = e(nc.semaphore("SX"))
        SD = e(nc.semaphore("SD"))
        DSs = [e(nc.semaphore(f"DS{i}")) for i in range(8)]
        ZSs = [e(nc.semaphore(f"ZS{i}")) for i in range(8)]
        XSs = [e(nc.semaphore(f"XS{i}")) for i in range(8)]
        YSs = [e(nc.semaphore(f"YS{i}")) for i in range(2)]
        SVT = e(nc.semaphore("SVT"))
        SH = e(nc.semaphore("SH"))
        SWB = e(nc.semaphore("SWB"))
        FSs = [e(nc.semaphore(f"FS{i}")) for i in range(8)]
        ZAs = [e(nc.semaphore(f"ZA{i}")) for i in range(8)]
        SZQ = [e(nc.semaphore(f"SZQ{i}")) for i in range(4)]
        SG_LO = e(nc.semaphore("SG_LO"))
        SG_HI = e(nc.semaphore("SG_HI"))
        SM = e(nc.semaphore("SM"))
        SV = e(nc.semaphore("SV"))
        SS = e(nc.semaphore("SS"))
        SY = e(nc.semaphore("SY"))

        block = e(nc.Block())

        fs_f = lambda t: 16 * (t // 8 + 1)
        fs_k = lambda t: 32 + 16 * (t // 8 + 1)
        zs_f = lambda t: 16 * (t // 8 + 1)
        xs_f = lambda t: 16 * (t // 8 + 1)
        ys_k = lambda t: 16 * (t // 2 + 1)
        def plan_ln(n_tiles, sv0, extra_per_tile):
            ng = n_tiles // 4
            cnt = {}
            sv = sv0
            for g in range(ng):
                for s4 in range(4):
                    i = 4 * g + s4
                    for op in extra_per_tile:
                        sv += 1
                        cnt[(op, i)] = sv
                    sv += 1
                    cnt[("sum", i)] = sv
                if g >= 1:
                    for op in ("smean", "svar", "msq", "varsub", "recip", "nmr"):
                        sv += 1
                        cnt[(op, g - 1)] = sv
                    for s4 in range(4):
                        sv += 1
                        cnt[("apply", 4 * (g - 1) + s4)] = sv
            for op in ("smean", "svar", "msq", "varsub", "recip", "nmr"):
                sv += 1
                cnt[(op, ng - 1)] = sv
            for s4 in range(4):
                sv += 1
                cnt[("apply", 4 * (ng - 1) + s4)] = sv
            return cnt, sv

        def plan_ln_af(n_tiles, sv0):
            ng = n_tiles // 4
            cnt = {}
            sv = sv0
            for s4 in range(4):
                sv += 1
                cnt[("sum", s4)] = sv
            for g in range(1, ng + 1):
                gm = g - 1
                for op in ("smean", "svar", "msq", "varsub", "recip", "nmr"):
                    sv += 1
                    cnt[(op, gm)] = sv
                for s4 in range(4):
                    sv += 1
                    cnt[("apply", 4 * gm + s4)] = sv
                if g < ng:
                    for s4 in range(4):
                        sv += 1
                        cnt[("sum", 4 * g + s4)] = sv
            return cnt, sv

        A_CNT, SV_A = plan_ln_af(TT, 0)
        a_square = lambda i: 5 * (i // 4) + (i % 4) + 1
        a_sqrt = lambda g: 5 * g + 5
        SS_A = 5 * 8
        a_ds = lambda i: 16 * (i // 8 + 1)
        a_za = lambda i: 16 * (i // 8 + 1)
        SM_C0, SV_C0 = 0, SV_A
        c_qkmm = lambda n: SM_C0 + n + 1
        c_vmm = lambda n: SM_C0 + 8 + n + 1
        c_qkev = lambda n: SV_C0 + 4 * n + 4
        c_vev = lambda n: SV_C0 + 32 + n + 1
        SM_C = SM_C0 + 16
        SV_C = SV_C0 + 40
        SM_D0, SS_D0, SV_D0 = SM_C, SS_A, SV_C
        d_P = lambda q: 0 if q == 0 else 24 * q - 8
        d_smm = lambda kt, c: (SM_D0 + d_P(kt // 2) + 4 * (c // 2) + 2 * (kt % 2)
                               + (c % 2) + 1)
        d_eidx = lambda kt, s: 2 * (4 * (kt // 2) + s) + (kt % 2)

        def d_einv(e):
            sig, a = e // 2, e % 2
            return 2 * (sig // 4) + a, sig % 4

        def d_outmm(t):
            q = t // 2
            if q <= 13:
                return SM_D0 + d_P(q + 2) - 4 * (1 - (t % 2))
            return SM_D0 + 368 + 8 * (q - 14) + 4 * (t % 2 + 1)
        d_exp = lambda kt, s: SS_D0 + 8 * (kt // 2) + 2 * s + (kt % 2) + 1
        D_CNT = {}
        _sv = SV_D0
        for _kt in range(4):
            _sv += 1
            D_CNT[("max", _kt)] = _sv
        for _q in range(16):
            for _a in range(2):
                _kt = 2 * _q + _a
                for _op in ("ellsum", "recip", "vsc", "xp"):
                    _sv += 1
                    D_CNT[(_op, _kt)] = _sv
            if _q + 2 < 16:
                for _a in range(2):
                    _sv += 1
                    D_CNT[("max", 2 * (_q + 2) + _a)] = _sv
        d_max = lambda kt: D_CNT[("max", kt)]
        d_ellsum = lambda kt: D_CNT[("ellsum", kt)]
        d_recip = lambda kt: D_CNT[("recip", kt)]
        d_vsc = lambda kt: D_CNT[("vsc", kt)]
        d_xp = lambda kt: D_CNT[("xp", kt)]
        SM_D = SM_D0 + 384
        SS_D = SS_D0 + 128
        SV_D = _sv
        assert SV_D == SV_D0 + 160
        e_evac = SV_D + 2
        e_tpmm = lambda b: SM_D + b + 1
        e_oxmm = SM_D + 16 + KT
        e_tpev = lambda b: e_evac + b + 1
        e_oxev = e_evac + 17
        SM_E = e_oxmm
        SV_E = e_oxev
        SV_F0, SS_F0 = SV_E, SS_D
        SV_TAIL = SV_F0 + 11
        SS_TAIL = SS_F0 + 2

        def plan_ln_af2(n_tiles, sv0):
            ng = n_tiles // 4
            cnt = {}
            sv = sv0
            for s4 in range(4):
                sv += 1
                cnt[("x1", s4)] = sv
            for g in range(1, ng + 1):
                gm = g - 1
                for op in ("smean", "svar", "msq", "varsub", "recip", "nmr"):
                    sv += 1
                    cnt[(op, gm)] = sv
                for s4 in range(4):
                    sv += 1
                    cnt[("apply", 4 * gm + s4)] = sv
                if g < ng:
                    for s4 in range(4):
                        i = 4 * g + s4
                        sv += 1
                        cnt[("x1", i)] = sv
            return cnt, sv

        F_CNT, SV_Floop = plan_ln_af2(TOWN, SV_TAIL)
        f_sumss = lambda t: SS_TAIL + 9 * (t // 4) + 2 * (t % 4) + 1
        f_square = lambda t: SS_TAIL + 9 * (t // 4) + 2 * (t % 4) + 2
        f_sqrt = lambda g: SS_TAIL + 9 * g + 9
        SV_F = SV_Floop
        SS_F = SS_TAIL + 36
        SD_E0 = 0
        sd_xrowd = SD_E0 + 16
        sd_outxrow = SD_E0 + 32
        sd_xlast = SD_E0 + 48
        sd_row63 = SD_E0 + 64
        sd_zero1 = SD_E0 + 80
        sd_zero2 = SD_E0 + 96
        SD_F = sd_zero2
        szt_g_hi = lambda ct: 16 * (ct + 1)
        SS_H0 = SS_F
        h_tsr = lambda ct: SS_H0 + ct + 1
        h_tsk = lambda ct: SS_H0 + 8 + ct + 1
        SS_H = SS_H0 + 16
        SV_H0 = SV_F + 1
        h_xr_all = SV_H0 + 8
        SV_H = SV_H0 + 16
        SH_ALL = 0
        SM_I0, SS_I0 = SM_E, SS_H
        i_mm = lambda t: SM_I0 + t + 1
        i_sig = lambda t: SS_I0 + t + 1
        SM_I = SM_I0 + 16
        SS_I = SS_I0 + 16
        SM_J0, SS_J0, SV_J0 = SM_I, SS_I, SV_H
        j_mm = lambda ix: SM_J0 + ix + 1
        j_relu = lambda ix: SS_J0 + ix + 1
        j_sq = lambda ix: SV_J0 + ix + 1
        SM_J = SM_J0 + 32
        SS_J = SS_J0 + 32
        SV_J = SV_J0 + 32
        SM_K0, SV_K0 = SM_J, SV_J
        k_mm = lambda t: SM_K0 + t + 1
        k_tmp = lambda t: SV_K0 + 2 * t + 1
        k_x2 = lambda t: SV_K0 + 2 * t + 2
        k_ydma = lambda t: 16 * (t + 1)

        @block.gpsimd
        def _(g):
            g.dma_start(out=wqk[:, :].rearrange("p (ct d) -> p ct d", d=128),
                        in_=wqk_in.rearrange("(ct p) d -> p ct d", p=128)).then_inc(SW, 16)
            g.dma_start(out=wkq[:, :].rearrange("p (ct d) -> p ct d", d=128),
                        in_=wkq_in.rearrange("(ct p) d -> p ct d", p=128)).then_inc(SW, 16)
            g.dma_start(out=wvv[:, :].rearrange("p (ct d) -> p ct d", d=128),
                        in_=wvv_in.rearrange("(ct p) d -> p ct d", p=128)).then_inc(SW, 16)
            g.dma_start(out=mu[:, :], in_=mu_in).then_inc(SW, 16)
            g.dma_start(out=hmask[:, :], in_=hmask_in).then_inc(SW, 16)
            g.dma_start(out=ident[:, :], in_=ident_in).then_inc(SW, 16)
        SW_SMALL = 96
        SWB_ALL = 48

        @block.sync
        def _(sy):
            for i in range(TT):
                s = i % 8
                if i >= 8:
                    sy.wait_ge(SV, A_CNT[("apply", i - 8)])
                    sy.wait_ge(SS, a_square(i - 8))
                sy.dma_start(out=xst[:, s * C:(s + 1) * C],
                             in_=x_in[i * 128:(i + 1) * 128, :]).then_inc(DSs[s], 16)

        @block.gpsimd
        def _(g):
            for j in range(TT):
                g.wait_ge(SV, A_CNT[("apply", j)])
                g.dma_start(out=zdram[j * 128:(j + 1) * 128, :],
                            in_=zst[:, (j % 8) * C:(j % 8 + 1) * C]).then_inc(ZAs[j % 8], 16)

        @block.scalar
        def _(sc):
            for g in range(8):
                gb = (g % 2) * 4
                for s in range(4):
                    i = g * 4 + s
                    sc.wait_ge(DSs[i % 8], a_ds(i))
                    sc.wait_ge(SS, a_square(i) - 1)
                    if i >= 8:
                        sc.wait_ge(ZAs[i % 8], a_za(i - 8))
                    if g >= 2:
                        sc.wait_ge(SV, A_CNT[("svar", g - 2)])
                    sc.activation(zst[:, (i % 8) * C:(i % 8 + 1) * C],
                                  xst[:, (i % 8) * C:(i % 8 + 1) * C],
                                  AF.Square,
                                  accum_out=sumsq[:, i % 8:i % 8 + 1]).then_inc(SS, 1)
                sc.wait_ge(SV, A_CNT[("varsub", g)])
                if g >= 2:
                    sc.wait_ge(SV, A_CNT[("recip", g - 2)])
                sc.activation(sstd[:, gb:gb + 4], svar[:, gb:gb + 4], AF.Sqrt,
                              bias=cconst[:, 0:1]).then_inc(SS, 1)
                if g % 2 == 1:
                    qtr = (g - 1) // 2
                    for s in range(8):
                        sc.wait_ge(ZAs[s], 16 * (qtr + 1))
                    for ctn in range(CT):
                        sc.dma_start(out=zT[ctn][:, qtr * 1024:(qtr + 1) * 1024],
                                     in_=zdram[qtr * 1024:(qtr + 1) * 1024,
                                               ctn * 128:(ctn + 1) * 128],
                                     transpose=True).then_inc(SZQ[qtr], 16)

        def emit_ln_fin(v, g, cnt, sq_last, sq_grp):
            gb = (g % 2) * 4
            cb = slice(4 * g % 8, 4 * g % 8 + 4)
            v.wait_ge(SS, sq_last)
            v.wait_ge(SV, cnt.get(("sum", 4 * g + 3), cnt.get(("x1", 4 * g + 3), 0)))
            v.tensor_scalar(smean[:, gb:gb + 4], sums[:, cb], cconst[:, 2:3],
                            None, ALU.mult).then_inc(SV, 1)
            v.tensor_scalar(svar[:, gb:gb + 4], sumsq[:, cb], cconst[:, 2:3],
                            None, ALU.mult).then_inc(SV, 1)
            v.wait_ge(SV, cnt[("smean", g)])
            v.tensor_tensor(sstd[:, gb:gb + 4], smean[:, gb:gb + 4],
                            smean[:, gb:gb + 4], ALU.mult).then_inc(SV, 1)
            v.wait_ge(SV, cnt[("msq", g)])
            v.tensor_tensor(svar[:, gb:gb + 4], svar[:, gb:gb + 4],
                            sstd[:, gb:gb + 4], ALU.subtract).then_inc(SV, 1)
            v.wait_ge(SS, sq_grp)
            v.reciprocal(srstd[:, gb:gb + 4], sstd[:, gb:gb + 4]).then_inc(SV, 1)
            v.wait_ge(SV, cnt[("recip", g)])
            v.scalar_tensor_tensor(snmr[:, gb:gb + 4], smean[:, gb:gb + 4],
                                   cconst[:, 1:2], srstd[:, gb:gb + 4],
                                   ALU.mult, ALU.mult).then_inc(SV, 1)

        @block.vector
        def _(v):
            v.memset(cconst[:, 0:1], EPS)
            v.memset(cconst[:, 1:2], -1.0)
            v.memset(cconst[:, 2:3], RC)
            v.memset(onec[:, :], 1.0)

            def a_apply_emit(v, g):
                gb = (g % 2) * 4
                for s in range(4):
                    i = g * 4 + s
                    if i >= 8:
                        v.wait_ge(ZAs[i % 8], a_za(i - 8))
                    v.wait_ge(SV, A_CNT[("nmr", g)])
                    v.tensor_scalar(zst[:, (i % 8) * C:(i % 8 + 1) * C],
                                    xst[:, (i % 8) * C:(i % 8 + 1) * C],
                                    srstd[:, gb + s:gb + s + 1],
                                    snmr[:, gb + s:gb + s + 1],
                                    ALU.mult, ALU.add).then_inc(SV, 1)

            def a_sums_emit(v, g):
                for s in range(4):
                    i = g * 4 + s
                    v.wait_ge(DSs[i % 8], a_ds(i))
                    if g >= 2:
                        v.wait_ge(SV, A_CNT[("smean", g - 2)])
                    v.reduce_sum(sums[:, i % 8:i % 8 + 1],
                                 xst[:, (i % 8) * C:(i % 8 + 1) * C],
                                 axis=AX.X).then_inc(SV, 1)

            a_sums_emit(v, 0)
            for g in range(1, 8):
                emit_ln_fin(v, g - 1, A_CNT, a_square(4 * (g - 1) + 3),
                            a_sqrt(g - 1))
                a_apply_emit(v, g - 1)
                a_sums_emit(v, g)
            emit_ln_fin(v, 7, A_CNT, a_square(31), a_sqrt(7))
            a_apply_emit(v, 7)


        cm_qk = [nc.psum_tensor(f"qk{i}", [128, 512], F32) for i in range(4)]
        qk = opens(cm_qk)
        cm_pvv = [nc.psum_tensor(f"pvv{i}", [128, 512], F32) for i in range(2)]
        pvv = opens(cm_pvv)

        @block.tensor
        def _(t):
            t.wait_ge(SW, SW_SMALL)
            for nch in range(8):
                buf = nch % 2
                t.wait_ge(SZQ[nch // 2], 128)
                if nch >= 2:
                    t.wait_ge(SV, c_qkev(nch - 2))
                for ct in range(CT):
                    t.matmul(qk[buf][:, :], wqk[:, ct * 128:(ct + 1) * 128],
                             zT[ct][:, nch * 512:(nch + 1) * 512],
                             start=(ct == 0), stop=(ct == CT - 1),
                             skip_group_check=True)
                for ct in range(CT):
                    ins = t.matmul(qk[2 + buf][:, :], wkq[:, ct * 128:(ct + 1) * 128],
                                   zT[ct][:, nch * 512:(nch + 1) * 512],
                                   start=(ct == 0), stop=(ct == CT - 1),
                                   skip_group_check=True)
                ins.then_inc(SM, 1)
            for nch in range(8):
                t.wait_ge(SZQ[nch // 2], 128)
                if nch >= 2:
                    t.wait_ge(SV, c_vev(nch - 2))
                for ct in range(CT):
                    ins = t.matmul(pvv[nch % 2][:, :],
                                   wvv[:, ct * 128:(ct + 1) * 128],
                                   zT[ct][:, nch * 512:(nch + 1) * 512],
                                   start=(ct == 0), stop=(ct == CT - 1),
                                   skip_group_check=True)
                ins.then_inc(SM, 1)

        @block.vector
        def _(v):
            for nch in range(8):
                sl = slice(nch * 512, (nch + 1) * 512)
                buf = nch % 2
                v.wait_ge(SM, c_qkmm(nch))
                v.tensor_copy(qrep[0:64, sl], qk[buf][0:64, :]).then_inc(SV, 1)
                v.tensor_copy(krep[64:128, sl], qk[buf][64:128, :]).then_inc(SV, 1)
                v.tensor_copy(krep[0:64, sl], qk[2 + buf][0:64, :]).then_inc(SV, 1)
                v.tensor_copy(qrep[64:128, sl], qk[2 + buf][64:128, :]).then_inc(SV, 1)
            for nch in range(8):
                sl = slice(nch * 512, (nch + 1) * 512)
                v.wait_ge(SM, c_vmm(nch))
                v.tensor_copy(vT_sb[0:64, sl], pvv[nch % 2][0:64, :]).then_inc(SV, 1)

        @block.sync
        def _(sy):
            sy.wait_ge(SV, SV_C)
            sy.dma_start(out=vdram, in_=vT_sb[0:64, :]).then_inc(SVT, 16)
            sy.wait_ge(SVT, 16)
            for kt in range(KT):
                sy.wait_ge(SVT, 16 * (kt + 1))
                sy.dma_start(out=v_sb[:, kt * D:(kt + 1) * D],
                             in_=vdram[:, kt * 128:(kt + 1) * 128],
                             transpose=True).then_inc(SVT, 16)

        closes(zT_cm)

        closes(cm_pvv)
        closes(cm_qk)
        out_cm = [
            nc.sbuf_tensor("out_tok", [128, TOWN * D], F32, side="right"),
        ]
        out_tok, = opens(out_cm)
        z1T_cm = [nc.sbuf_tensor(f"z1T{i}", [128, 2080], BF16, side="right") for i in range(CT)]
        z1T = opens(z1T_cm)
        out2_cm = [
            nc.sbuf_tensor("out_sb", [128, 1024], F32, side="right"),
            nc.sbuf_tensor("outx_sb", [64, 1], F32, side="right"),
            nc.sbuf_tensor("xp_sb", [128, KT], BF16, side="right"),
        ]
        out_sb, outx_sb, xp_sb = opens(out2_cm)
        cm_p23 = [nc.sbuf_tensor("p_sb1", [128, S], BF16, side="right"),
                  nc.sbuf_tensor("p_sb2", [128, S], BF16, side="right"),
                  nc.sbuf_tensor("p_sb3", [128, S], BF16, side="right")]
        p_sb.extend(opens(cm_p23))
        cm_dp = nc.psum_tensor("dpool", [128, 4096], F32)
        dpool = cm_dp.__enter__()
        units = [dpool[:, 1024 + u * 1024:2048 + u * 1024] for u in range(3)]

        @block.tensor
        def _(t):
            t.wait_ge(SV, SV_C)
            for q in range(KT // 2):
                if q >= 2:
                    for tt_ in (2 * (q - 2), 2 * (q - 2) + 1):
                        t.wait_ge(SV, d_vsc(tt_))
                        t.wait_ge(SS, d_exp(tt_, 3))
                        for w in range(4):
                            half = w // 2
                            t.matmul(dpool[64 * half:64 * half + 64,
                                           (w % 2) * 512:(w % 2) * 512 + 512],
                                     vs_sb[:, tt_ * D:(tt_ + 1) * D],
                                     p_sb[tt_ % 4][:, w * 512:(w + 1) * 512],
                                     start=(tt_ == 0), stop=(tt_ == KT - 1),
                                     skip_group_check=True,
                                     tile_position=(0, 64 * half)).then_inc(SM, 1)
                for s in range(4):
                    for a in range(2):
                        kt = 2 * q + a
                        e = d_eidx(kt, s)
                        u = e % 3
                        if e >= 3:
                            ktp_, sp_ = d_einv(e - 3)
                            t.wait_ge(SS, d_exp(ktp_, sp_))
                            if sp_ == 0:
                                t.wait_ge(SV, d_max(ktp_))
                        for cc in range(2):
                            c = 2 * s + cc
                            t.matmul(units[u][:, cc * 512:(cc + 1) * 512],
                                     krep[64 * a:64 * (a + 1), kt * 128:(kt + 1) * 128],
                                     qrep[64 * a:64 * (a + 1), c * 512:(c + 1) * 512],
                                     start=True, stop=True, skip_group_check=True,
                                     tile_position=(64 * a, 0)).then_inc(SM, 1)
            for tt_ in range(KT - 4, KT):
                t.wait_ge(SV, d_vsc(tt_))
                t.wait_ge(SS, d_exp(tt_, 3))
                for w in range(4):
                    half = w // 2
                    t.matmul(dpool[64 * half:64 * half + 64,
                                   (w % 2) * 512:(w % 2) * 512 + 512],
                             vs_sb[:, tt_ * D:(tt_ + 1) * D],
                             p_sb[tt_ % 4][:, w * 512:(w + 1) * 512],
                             start=(tt_ == 0), stop=(tt_ == KT - 1),
                             skip_group_check=True,
                             tile_position=(0, 64 * half)).then_inc(SM, 1)

        @block.vector
        def _(v):
            def d_maxes(v, q):
                for a in range(2):
                    kt = 2 * q + a
                    u0 = d_eidx(kt, 0) % 3
                    v.wait_ge(SM, d_smm(kt, 1))
                    v.reduce_max(negm[:, kt:kt + 1],
                                 units[u0][:, 0:512:4],
                                 axis=AX.X, negate=True).then_inc(SV, 1)

            d_maxes(v, 0)
            d_maxes(v, 1)
            for q in range(KT // 2):
                for a in range(2):
                    kt = 2 * q + a
                    v.wait_ge(SS, d_exp(kt, 3))
                    v.reduce_sum(ell[:, kt:kt + 1],
                                 ellp[:, 4 * (kt % 4):4 * (kt % 4) + 4],
                                 axis=AX.X).then_inc(SV, 1)
                    v.wait_ge(SV, d_ellsum(kt))
                    v.reciprocal(rell[:, kt:kt + 1], ell[:, kt:kt + 1]).then_inc(SV, 1)
                    v.wait_ge(SV, d_recip(kt))
                    v.wait_ge(SVT, 16 * (kt + 2))
                    v.tensor_scalar(vs_sb[:, kt * D:(kt + 1) * D],
                                    v_sb[:, kt * D:(kt + 1) * D],
                                    rell[:, kt:kt + 1], None, ALU.mult).then_inc(SV, 1)
                    v.tensor_copy(xp_sb[:, kt:kt + 1],
                                  p_sb[kt % 4][:, 4095:4096]).then_inc(SV, 1)
                if q + 2 < KT // 2:
                    d_maxes(v, q + 2)

        @block.scalar
        def _(sc):
            for q in range(KT // 2):
                for s in range(4):
                    for a in range(2):
                        kt = 2 * q + a
                        u = d_eidx(kt, s) % 3
                        if s == 0:
                            if kt == 0:
                                sc.wait_ge(SVT, 16)
                            sc.wait_ge(SV, d_max(kt))
                            if kt >= 4:
                                sc.wait_ge(SV, d_ellsum(kt - 4))
                                sc.wait_ge(SM, d_outmm(kt - 4))
                        sc.wait_ge(SM, d_smm(kt, 2 * s + 1))
                        sc.activation(p_sb[kt % 4][:, s * 1024:(s + 1) * 1024],
                                      units[u][:, :], AF.Exp,
                                      bias=negm[:, kt:kt + 1],
                                      accum_out=ellp[:, 4 * (kt % 4) + s:
                                                     4 * (kt % 4) + s + 1]).then_inc(SS, 1)

        @block.gpsimd
        def _(g):
            g.wait_ge(SS, d_exp(7, 3))
            g.dma_start(out=wrT[:, :].rearrange("p (ct d) -> p ct d", d=C),
                        in_=wrT_in.rearrange("(ct p) d -> p ct d", p=128)).then_inc(SWB, 16)
            g.dma_start(out=wkT[:, :].rearrange("p (ct d) -> p ct d", d=C),
                        in_=wkT_in.rearrange("(ct p) d -> p ct d", p=128)).then_inc(SWB, 16)
            g.dma_start(out=wvT[:, :].rearrange("p (ct d) -> p ct d", d=C),
                        in_=wvT_in.rearrange("(ct p) d -> p ct d", p=128)).then_inc(SWB, 16)

        @block.vector
        def _(v):
            v.wait_ge(SM, d_outmm(KT - 1))
            v.tensor_copy(out_sb[0:64, 0:1024], dpool[0:64, 0:1024]).then_inc(SV, 1)
            v.tensor_copy(out_sb[64:128, 0:1024], dpool[64:128, 0:1024]).then_inc(SV, 1)

        if debug:
            @block.gpsimd
            def _(g):
                g.wait_ge(SV, e_evac)
                g.dma_start(out=d_ell[:, 0:KT], in_=ell[:, :]).then_inc(SW, 16)
                g.dma_start(out=d_ell[:, KT:2 * KT], in_=negm[:, :]).then_inc(SW, 16)
                g.dma_start(out=d_out[:, 0:1024], in_=out_sb[0:64, 0:1024]).then_inc(SW, 16)
                g.wait_ge(SW, SW_SMALL + 112)

        cm_dp.__exit__(None, None, None)
        cm_tp = [nc.psum_tensor(f"tp{i}", [128, 64], F32) for i in range(2)]
        tp_ps = opens(cm_tp)
        cm_oxe = nc.psum_tensor("oxe", [64, 64], F32)
        ox_e = cm_oxe.__enter__()

        @block.tensor
        def _(t):
            t.wait_ge(SV, e_evac)
            t.wait_ge(SS, SS_D)
            for b in range(16):
                if b >= 2:
                    t.wait_ge(SV, e_tpev(b - 2))
                hb = b // 8
                t.transpose(tp_ps[b % 2][:, :],
                            out_sb[64 * hb:64 * hb + 64,
                                   (b % 8) * 128:(b % 8 + 1) * 128],
                            ident[64 * hb:64 * hb + 64,
                                  64 * hb:64 * hb + 64]).then_inc(SM, 1)
            for kt in range(KT):
                t.matmul(ox_e[:, 0:1],
                         vs_sb[:, kt * D:(kt + 1) * D],
                         xp_sb[:, kt:kt + 1],
                         start=(kt == 0), stop=(kt == KT - 1),
                         skip_group_check=True).then_inc(SM, 1)

        @block.vector
        def _(v):
            for b in range(16):
                v.wait_ge(SM, e_tpmm(b))
                v.tensor_copy(out_tok[:, b * D:(b + 1) * D],
                              tp_ps[b % 2][:, :]).then_inc(SV, 1)
            v.wait_ge(SM, e_oxmm)
            v.tensor_copy(outx_sb[:, :], ox_e[:, 0:1]).then_inc(SV, 1)

        @block.sync
        def _(sy):
            sy.wait_ge(SV, e_oxev)
            sy.wait_ge(SD, SD_E0)
            sy.dma_start(out=xrow_d.rearrange("a (p o) -> p (a o)", p=64),
                         in_=outx_sb[0:64, 0:1]).then_inc(SD, 16)
            sy.wait_ge(SD, sd_xrowd)
            sy.dma_start(out=outx_row[:, :], in_=xrow_d).then_inc(SD, 16)
            sy.wait_ge(SD, sd_outxrow)
            sy.dma_start(out=xlast, in_=x_in[4095:4096, :]).then_inc(SD, 16)

        if debug:
            @block.gpsimd
            def _(g):
                g.wait_ge(SV, SV_E)
                g.dma_start(out=d_tok[:, 0:TOWN * D], in_=out_tok[:, :]).then_inc(SW, 16)
                g.dma_start(out=d_tok[:, TOWN * D:], in_=out_tok[:, 0:D]).then_inc(SW, 16)
                g.wait_ge(SW, SW_SMALL + 144)

        cm_oxe.__exit__(None, None, None)
        closes(cm_tp)
        closes(cm_p23)
        closes(attn_cm)
        x1f_cm = [nc.sbuf_tensor("x1b", [128, 8 * C], F32),
                  nc.sbuf_tensor("z1b", [128, 8 * C], BF16)]
        x1b, z1b = opens(x1f_cm)

        @block.gpsimd
        def _(g_):
            for t_ in range(TOWN):
                if t_ >= 8:
                    g_.wait_ge(SV, F_CNT[("apply", t_ - 8)])
                    g_.wait_ge(SS, f_square(t_ - 8))
                else:
                    g_.wait_ge(SV, A_CNT[("apply", 24 + t_)])
                    g_.wait_ge(SS, a_square(24 + t_))
                g_.dma_start(out=xst[:, (t_ % 8) * C:(t_ % 8 + 1) * C],
                             in_=x_in[t_ * 128:(t_ + 1) * 128, :]).then_inc(FSs[t_ % 8], 16)

        @block.sync
        def _(sy):
            sy.wait_ge(SV, SV_F0 + 10)
            sy.wait_ge(SD, sd_row63 - 16)
            sy.dma_start(out=z1dram[15:16, :], in_=z1xm).then_inc(SD, 16)
            sy.wait_ge(SV, SV_F0 + 11)
            sy.wait_ge(SD, sd_zero1 - 16)
            sy.dma_start(out=z1dram[0:15, :],
                         in_=bass.AP(zst, 0, [[8 * C, 15], [1, C]])).then_inc(SD, 16)
            sy.wait_ge(SD, sd_zero2 - 16)
            sy.dma_start(out=z1dram[2064:2080, :],
                         in_=bass.AP(zst, 0, [[8 * C, 16], [1, C]])).then_inc(SD, 16)
            for g in range(4):
                for s in range(4):
                    t_ = g * 4 + s
                    sy.wait_ge(SV, F_CNT[("apply", t_)])
                    sy.dma_start(out=z1dram[16 + t_ * 128:16 + (t_ + 1) * 128, :],
                                 in_=z1b[:, (t_ % 8) * C:(t_ % 8 + 1) * C]).then_inc(ZSs[t_ % 8], 16)

        @block.vector
        def _(v):
            def f_x1_sum(v, t_):
                s8 = t_ % 8
                v.wait_ge(FSs[s8], fs_f(t_))
                if t_ >= 8:
                    v.wait_ge(SS, f_square(t_ - 8))
                v.tensor_tensor(
                    x1b[:, s8 * C:(s8 + 1) * C],
                    xst[:, s8 * C:(s8 + 1) * C],
                    bass.AP(out_tok, t_ * D, [[TOWN * D, 128], [0, 16], [1, D]]),
                    ALU.add).then_inc(SV, 1)

            def f_apply_emit(v, g):
                gb = (g % 2) * 4
                for s in range(4):
                    t_ = g * 4 + s
                    if t_ >= 8:
                        v.wait_ge(ZSs[t_ % 8], zs_f(t_ - 8))
                    v.wait_ge(SV, F_CNT[("nmr", g)])
                    v.tensor_scalar(z1b[:, (t_ % 8) * C:(t_ % 8 + 1) * C],
                                    x1b[:, (t_ % 8) * C:(t_ % 8 + 1) * C],
                                    srstd[:, gb + s:gb + s + 1],
                                    snmr[:, gb + s:gb + s + 1],
                                    ALU.mult, ALU.add).then_inc(SV, 1)

            v.wait_ge(SD, sd_xlast)
            v.tensor_tensor(x1x, xlast,
                            bass.AP(outx_row, 0, [[D, 1], [0, 16], [1, D]]),
                            ALU.add).then_inc(SV, 1)
            v.wait_ge(SV, SV_F0 + 1)
            v.reduce_sum(rs[:, 0:1], x1x, axis=AX.X).then_inc(SV, 1)
            v.wait_ge(SS, SS_F0 + 1)
            v.wait_ge(SV, SV_F0 + 2)
            v.tensor_scalar(rs[:, 2:3], rs[:, 0:1], cconst[0:1, 2:3], None, ALU.mult).then_inc(SV, 1)
            v.tensor_scalar(rs[:, 3:4], rs[:, 1:2], cconst[0:1, 2:3], None, ALU.mult).then_inc(SV, 1)
            v.wait_ge(SV, SV_F0 + 3)
            v.tensor_tensor(rs[:, 4:5], rs[:, 2:3], rs[:, 2:3], ALU.mult).then_inc(SV, 1)
            v.wait_ge(SV, SV_F0 + 5)
            v.tensor_tensor(rs[:, 5:6], rs[:, 3:4], rs[:, 4:5], ALU.subtract).then_inc(SV, 1)
            v.wait_ge(SS, SS_TAIL)
            v.reciprocal(rs[:, 7:8], rs[:, 6:7]).then_inc(SV, 1)
            v.wait_ge(SV, SV_F0 + 7)
            v.scalar_tensor_tensor(rs[:, 8:9], rs[:, 2:3], cconst[0:1, 1:2], rs[:, 7:8],
                                   ALU.mult, ALU.mult).then_inc(SV, 1)
            v.wait_ge(SV, SV_F0 + 8)
            v.tensor_scalar(z1x, x1x, rs[:, 7:8], rs[:, 8:9],
                            ALU.mult, ALU.add).then_inc(SV, 1)
            v.wait_ge(SW, SW_SMALL)
            v.wait_ge(SV, SV_F0 + 9)
            v.tensor_scalar(z1xm, z1x, hmask[0:1, 0:1], None,
                            ALU.mult).then_inc(SV, 1)
            v.memset(zst[:, 0:C], 0.0).then_inc(SV, 1)
            for s in range(4):
                f_x1_sum(v, s)
            for g in range(1, 4):
                emit_ln_fin(v, g - 1, F_CNT, f_square(4 * (g - 1) + 3),
                            f_sqrt(g - 1))
                f_apply_emit(v, g - 1)
                for s in range(4):
                    f_x1_sum(v, g * 4 + s)
            emit_ln_fin(v, 3, F_CNT, f_square(15), f_sqrt(3))
            f_apply_emit(v, 3)

        @block.scalar
        def _(sc):
            sc.wait_ge(SV, SV_F0 + 1)
            sc.activation(jrow, x1x, AF.Square,
                          accum_out=rs[:, 1:2]).then_inc(SS, 1)
            sc.wait_ge(SV, SV_F0 + 6)
            sc.activation(rs[:, 6:7], rs[:, 5:6], AF.Sqrt, bias=cconst[0:1, 0:1]).then_inc(SS, 1)
            for g in range(4):
                gb = (g % 2) * 4
                for s in range(4):
                    t_ = g * 4 + s
                    sc.wait_ge(SV, F_CNT[("x1", t_)])
                    sc.wait_ge(SS, f_sumss(t_) - 1)
                    if g >= 2:
                        sc.wait_ge(SV, F_CNT[("svar", g - 2)])
                        sc.wait_ge(SV, F_CNT[("smean", g - 2)])
                    sc.activation(zst[:, 2 * C:3 * C], x1b[:, (t_ % 8) * C:(t_ % 8 + 1) * C],
                                  AF.Copy,
                                  accum_out=sums[:, t_ % 8:t_ % 8 + 1]).then_inc(SS, 1)
                    sc.wait_ge(SS, f_sumss(t_))
                    sc.activation(zst[:, 2 * C:3 * C], x1b[:, (t_ % 8) * C:(t_ % 8 + 1) * C],
                                  AF.Square,
                                  accum_out=sumsq[:, t_ % 8:t_ % 8 + 1]).then_inc(SS, 1)
                sc.wait_ge(SV, F_CNT[("varsub", g)])
                if g >= 2:
                    sc.wait_ge(SV, F_CNT[("recip", g - 2)])
                sc.activation(sstd[:, gb:gb + 4], svar[:, gb:gb + 4], AF.Sqrt,
                              bias=cconst[:, 0:1]).then_inc(SS, 1)

        @block.scalar
        def _(sc):
            sc.wait_ge(SD, sd_zero1)
            for s in range(8):
                sc.wait_ge(ZSs[s], 16)
            for ctn in range(CT):
                sc.dma_start(out=z1T[ctn][:, 0:1040],
                             in_=z1dram[0:1040, ctn * 128:(ctn + 1) * 128],
                             transpose=True).then_inc(SG_LO, 16)
        @block.sync
        def _(sy):
            sy.wait_ge(SD, sd_zero2)
            for s in range(8):
                sy.wait_ge(ZSs[s], 32)
            sy.wait_ge(SG_LO, 128)
            for ctn in range(CT):
                sy.wait_ge(SG_HI, 16 * ctn)
                sy.dma_start(out=z1T[ctn][:, 1040:2080],
                             in_=z1dram[1040:2080, ctn * 128:(ctn + 1) * 128],
                             transpose=True).then_inc(SG_HI, 16)

        if debug:
            @block.gpsimd
            def _(g):
                g.wait_ge(SG_HI, 128)
                g.dma_start(out=d_z1, in_=z1dram).then_inc(SW, 16)
                g.wait_ge(SW, SW_SMALL + 160)

        closes(x1f_cm)
        closes(out2_cm)
        xk_cm = [nc.sbuf_tensor(f"xk{i}", [128, 2048], BF16) for i in range(CT)]
        xk = opens(xk_cm)
        xr_cm = [nc.sbuf_tensor(f"xr{i}", [128, 2048], BF16) for i in range(CT)]
        xr = opens(xr_cm)
        dbuf = zst[:, 0:2048]

        @block.scalar
        def _(sc):
            sc.wait_ge(SV, F_CNT[("apply", 15)])
            for ct in range(CT):
                sc.wait_ge(SG_HI, szt_g_hi(ct))
                sc.activation(xr[ct][:, :], z1T[ct][:, 16:2064], AF.Identity,
                              bias=0.0, scale=mu[:, ct:ct + 1]).then_inc(SS, 1)
            for ct in range(CT):
                sc.activation(xk[ct][:, :], z1T[ct][:, 16:2064], AF.Identity,
                              bias=0.0, scale=mu[:, 8 + ct:9 + ct]).then_inc(SS, 1)

        @block.vector
        def _(v):
            v.wait_ge(SS, f_square(15))
            v.scalar_tensor_tensor(mu1m[:, :], mu[:, :], -1.0, onec[:, :],
                                   ALU.mult, ALU.add).then_inc(SV, 1)
            v.wait_ge(SV, SV_H0)
            for ct in range(CT):
                v.wait_ge(SS, h_tsr(ct))
                v.wait_ge(SV, SV_H0 + ct)
                v.scalar_tensor_tensor(xr[ct][:, :], z1T[ct][:, 15:2063],
                                       mu1m[:, ct:ct + 1],
                                       xr[ct][:, :], ALU.mult,
                                       ALU.add).then_inc(SV, 1)
            for ct in range(CT):
                v.wait_ge(SS, h_tsk(ct))
                v.wait_ge(SV, SV_H0 + 8 + ct)
                v.scalar_tensor_tensor(xk[ct][:, :], z1T[ct][:, 15:2063],
                                       mu1m[:, 8 + ct:9 + ct],
                                       xk[ct][:, :], ALU.mult,
                                       ALU.add).then_inc(SV, 1)

        closes(z1T_cm)
        r_sb_cm = [nc.sbuf_tensor("r_sb", [128, TOWN * C], BF16, side="right")]
        r_sb, = opens(r_sb_cm)
        cm_rps = [nc.psum_tensor(f"rps{i}", [128, C], F32) for i in range(3)]
        r_ps = opens(cm_rps)

        @block.tensor
        def _(t):
            t.wait_ge(SV, h_xr_all)
            t.wait_ge(SS, SS_F)
            t.wait_ge(SWB, SWB_ALL)
            for t_ in range(TOWN):
                if t_ >= 3:
                    t.wait_ge(SS, i_sig(t_ - 3))
                for ct in range(CT):
                    for hh in range(2):
                        ins = t.matmul(
                            r_ps[t_ % 3][:, hh * 512:(hh + 1) * 512],
                            xr[ct][:, t_ * 128:(t_ + 1) * 128],
                            wrT[:, ct * C + hh * 512:ct * C + (hh + 1) * 512],
                            start=(ct == 0), stop=(ct == CT - 1),
                            skip_group_check=True)
                ins.then_inc(SM, 1)

        @block.scalar
        def _(sc):
            sc.wait_ge(SV, SV_H)
            for t_ in range(TOWN):
                sc.wait_ge(SM, i_mm(t_))
                sc.activation(r_sb[:, t_ * C:(t_ + 1) * C], r_ps[t_ % 3][:, :],
                              AF.Sigmoid).then_inc(SS, 1)

        closes(cm_rps)
        closes(xr_cm)

        if debug:
            @block.gpsimd
            def _(g):
                g.wait_ge(SS, SS_I)
                g.dma_start(out=d_r, in_=r_sb[:, :]).then_inc(SW, 16)
                g.wait_ge(SW, SW_SMALL + 192)

        kaT_cm = [nc.sbuf_tensor(f"kaT{i}", [128, 2048], BF16, side="right") for i in range(CT)]
        kaT = opens(kaT_cm)
        cm_ktp = [nc.psum_tensor(f"ktp{i}", [128, 512], F32) for i in range(3)]
        kt_ps = opens(cm_ktp)

        @block.tensor
        def _(t):
            t.wait_ge(SS, SS_I)
            for dt in range(CT):
                for tc in range(4):
                    ix = dt * 4 + tc
                    if ix >= 3:
                        t.wait_ge(SS, j_relu(ix - 3))
                    for ct in range(CT):
                        ins = t.matmul(
                            kt_ps[ix % 3][:, :],
                            wkT[:, ct * C + dt * 128:ct * C + (dt + 1) * 128],
                            xk[ct][:, tc * 512:(tc + 1) * 512],
                            start=(ct == 0), stop=(ct == CT - 1),
                            skip_group_check=True)
                    ins.then_inc(SM, 1)

        @block.scalar
        def _(sc):
            for dt in range(CT):
                for tc in range(4):
                    ix = dt * 4 + tc
                    sc.wait_ge(SM, j_mm(ix))
                    if ix >= 4:
                        sc.wait_ge(SV, j_sq(ix - 4))
                    sc.activation(kaT[dt][:, tc * 512:(tc + 1) * 512],
                                  kt_ps[ix % 3][:, :], AF.Relu).then_inc(SS, 1)

        @block.vector
        def _(v):
            for dt in range(CT):
                for tc in range(4):
                    ix = dt * 4 + tc
                    v.wait_ge(SS, j_relu(ix))
                    v.tensor_tensor(kaT[dt][:, tc * 512:(tc + 1) * 512],
                                    kaT[dt][:, tc * 512:(tc + 1) * 512],
                                    kaT[dt][:, tc * 512:(tc + 1) * 512],
                                    ALU.mult).then_inc(SV, 1)

        closes(cm_ktp)
        closes(xk_cm)

        if debug:
            @block.gpsimd
            def _(g):
                g.wait_ge(SV, SV_J)
                for dt in range(CT):
                    g.dma_start(out=d_ka[:, dt * 2048:(dt + 1) * 2048],
                                in_=kaT[dt][:, :]).then_inc(SW, 16)
                g.wait_ge(SW, SW_SMALL + 192 + 128)

        k32_cm = [nc.sbuf_tensor("tmp32", [128, 2 * C], F32, side="right"),
                  nc.sbuf_tensor("x2b", [128, 2 * C], F32, side="right")]
        tmp32, x2b = opens(k32_cm)
        cm_ops = [nc.psum_tensor(f"ops{i}", [128, C], F32) for i in range(3)]
        o_ps = opens(cm_ops)

        @block.tensor
        def _(t):
            t.wait_ge(SV, SV_J)
            t.wait_ge(SS, SS_J)
            for t_ in range(TOWN):
                if t_ >= 3:
                    t.wait_ge(SV, k_tmp(t_ - 3))
                for dt in range(CT):
                    for hh in range(2):
                        ins = t.matmul(
                            o_ps[t_ % 3][:, hh * 512:(hh + 1) * 512],
                            kaT[dt][:, t_ * 128:(t_ + 1) * 128],
                            wvT[:, dt * C + hh * 512:dt * C + (hh + 1) * 512],
                            start=(dt == 0), stop=(dt == CT - 1),
                            skip_group_check=True)
                ins.then_inc(SM, 1)

        @block.gpsimd
        def _(g):
            def k_xadd(g, t_):
                g.wait_ge(FSs[t_ % 8], fs_k(t_))
                g.tensor_tensor(
                    xst[:, (t_ % 8) * C:(t_ % 8 + 1) * C],
                    xst[:, (t_ % 8) * C:(t_ % 8 + 1) * C],
                    bass.AP(out_tok, t_ * D, [[TOWN * D, 128], [0, 16], [1, D]]),
                    ALU.add).then_inc(SH, 1)

            for t_ in range(TOWN):
                if t_ >= 4:
                    g.wait_ge(SV, k_x2(t_ - 4))
                else:
                    g.wait_ge(SV, SV_J)
                if t_ >= 8:
                    g.wait_ge(FSs[t_ % 8], fs_k(t_ - 8))
                g.dma_start(out=xst[:, (t_ % 8) * C:(t_ % 8 + 1) * C],
                            in_=x_in[t_ * 128:(t_ + 1) * 128, :]).then_inc(FSs[t_ % 8], 16)
                if t_ >= 1:
                    k_xadd(g, t_ - 1)
            k_xadd(g, TOWN - 1)

        def k_ydma_emit(sy, t_):
            sy.wait_ge(SV, k_x2(t_))
            sy.dma_start(out=y_out[t_ * 128:(t_ + 1) * 128, :],
                         in_=x2b[:, (t_ % 2) * C:(t_ % 2 + 1) * C]).then_inc(YSs[t_ % 2], 16)

        @block.sync
        def _(sy):
            for t_ in range(TOWN):
                if t_ >= 2:
                    k_ydma_emit(sy, t_ - 2)
            k_ydma_emit(sy, TOWN - 2)
            k_ydma_emit(sy, TOWN - 1)
            sy.wait_ge(YSs[0], 16 * (TOWN // 2))
            sy.wait_ge(YSs[1], 16 * (TOWN // 2))

        @block.vector
        def _(v):
            for t_ in range(TOWN):
                s = t_ % 4
                v.wait_ge(SM, k_mm(t_))
                v.wait_ge(SS, i_sig(t_))
                if t_ >= 2:
                    v.wait_ge(SV, k_x2(t_ - 2))
                    v.wait_ge(YSs[t_ % 2], ys_k(t_ - 2))
                v.tensor_tensor(tmp32[:, (t_ % 2) * C:(t_ % 2 + 1) * C],
                                o_ps[t_ % 3][:, :],
                                r_sb[:, t_ * C:(t_ + 1) * C], ALU.mult).then_inc(SV, 1)
                v.wait_ge(SH, t_ + 1)
                v.wait_ge(SV, k_tmp(t_))
                v.tensor_tensor(x2b[:, (t_ % 2) * C:(t_ % 2 + 1) * C],
                                tmp32[:, (t_ % 2) * C:(t_ % 2 + 1) * C],
                                xst[:, (t_ % 8) * C:(t_ % 8 + 1) * C], ALU.add).then_inc(SV, 1)


        closes(cm_ops)
        closes(k32_cm)
        closes(kaT_cm)
        closes(r_sb_cm)
        closes(out_cm)

    return nc


def kernel(L0_attn_x, L0_attn_Wq, L0_attn_Wk, L0_attn_Wv,
           L0_ffn_W_r, L0_ffn_W_k, L0_ffn_W_v, L0_ffn_mu_r, L0_ffn_mu_k):
    x = np.ascontiguousarray(np.asarray(L0_attn_x, dtype=np.float32))
    Wq = np.asarray(L0_attn_Wq, dtype=np.float32)
    Wk = np.asarray(L0_attn_Wk, dtype=np.float32)
    Wv = np.asarray(L0_attn_Wv, dtype=np.float32)
    Wr = np.asarray(L0_ffn_W_r, dtype=np.float32)
    Wkf = np.asarray(L0_ffn_W_k, dtype=np.float32)
    Wvf = np.asarray(L0_ffn_W_v, dtype=np.float32)
    mur = np.asarray(L0_ffn_mu_r, dtype=np.float32)
    muk = np.asarray(L0_ffn_mu_k, dtype=np.float32)

    wqk = np.ascontiguousarray(np.concatenate([Wq, Wk], axis=1))
    wkq = np.ascontiguousarray(np.concatenate([Wk, Wq], axis=1))
    wvv = np.ascontiguousarray(np.concatenate([Wv, Wv], axis=1))
    wrT = np.ascontiguousarray(Wr.T)
    wkT = np.ascontiguousarray(Wkf.T)
    wvT = np.ascontiguousarray(Wvf.T)
    mu_sb = np.ascontiguousarray(
        np.concatenate([mur.reshape(8, 128).T, muk.reshape(8, 128).T], axis=1))
    ident = np.ascontiguousarray(np.eye(128, dtype=np.float32))

    if "nc" not in _PROGRAM_CACHE:
        _PROGRAM_CACHE["nc"] = build_program()
    nc = _PROGRAM_CACHE["nc"]

    in_maps = []
    for c in range(8):
        b, h = c // 2, c % 2
        own = x[b, h * 2048:(h + 1) * 2048]
        other = x[b, (1 - h) * 2048:(2 - h) * 2048]
        xp = np.ascontiguousarray(np.concatenate([own, other], axis=0))
        in_maps.append({
            "x_in": xp,
            "wqk_in": wqk, "wkq_in": wkq, "wvv_in": wvv,
            "wrT_in": wrT, "wkT_in": wkT, "wvT_in": wvT,
            "mu_in": mu_sb,
            "hmask_in": np.array([[float(h)]], dtype=np.float32),
            "ident_in": ident,
        })

    res = run_bass_kernel_spmd(nc, in_maps, core_ids=list(range(8)))

    y = np.empty((4, S, C), dtype=np.float32)
    for c in range(8):
        b, h = c // 2, c % 2
        y[b, h * 2048:(h + 1) * 2048] = np.asarray(res.results[c]["y_out"])
    return y


# version 26
# speedup vs baseline: 1.1693x; 1.0175x over previous
import numpy as np
from contextlib import ExitStack

import concourse.bass as bass
import concourse.mybir as mybir
from concourse.bass_utils import run_bass_kernel_spmd

F32 = mybir.dt.float32
BF16 = mybir.dt.bfloat16
AF = mybir.ActivationFunctionType
ALU = mybir.AluOpType
AX = mybir.AxisListType

S = 4096
C = 1024
D = 64
TOWN = 16
TT = 32
CT = 8
KT = 32
EPS = 1e-5
RC = 1.0 / C

_PROGRAM_CACHE = {}


def build_program(debug=False):
    nc = bass.Bass("TRN2", target_bir_lowering=False)

    x_in = nc.dram_tensor("x_in", [S, C], F32, kind="ExternalInput").ap()
    wqk_in = nc.dram_tensor("wqk_in", [C, 128], F32, kind="ExternalInput").ap()
    wkq_in = nc.dram_tensor("wkq_in", [C, 128], F32, kind="ExternalInput").ap()
    wvv_in = nc.dram_tensor("wvv_in", [C, 128], F32, kind="ExternalInput").ap()
    wrT_in = nc.dram_tensor("wrT_in", [C, C], F32, kind="ExternalInput").ap()
    wkT_in = nc.dram_tensor("wkT_in", [C, C], F32, kind="ExternalInput").ap()
    wvT_in = nc.dram_tensor("wvT_in", [C, C], F32, kind="ExternalInput").ap()
    mu_in = nc.dram_tensor("mu_in", [128, 16], F32, kind="ExternalInput").ap()
    hmask_in = nc.dram_tensor("hmask_in", [1, 1], F32, kind="ExternalInput").ap()
    ident_in = nc.dram_tensor("ident_in", [128, 128], F32, kind="ExternalInput").ap()
    y_out = nc.dram_tensor("y_out", [2048, C], F32, kind="ExternalOutput").ap()

    if debug:
        d_z = nc.dram_tensor("d_z", [S, C], F32, kind="ExternalOutput").ap()
        d_qk = nc.dram_tensor("d_qk", [256, S], F32, kind="ExternalOutput").ap()
        d_v = nc.dram_tensor("d_v", [128, KT * D], F32, kind="ExternalOutput").ap()
        d_out = nc.dram_tensor("d_out", [64, 2176], F32, kind="ExternalOutput").ap()
        d_ell = nc.dram_tensor("d_ell", [128, 2 * KT], F32, kind="ExternalOutput").ap()
        d_tok = nc.dram_tensor("d_tok", [128, TOWN * D + D], F32, kind="ExternalOutput").ap()
        d_z1 = nc.dram_tensor("d_z1", [2080, C], F32, kind="ExternalOutput").ap()
        d_x1 = nc.dram_tensor("d_x1", [2048, C], F32, kind="ExternalOutput").ap()
        d_r = nc.dram_tensor("d_r", [128, TOWN * C], F32, kind="ExternalOutput").ap()
        d_ka = nc.dram_tensor("d_ka", [128, CT * 2048], F32, kind="ExternalOutput").ap()
    zdram = nc.dram_tensor("zdram", [S, C], BF16).ap()
    z1dram = nc.dram_tensor("z1dram", [2080, C], BF16).ap()
    x1dram = nc.dram_tensor("x1dram", [2048, C], F32).ap()
    xrow_d = nc.dram_tensor("xrow_d", [1, D], F32).ap()
    vdram = nc.dram_tensor("vdram", [64, S], BF16).ap()

    ctx = ExitStack()
    e = ctx.enter_context
    with ctx:
        wqk = e(nc.sbuf_tensor("wqk", [128, 8 * 128], BF16))
        wkq = e(nc.sbuf_tensor("wkq", [128, 8 * 128], BF16))
        wvv = e(nc.sbuf_tensor("wvv", [128, 8 * 128], BF16))
        wrT = e(nc.sbuf_tensor("wrT", [128, 8 * C], BF16))
        wkT = e(nc.sbuf_tensor("wkT", [128, 8 * C], BF16))
        wvT = e(nc.sbuf_tensor("wvT", [128, 8 * C], BF16))
        mu = e(nc.sbuf_tensor("mu", [128, 16], F32))
        mu1m = e(nc.sbuf_tensor("mu1m", [128, 16], F32))
        onec = e(nc.sbuf_tensor("onec", [128, 16], F32))
        hmask = e(nc.sbuf_tensor("hmask", [1, 1], F32))
        ident = e(nc.sbuf_tensor("ident", [128, 128], F32))

        cconst = e(nc.sbuf_tensor("cconst", [128, 4], F32))
        xst = e(nc.sbuf_tensor("xst", [128, 8 * C], F32))
        zst = e(nc.sbuf_tensor("zst", [128, 8 * C], BF16))
        sums = e(nc.sbuf_tensor("sums", [128, 8], F32))
        sumsq = e(nc.sbuf_tensor("sumsq", [128, 8], F32))
        smean = e(nc.sbuf_tensor("smean", [128, 8], F32))
        svar = e(nc.sbuf_tensor("svar", [128, 8], F32))
        sstd = e(nc.sbuf_tensor("sstd", [128, 8], F32))
        srstd = e(nc.sbuf_tensor("srstd", [128, 8], F32))
        snmr = e(nc.sbuf_tensor("snmr", [128, 8], F32))

        rowf = e(nc.sbuf_tensor("rowf", [128, C], F32))
        xlast = rowf[0:1, :]
        x1x = rowf[32:33, :]
        z1x = zst[0:1, C:2 * C]
        z1xm = zst[32:33, C:2 * C]
        jrow = zst[64:65, C:2 * C]
        outx_row = e(nc.sbuf_tensor("outx_row", [1, D], F32))
        rs = e(nc.sbuf_tensor("rs", [1, 12], F32))

        def opens(cms):
            return [cm.__enter__() for cm in cms]

        def closes(cms):
            for cm in reversed(cms):
                cm.__exit__(None, None, None)

        zT_cm = [nc.sbuf_tensor(f"zT{i}", [128, S], BF16, side="right") for i in range(CT)]
        zT = opens(zT_cm)
        attn_cm = [
            nc.sbuf_tensor("qrep", [128, S], BF16),
            nc.sbuf_tensor("krep", [128, S], BF16),
            nc.sbuf_tensor("v_sb", [128, KT * D], BF16),
            nc.sbuf_tensor("p_sb0", [128, S], BF16),
            nc.sbuf_tensor("negm", [128, KT], F32),
            nc.sbuf_tensor("ellp", [128, 16], F32),
            nc.sbuf_tensor("ell", [128, KT], F32),
            nc.sbuf_tensor("rell", [128, KT], F32),
        ]
        (qrep, krep, v_sb, p_sb0, negm, ellp, ell, rell) = opens(attn_cm)
        vs_sb = v_sb
        vT_sb = p_sb0
        p_sb = [p_sb0]

        SW = e(nc.semaphore("SW"))
        SX = e(nc.semaphore("SX"))
        SD = e(nc.semaphore("SD"))
        DSs = [e(nc.semaphore(f"DS{i}")) for i in range(8)]
        ZSs = [e(nc.semaphore(f"ZS{i}")) for i in range(8)]
        XSs = [e(nc.semaphore(f"XS{i}")) for i in range(8)]
        YSs = [e(nc.semaphore(f"YS{i}")) for i in range(2)]
        SVT = e(nc.semaphore("SVT"))
        SH = e(nc.semaphore("SH"))
        SWB = e(nc.semaphore("SWB"))
        FSs = [e(nc.semaphore(f"FS{i}")) for i in range(8)]
        ZAs = [e(nc.semaphore(f"ZA{i}")) for i in range(8)]
        SZQ = [e(nc.semaphore(f"SZQ{i}")) for i in range(4)]
        SG_LO = e(nc.semaphore("SG_LO"))
        SG_HI = e(nc.semaphore("SG_HI"))
        SM = e(nc.semaphore("SM"))
        SV = e(nc.semaphore("SV"))
        SS = e(nc.semaphore("SS"))
        SY = e(nc.semaphore("SY"))

        block = e(nc.Block())

        fs_f = lambda t: 16 * (t // 8 + 1)
        fs_k = lambda t: 32 + 16 * (t // 8 + 1)
        zs_f = lambda t: 16 * (t // 8 + 1)
        xs_f = lambda t: 16 * (t // 8 + 1)
        ys_k = lambda t: 16 * (t // 2 + 1)
        def plan_ln(n_tiles, sv0, extra_per_tile):
            ng = n_tiles // 4
            cnt = {}
            sv = sv0
            for g in range(ng):
                for s4 in range(4):
                    i = 4 * g + s4
                    for op in extra_per_tile:
                        sv += 1
                        cnt[(op, i)] = sv
                    sv += 1
                    cnt[("sum", i)] = sv
                if g >= 1:
                    for op in ("smean", "svar", "msq", "varsub", "recip", "nmr"):
                        sv += 1
                        cnt[(op, g - 1)] = sv
                    for s4 in range(4):
                        sv += 1
                        cnt[("apply", 4 * (g - 1) + s4)] = sv
            for op in ("smean", "svar", "msq", "varsub", "recip", "nmr"):
                sv += 1
                cnt[(op, ng - 1)] = sv
            for s4 in range(4):
                sv += 1
                cnt[("apply", 4 * (ng - 1) + s4)] = sv
            return cnt, sv

        def plan_ln_af(n_tiles, sv0):
            ng = n_tiles // 4
            cnt = {}
            sv = sv0
            for s4 in range(4):
                sv += 1
                cnt[("sum", s4)] = sv
            for g in range(1, ng + 1):
                gm = g - 1
                for op in ("smean", "svar", "msq", "varsub", "recip", "nmr"):
                    sv += 1
                    cnt[(op, gm)] = sv
                for s4 in range(4):
                    sv += 1
                    cnt[("apply", 4 * gm + s4)] = sv
                if g < ng:
                    for s4 in range(4):
                        sv += 1
                        cnt[("sum", 4 * g + s4)] = sv
            return cnt, sv

        A_CNT, SV_A = plan_ln_af(TT, 0)
        a_square = lambda i: 5 * (i // 4) + (i % 4) + 1
        a_sqrt = lambda g: 5 * g + 5
        SS_A = 5 * 8
        a_ds = lambda i: 16 * (i // 8 + 1)
        a_za = lambda i: 16 * (i // 8 + 1)
        SM_C0, SV_C0 = 0, SV_A
        c_qkmm = lambda n: SM_C0 + n + 1
        c_vmm = lambda n: SM_C0 + 8 + n + 1
        c_qkev = lambda n: SV_C0 + 4 * n + 4
        c_vev = lambda n: SV_C0 + 32 + n + 1
        SM_C = SM_C0 + 16
        SV_C = SV_C0 + 40
        SM_D0, SS_D0, SV_D0 = SM_C, SS_A, SV_C
        d_P = lambda q: 0 if q == 0 else 24 * q - 8
        d_smm = lambda kt, c: (SM_D0 + d_P(kt // 2) + 4 * (c // 2) + 2 * (kt % 2)
                               + (c % 2) + 1)
        d_eidx = lambda kt, s: 2 * (4 * (kt // 2) + s) + (kt % 2)

        def d_einv(e):
            sig, a = e // 2, e % 2
            return 2 * (sig // 4) + a, sig % 4

        def d_outmm(t):
            q = t // 2
            if q <= 13:
                return SM_D0 + d_P(q + 2) - 4 * (1 - (t % 2))
            return SM_D0 + 368 + 8 * (q - 14) + 4 * (t % 2 + 1)
        d_exp = lambda kt, s: SS_D0 + 8 * (kt // 2) + 2 * s + (kt % 2) + 1
        D_CNT = {}
        _sv = SV_D0
        for _kt in range(4):
            _sv += 1
            D_CNT[("max", _kt)] = _sv
        for _q in range(16):
            for _a in range(2):
                _kt = 2 * _q + _a
                for _op in ("ellsum", "recip", "vsc", "xp"):
                    _sv += 1
                    D_CNT[(_op, _kt)] = _sv
            if _q + 2 < 16:
                for _a in range(2):
                    _sv += 1
                    D_CNT[("max", 2 * (_q + 2) + _a)] = _sv
        d_max = lambda kt: D_CNT[("max", kt)]
        d_ellsum = lambda kt: D_CNT[("ellsum", kt)]
        d_recip = lambda kt: D_CNT[("recip", kt)]
        d_vsc = lambda kt: D_CNT[("vsc", kt)]
        d_xp = lambda kt: D_CNT[("xp", kt)]
        SM_D = SM_D0 + 384
        SS_D = SS_D0 + 128
        SV_D = _sv
        assert SV_D == SV_D0 + 160
        e_evac = SV_D + 2
        e_tpmm = lambda b: SM_D + b + 1
        e_oxmm = SM_D + 16 + KT
        e_tpev = lambda b: e_evac + b + 1
        e_oxev = e_evac + 17
        SM_E = e_oxmm
        SV_E = e_oxev
        SV_F0, SS_F0 = SV_E, SS_D
        SV_TAIL = SV_F0 + 11
        SS_TAIL = SS_F0 + 2

        def plan_ln_af2(n_tiles, sv0):
            ng = n_tiles // 4
            cnt = {}
            sv = sv0
            for s4 in range(4):
                sv += 1
                cnt[("x1", s4)] = sv
            for g in range(1, ng + 1):
                gm = g - 1
                for op in ("smean", "svar", "msq", "varsub", "recip", "nmr"):
                    sv += 1
                    cnt[(op, gm)] = sv
                for s4 in range(4):
                    sv += 1
                    cnt[("apply", 4 * gm + s4)] = sv
                if g < ng:
                    for s4 in range(4):
                        i = 4 * g + s4
                        sv += 1
                        cnt[("x1", i)] = sv
            return cnt, sv

        F_CNT, SV_Floop = plan_ln_af2(TOWN, SV_TAIL)
        f_sumss = lambda t: SS_TAIL + 9 * (t // 4) + 2 * (t % 4) + 1
        f_square = lambda t: SS_TAIL + 9 * (t // 4) + 2 * (t % 4) + 2
        f_sqrt = lambda g: SS_TAIL + 9 * g + 9
        SV_F = SV_Floop
        SS_F = SS_TAIL + 36
        SD_E0 = 0
        sd_xrowd = SD_E0 + 16
        sd_outxrow = SD_E0 + 32
        sd_xlast = SD_E0 + 48
        sd_row63 = SD_E0 + 64
        sd_zero1 = SD_E0 + 80
        sd_zero2 = SD_E0 + 96
        SD_F = sd_zero2
        szt_g_hi = lambda ct: 16 * (ct + 1)
        SS_H0 = SS_F
        h_tsr = lambda ct: SS_H0 + ct + 1
        h_tsk = lambda ct: SS_H0 + 8 + ct + 1
        SS_H = SS_H0 + 16
        SV_H0 = SV_F + 1
        h_xr_all = SV_H0 + 8
        SV_H = SV_H0 + 16
        SH_ALL = 0
        SM_I0, SS_I0 = SM_E, SS_H
        i_mm = lambda t: SM_I0 + t + 1
        i_sig = lambda t: SS_I0 + t + 1
        SM_I = SM_I0 + 16
        SS_I = SS_I0 + 16
        SM_J0, SS_J0, SV_J0 = SM_I, SS_I, SV_H
        j_mm = lambda ix: SM_J0 + ix + 1
        j_relu = lambda ix: SS_J0 + ix + 1
        j_sq = lambda ix: SV_J0 + ix + 1
        SM_J = SM_J0 + 32
        SS_J = SS_J0 + 32
        SV_J = SV_J0 + 32
        SM_K0, SV_K0 = SM_J, SV_J
        k_mm = lambda t: SM_K0 + t + 1
        k_tmp = lambda t: SV_K0 + 2 * t + 1
        k_x2 = lambda t: SV_K0 + 2 * t + 2
        k_ydma = lambda t: 16 * (t + 1)

        @block.gpsimd
        def _(g):
            g.dma_start(out=wqk[:, :].rearrange("p (ct d) -> p ct d", d=128),
                        in_=wqk_in.rearrange("(ct p) d -> p ct d", p=128)).then_inc(SW, 16)
            g.dma_start(out=wkq[:, :].rearrange("p (ct d) -> p ct d", d=128),
                        in_=wkq_in.rearrange("(ct p) d -> p ct d", p=128)).then_inc(SW, 16)
            g.dma_start(out=wvv[:, :].rearrange("p (ct d) -> p ct d", d=128),
                        in_=wvv_in.rearrange("(ct p) d -> p ct d", p=128)).then_inc(SW, 16)
            g.dma_start(out=mu[:, :], in_=mu_in).then_inc(SW, 16)
            g.dma_start(out=hmask[:, :], in_=hmask_in).then_inc(SW, 16)
            g.dma_start(out=ident[:, :], in_=ident_in).then_inc(SW, 16)
        SW_SMALL = 96
        SWB_ALL = 48

        @block.sync
        def _(sy):
            for i in range(TT):
                s = i % 8
                if i >= 8:
                    sy.wait_ge(SV, A_CNT[("apply", i - 8)])
                    sy.wait_ge(SS, a_square(i - 8))
                sy.dma_start(out=xst[:, s * C:(s + 1) * C],
                             in_=x_in[i * 128:(i + 1) * 128, :]).then_inc(DSs[s], 16)
                if i >= 8:
                    j = i - 8
                    sy.dma_start(out=zdram[j * 128:(j + 1) * 128, :],
                                 in_=zst[:, (j % 8) * C:(j % 8 + 1) * C]).then_inc(ZAs[j % 8], 16)
            for j in range(TT - 8, TT):
                sy.wait_ge(SV, A_CNT[("apply", j)])
                sy.dma_start(out=zdram[j * 128:(j + 1) * 128, :],
                             in_=zst[:, (j % 8) * C:(j % 8 + 1) * C]).then_inc(ZAs[j % 8], 16)

        @block.scalar
        def _(sc):
            for g in range(8):
                gb = (g % 2) * 4
                for s in range(4):
                    i = g * 4 + s
                    sc.wait_ge(DSs[i % 8], a_ds(i))
                    sc.wait_ge(SS, a_square(i) - 1)
                    if i >= 8:
                        sc.wait_ge(ZAs[i % 8], a_za(i - 8))
                    if g >= 2:
                        sc.wait_ge(SV, A_CNT[("svar", g - 2)])
                    sc.activation(zst[:, (i % 8) * C:(i % 8 + 1) * C],
                                  xst[:, (i % 8) * C:(i % 8 + 1) * C],
                                  AF.Square,
                                  accum_out=sumsq[:, i % 8:i % 8 + 1]).then_inc(SS, 1)
                sc.wait_ge(SV, A_CNT[("varsub", g)])
                if g >= 2:
                    sc.wait_ge(SV, A_CNT[("recip", g - 2)])
                sc.activation(sstd[:, gb:gb + 4], svar[:, gb:gb + 4], AF.Sqrt,
                              bias=cconst[:, 0:1]).then_inc(SS, 1)
                if g % 2 == 1:
                    qtr = (g - 1) // 2
                    for s in range(8):
                        sc.wait_ge(ZAs[s], 16 * (qtr + 1))
                    for ctn in range(CT):
                        sc.dma_start(out=zT[ctn][:, qtr * 1024:(qtr + 1) * 1024],
                                     in_=zdram[qtr * 1024:(qtr + 1) * 1024,
                                               ctn * 128:(ctn + 1) * 128],
                                     transpose=True).then_inc(SZQ[qtr], 16)

        def emit_ln_fin(v, g, cnt, sq_last, sq_grp):
            gb = (g % 2) * 4
            cb = slice(4 * g % 8, 4 * g % 8 + 4)
            v.wait_ge(SS, sq_last)
            v.wait_ge(SV, cnt.get(("sum", 4 * g + 3), cnt.get(("x1", 4 * g + 3), 0)))
            v.tensor_scalar(smean[:, gb:gb + 4], sums[:, cb], cconst[:, 2:3],
                            None, ALU.mult).then_inc(SV, 1)
            v.tensor_scalar(svar[:, gb:gb + 4], sumsq[:, cb], cconst[:, 2:3],
                            None, ALU.mult).then_inc(SV, 1)
            v.wait_ge(SV, cnt[("smean", g)])
            v.tensor_tensor(sstd[:, gb:gb + 4], smean[:, gb:gb + 4],
                            smean[:, gb:gb + 4], ALU.mult).then_inc(SV, 1)
            v.wait_ge(SV, cnt[("msq", g)])
            v.tensor_tensor(svar[:, gb:gb + 4], svar[:, gb:gb + 4],
                            sstd[:, gb:gb + 4], ALU.subtract).then_inc(SV, 1)
            v.wait_ge(SS, sq_grp)
            v.reciprocal(srstd[:, gb:gb + 4], sstd[:, gb:gb + 4]).then_inc(SV, 1)
            v.wait_ge(SV, cnt[("recip", g)])
            v.scalar_tensor_tensor(snmr[:, gb:gb + 4], smean[:, gb:gb + 4],
                                   cconst[:, 1:2], srstd[:, gb:gb + 4],
                                   ALU.mult, ALU.mult).then_inc(SV, 1)

        @block.vector
        def _(v):
            v.memset(cconst[:, 0:1], EPS)
            v.memset(cconst[:, 1:2], -1.0)
            v.memset(cconst[:, 2:3], RC)
            v.memset(onec[:, :], 1.0)

            def a_apply_emit(v, g):
                gb = (g % 2) * 4
                for s in range(4):
                    i = g * 4 + s
                    if i >= 8:
                        v.wait_ge(ZAs[i % 8], a_za(i - 8))
                    v.wait_ge(SV, A_CNT[("nmr", g)])
                    v.tensor_scalar(zst[:, (i % 8) * C:(i % 8 + 1) * C],
                                    xst[:, (i % 8) * C:(i % 8 + 1) * C],
                                    srstd[:, gb + s:gb + s + 1],
                                    snmr[:, gb + s:gb + s + 1],
                                    ALU.mult, ALU.add).then_inc(SV, 1)

            def a_sums_emit(v, g):
                for s in range(4):
                    i = g * 4 + s
                    v.wait_ge(DSs[i % 8], a_ds(i))
                    if g >= 2:
                        v.wait_ge(SV, A_CNT[("smean", g - 2)])
                    v.reduce_sum(sums[:, i % 8:i % 8 + 1],
                                 xst[:, (i % 8) * C:(i % 8 + 1) * C],
                                 axis=AX.X).then_inc(SV, 1)

            a_sums_emit(v, 0)
            for g in range(1, 8):
                emit_ln_fin(v, g - 1, A_CNT, a_square(4 * (g - 1) + 3),
                            a_sqrt(g - 1))
                a_apply_emit(v, g - 1)
                a_sums_emit(v, g)
            emit_ln_fin(v, 7, A_CNT, a_square(31), a_sqrt(7))
            a_apply_emit(v, 7)


        cm_qk = [nc.psum_tensor(f"qk{i}", [128, 512], F32) for i in range(4)]
        qk = opens(cm_qk)
        cm_pvv = [nc.psum_tensor(f"pvv{i}", [128, 512], F32) for i in range(2)]
        pvv = opens(cm_pvv)

        @block.tensor
        def _(t):
            t.wait_ge(SW, SW_SMALL)
            for nch in range(8):
                buf = nch % 2
                t.wait_ge(SZQ[nch // 2], 128)
                if nch >= 2:
                    t.wait_ge(SV, c_qkev(nch - 2))
                for ct in range(CT):
                    t.matmul(qk[buf][:, :], wqk[:, ct * 128:(ct + 1) * 128],
                             zT[ct][:, nch * 512:(nch + 1) * 512],
                             start=(ct == 0), stop=(ct == CT - 1),
                             skip_group_check=True)
                for ct in range(CT):
                    ins = t.matmul(qk[2 + buf][:, :], wkq[:, ct * 128:(ct + 1) * 128],
                                   zT[ct][:, nch * 512:(nch + 1) * 512],
                                   start=(ct == 0), stop=(ct == CT - 1),
                                   skip_group_check=True)
                ins.then_inc(SM, 1)
            for nch in range(8):
                t.wait_ge(SZQ[nch // 2], 128)
                if nch >= 2:
                    t.wait_ge(SV, c_vev(nch - 2))
                for ct in range(CT):
                    ins = t.matmul(pvv[nch % 2][:, :],
                                   wvv[:, ct * 128:(ct + 1) * 128],
                                   zT[ct][:, nch * 512:(nch + 1) * 512],
                                   start=(ct == 0), stop=(ct == CT - 1),
                                   skip_group_check=True)
                ins.then_inc(SM, 1)

        @block.vector
        def _(v):
            for nch in range(8):
                sl = slice(nch * 512, (nch + 1) * 512)
                buf = nch % 2
                v.wait_ge(SM, c_qkmm(nch))
                v.tensor_copy(qrep[0:64, sl], qk[buf][0:64, :]).then_inc(SV, 1)
                v.tensor_copy(krep[64:128, sl], qk[buf][64:128, :]).then_inc(SV, 1)
                v.tensor_copy(krep[0:64, sl], qk[2 + buf][0:64, :]).then_inc(SV, 1)
                v.tensor_copy(qrep[64:128, sl], qk[2 + buf][64:128, :]).then_inc(SV, 1)
            for nch in range(8):
                sl = slice(nch * 512, (nch + 1) * 512)
                v.wait_ge(SM, c_vmm(nch))
                v.tensor_copy(vT_sb[0:64, sl], pvv[nch % 2][0:64, :]).then_inc(SV, 1)

        @block.sync
        def _(sy):
            sy.wait_ge(SV, SV_C)
            sy.dma_start(out=vdram, in_=vT_sb[0:64, :]).then_inc(SVT, 16)
            sy.wait_ge(SVT, 16)
            for kt in range(KT):
                sy.wait_ge(SVT, 16 * (kt + 1))
                sy.dma_start(out=v_sb[:, kt * D:(kt + 1) * D],
                             in_=vdram[:, kt * 128:(kt + 1) * 128],
                             transpose=True).then_inc(SVT, 16)

        closes(zT_cm)

        closes(cm_pvv)
        closes(cm_qk)
        out_cm = [
            nc.sbuf_tensor("out_tok", [128, TOWN * D], F32, side="right"),
        ]
        out_tok, = opens(out_cm)
        z1T_cm = [nc.sbuf_tensor(f"z1T{i}", [128, 2080], BF16, side="right") for i in range(CT)]
        z1T = opens(z1T_cm)
        out2_cm = [
            nc.sbuf_tensor("out_sb", [128, 1024], F32, side="right"),
            nc.sbuf_tensor("outx_sb", [64, 1], F32, side="right"),
            nc.sbuf_tensor("xp_sb", [128, KT], BF16, side="right"),
        ]
        out_sb, outx_sb, xp_sb = opens(out2_cm)
        cm_p23 = [nc.sbuf_tensor("p_sb1", [128, S], BF16, side="right"),
                  nc.sbuf_tensor("p_sb2", [128, S], BF16, side="right"),
                  nc.sbuf_tensor("p_sb3", [128, S], BF16, side="right")]
        p_sb.extend(opens(cm_p23))
        cm_dp = nc.psum_tensor("dpool", [128, 4096], F32)
        dpool = cm_dp.__enter__()
        units = [dpool[:, 1024 + u * 1024:2048 + u * 1024] for u in range(3)]

        @block.tensor
        def _(t):
            t.wait_ge(SV, SV_C)
            for q in range(KT // 2):
                if q >= 2:
                    for tt_ in (2 * (q - 2), 2 * (q - 2) + 1):
                        t.wait_ge(SV, d_vsc(tt_))
                        t.wait_ge(SS, d_exp(tt_, 3))
                        for w in range(4):
                            half = w // 2
                            t.matmul(dpool[64 * half:64 * half + 64,
                                           (w % 2) * 512:(w % 2) * 512 + 512],
                                     vs_sb[:, tt_ * D:(tt_ + 1) * D],
                                     p_sb[(tt_ + 1) % 4][:, w * 512:(w + 1) * 512],
                                     start=(tt_ == 0), stop=(tt_ == KT - 1),
                                     skip_group_check=True,
                                     tile_position=(0, 64 * half)).then_inc(SM, 1)
                for s in range(4):
                    for a in range(2):
                        kt = 2 * q + a
                        e = d_eidx(kt, s)
                        u = e % 3
                        if e >= 3:
                            ktp_, sp_ = d_einv(e - 3)
                            t.wait_ge(SS, d_exp(ktp_, sp_))
                            if sp_ == 0:
                                t.wait_ge(SV, d_max(ktp_))
                        for cc in range(2):
                            c = 2 * s + cc
                            t.matmul(units[u][:, cc * 512:(cc + 1) * 512],
                                     krep[64 * a:64 * (a + 1), kt * 128:(kt + 1) * 128],
                                     qrep[64 * a:64 * (a + 1), c * 512:(c + 1) * 512],
                                     start=True, stop=True, skip_group_check=True,
                                     tile_position=(64 * a, 0)).then_inc(SM, 1)
            for tt_ in range(KT - 4, KT):
                t.wait_ge(SV, d_vsc(tt_))
                t.wait_ge(SS, d_exp(tt_, 3))
                for w in range(4):
                    half = w // 2
                    t.matmul(dpool[64 * half:64 * half + 64,
                                   (w % 2) * 512:(w % 2) * 512 + 512],
                             vs_sb[:, tt_ * D:(tt_ + 1) * D],
                             p_sb[(tt_ + 1) % 4][:, w * 512:(w + 1) * 512],
                             start=(tt_ == 0), stop=(tt_ == KT - 1),
                             skip_group_check=True,
                             tile_position=(0, 64 * half)).then_inc(SM, 1)

        @block.vector
        def _(v):
            def d_maxes(v, q):
                for a in range(2):
                    kt = 2 * q + a
                    u0 = d_eidx(kt, 0) % 3
                    v.wait_ge(SM, d_smm(kt, 1))
                    v.reduce_max(negm[:, kt:kt + 1],
                                 units[u0][:, 0:512:4],
                                 axis=AX.X, negate=True).then_inc(SV, 1)

            d_maxes(v, 0)
            d_maxes(v, 1)
            for q in range(KT // 2):
                for a in range(2):
                    kt = 2 * q + a
                    v.wait_ge(SS, d_exp(kt, 3))
                    v.reduce_sum(ell[:, kt:kt + 1],
                                 ellp[:, 4 * (kt % 4):4 * (kt % 4) + 4],
                                 axis=AX.X).then_inc(SV, 1)
                    v.wait_ge(SV, d_ellsum(kt))
                    v.reciprocal(rell[:, kt:kt + 1], ell[:, kt:kt + 1]).then_inc(SV, 1)
                    v.wait_ge(SV, d_recip(kt))
                    v.wait_ge(SVT, 16 * (kt + 2))
                    v.tensor_scalar(vs_sb[:, kt * D:(kt + 1) * D],
                                    v_sb[:, kt * D:(kt + 1) * D],
                                    rell[:, kt:kt + 1], None, ALU.mult).then_inc(SV, 1)
                    v.tensor_copy(xp_sb[:, kt:kt + 1],
                                  p_sb[(kt + 1) % 4][:, 4095:4096]).then_inc(SV, 1)
                if q + 2 < KT // 2:
                    d_maxes(v, q + 2)

        @block.scalar
        def _(sc):
            for q in range(KT // 2):
                for s in range(4):
                    for a in range(2):
                        kt = 2 * q + a
                        u = d_eidx(kt, s) % 3
                        if s == 0:
                            if kt == 3:
                                sc.wait_ge(SVT, 16)
                            sc.wait_ge(SV, d_max(kt))
                            if kt >= 4:
                                sc.wait_ge(SV, d_ellsum(kt - 4))
                                sc.wait_ge(SM, d_outmm(kt - 4))
                        sc.wait_ge(SM, d_smm(kt, 2 * s + 1))
                        sc.activation(p_sb[(kt + 1) % 4][:, s * 1024:(s + 1) * 1024],
                                      units[u][:, :], AF.Exp,
                                      bias=negm[:, kt:kt + 1],
                                      accum_out=ellp[:, 4 * (kt % 4) + s:
                                                     4 * (kt % 4) + s + 1]).then_inc(SS, 1)

        @block.gpsimd
        def _(g):
            g.wait_ge(SS, d_exp(7, 3))
            g.dma_start(out=wrT[:, :].rearrange("p (ct d) -> p ct d", d=C),
                        in_=wrT_in.rearrange("(ct p) d -> p ct d", p=128)).then_inc(SWB, 16)
            g.dma_start(out=wkT[:, :].rearrange("p (ct d) -> p ct d", d=C),
                        in_=wkT_in.rearrange("(ct p) d -> p ct d", p=128)).then_inc(SWB, 16)
            g.dma_start(out=wvT[:, :].rearrange("p (ct d) -> p ct d", d=C),
                        in_=wvT_in.rearrange("(ct p) d -> p ct d", p=128)).then_inc(SWB, 16)

        @block.vector
        def _(v):
            v.wait_ge(SM, d_outmm(KT - 1))
            v.tensor_copy(out_sb[0:64, 0:1024], dpool[0:64, 0:1024]).then_inc(SV, 1)
            v.tensor_copy(out_sb[64:128, 0:1024], dpool[64:128, 0:1024]).then_inc(SV, 1)

        if debug:
            @block.gpsimd
            def _(g):
                g.wait_ge(SV, e_evac)
                g.dma_start(out=d_ell[:, 0:KT], in_=ell[:, :]).then_inc(SW, 16)
                g.dma_start(out=d_ell[:, KT:2 * KT], in_=negm[:, :]).then_inc(SW, 16)
                g.dma_start(out=d_out[:, 0:1024], in_=out_sb[0:64, 0:1024]).then_inc(SW, 16)
                g.wait_ge(SW, SW_SMALL + 112)

        cm_dp.__exit__(None, None, None)
        cm_tp = [nc.psum_tensor(f"tp{i}", [128, 64], F32) for i in range(4)]
        tp_ps = opens(cm_tp)
        cm_oxe = nc.psum_tensor("oxe", [64, 64], F32)
        ox_e = cm_oxe.__enter__()

        @block.tensor
        def _(t):
            t.wait_ge(SV, e_evac)
            t.wait_ge(SS, SS_D)
            for b in range(16):
                if b >= 4:
                    t.wait_ge(SV, e_tpev(b - 4))
                hb = b // 8
                t.transpose(tp_ps[b % 4][:, :],
                            out_sb[64 * hb:64 * hb + 64,
                                   (b % 8) * 128:(b % 8 + 1) * 128],
                            ident[64 * hb:64 * hb + 64,
                                  64 * hb:64 * hb + 64]).then_inc(SM, 1)
            for kt in range(KT):
                t.matmul(ox_e[:, 0:1],
                         vs_sb[:, kt * D:(kt + 1) * D],
                         xp_sb[:, kt:kt + 1],
                         start=(kt == 0), stop=(kt == KT - 1),
                         skip_group_check=True).then_inc(SM, 1)

        @block.vector
        def _(v):
            for b in range(16):
                v.wait_ge(SM, e_tpmm(b))
                v.tensor_copy(out_tok[:, b * D:(b + 1) * D],
                              tp_ps[b % 4][:, :]).then_inc(SV, 1)
            v.wait_ge(SM, e_oxmm)
            v.tensor_copy(outx_sb[:, :], ox_e[:, 0:1]).then_inc(SV, 1)

        @block.sync
        def _(sy):
            sy.wait_ge(SV, e_oxev)
            sy.wait_ge(SD, SD_E0)
            sy.dma_start(out=xrow_d.rearrange("a (p o) -> p (a o)", p=64),
                         in_=outx_sb[0:64, 0:1]).then_inc(SD, 16)
            sy.wait_ge(SD, sd_xrowd)
            sy.dma_start(out=outx_row[:, :], in_=xrow_d).then_inc(SD, 16)
            sy.wait_ge(SD, sd_outxrow)
            sy.dma_start(out=xlast, in_=x_in[4095:4096, :]).then_inc(SD, 16)

        if debug:
            @block.gpsimd
            def _(g):
                g.wait_ge(SV, SV_E)
                g.dma_start(out=d_tok[:, 0:TOWN * D], in_=out_tok[:, :]).then_inc(SW, 16)
                g.dma_start(out=d_tok[:, TOWN * D:], in_=out_tok[:, 0:D]).then_inc(SW, 16)
                g.wait_ge(SW, SW_SMALL + 144)

        cm_oxe.__exit__(None, None, None)
        closes(cm_tp)
        closes(cm_p23)
        closes(attn_cm)
        x1f_cm = [nc.sbuf_tensor("x1b", [128, 8 * C], F32),
                  nc.sbuf_tensor("z1b", [128, 8 * C], BF16)]
        x1b, z1b = opens(x1f_cm)

        @block.gpsimd
        def _(g_):
            for t_ in range(TOWN):
                if t_ >= 8:
                    g_.wait_ge(SV, F_CNT[("apply", t_ - 8)])
                    g_.wait_ge(SS, f_square(t_ - 8))
                else:
                    g_.wait_ge(SV, A_CNT[("apply", 24 + t_)])
                    g_.wait_ge(SS, a_square(24 + t_))
                g_.dma_start(out=xst[:, (t_ % 8) * C:(t_ % 8 + 1) * C],
                             in_=x_in[t_ * 128:(t_ + 1) * 128, :]).then_inc(FSs[t_ % 8], 16)

        @block.sync
        def _(sy):
            sy.wait_ge(SV, SV_F0 + 10)
            sy.wait_ge(SD, sd_row63 - 16)
            sy.dma_start(out=z1dram[15:16, :], in_=z1xm).then_inc(SD, 16)
            sy.wait_ge(SV, SV_F0 + 11)
            sy.wait_ge(SD, sd_zero1 - 16)
            sy.dma_start(out=z1dram[0:15, :],
                         in_=bass.AP(zst, 0, [[8 * C, 15], [1, C]])).then_inc(SD, 16)
            sy.wait_ge(SD, sd_zero2 - 16)
            sy.dma_start(out=z1dram[2064:2080, :],
                         in_=bass.AP(zst, 0, [[8 * C, 16], [1, C]])).then_inc(SD, 16)
            for g in range(4):
                for s in range(4):
                    t_ = g * 4 + s
                    sy.wait_ge(SV, F_CNT[("apply", t_)])
                    sy.dma_start(out=z1dram[16 + t_ * 128:16 + (t_ + 1) * 128, :],
                                 in_=z1b[:, (t_ % 8) * C:(t_ % 8 + 1) * C]).then_inc(ZSs[t_ % 8], 16)

        @block.vector
        def _(v):
            def f_x1_sum(v, t_):
                s8 = t_ % 8
                v.wait_ge(FSs[s8], fs_f(t_))
                if t_ >= 8:
                    v.wait_ge(SS, f_square(t_ - 8))
                v.tensor_tensor(
                    x1b[:, s8 * C:(s8 + 1) * C],
                    xst[:, s8 * C:(s8 + 1) * C],
                    bass.AP(out_tok, t_ * D, [[TOWN * D, 128], [0, 16], [1, D]]),
                    ALU.add).then_inc(SV, 1)

            def f_apply_emit(v, g):
                gb = (g % 2) * 4
                for s in range(4):
                    t_ = g * 4 + s
                    if t_ >= 8:
                        v.wait_ge(ZSs[t_ % 8], zs_f(t_ - 8))
                    v.wait_ge(SV, F_CNT[("nmr", g)])
                    v.tensor_scalar(z1b[:, (t_ % 8) * C:(t_ % 8 + 1) * C],
                                    x1b[:, (t_ % 8) * C:(t_ % 8 + 1) * C],
                                    srstd[:, gb + s:gb + s + 1],
                                    snmr[:, gb + s:gb + s + 1],
                                    ALU.mult, ALU.add).then_inc(SV, 1)

            v.wait_ge(SD, sd_xlast)
            v.tensor_tensor(x1x, xlast,
                            bass.AP(outx_row, 0, [[D, 1], [0, 16], [1, D]]),
                            ALU.add).then_inc(SV, 1)
            v.wait_ge(SV, SV_F0 + 1)
            v.reduce_sum(rs[:, 0:1], x1x, axis=AX.X).then_inc(SV, 1)
            v.wait_ge(SS, SS_F0 + 1)
            v.wait_ge(SV, SV_F0 + 2)
            v.tensor_scalar(rs[:, 2:3], rs[:, 0:1], cconst[0:1, 2:3], None, ALU.mult).then_inc(SV, 1)
            v.tensor_scalar(rs[:, 3:4], rs[:, 1:2], cconst[0:1, 2:3], None, ALU.mult).then_inc(SV, 1)
            v.wait_ge(SV, SV_F0 + 3)
            v.tensor_tensor(rs[:, 4:5], rs[:, 2:3], rs[:, 2:3], ALU.mult).then_inc(SV, 1)
            v.wait_ge(SV, SV_F0 + 5)
            v.tensor_tensor(rs[:, 5:6], rs[:, 3:4], rs[:, 4:5], ALU.subtract).then_inc(SV, 1)
            v.wait_ge(SS, SS_TAIL)
            v.reciprocal(rs[:, 7:8], rs[:, 6:7]).then_inc(SV, 1)
            v.wait_ge(SV, SV_F0 + 7)
            v.scalar_tensor_tensor(rs[:, 8:9], rs[:, 2:3], cconst[0:1, 1:2], rs[:, 7:8],
                                   ALU.mult, ALU.mult).then_inc(SV, 1)
            v.wait_ge(SV, SV_F0 + 8)
            v.tensor_scalar(z1x, x1x, rs[:, 7:8], rs[:, 8:9],
                            ALU.mult, ALU.add).then_inc(SV, 1)
            v.wait_ge(SW, SW_SMALL)
            v.wait_ge(SV, SV_F0 + 9)
            v.tensor_scalar(z1xm, z1x, hmask[0:1, 0:1], None,
                            ALU.mult).then_inc(SV, 1)
            v.memset(zst[:, 0:C], 0.0).then_inc(SV, 1)
            for s in range(4):
                f_x1_sum(v, s)
            for g in range(1, 4):
                emit_ln_fin(v, g - 1, F_CNT, f_square(4 * (g - 1) + 3),
                            f_sqrt(g - 1))
                f_apply_emit(v, g - 1)
                for s in range(4):
                    f_x1_sum(v, g * 4 + s)
            emit_ln_fin(v, 3, F_CNT, f_square(15), f_sqrt(3))
            f_apply_emit(v, 3)

        @block.scalar
        def _(sc):
            sc.wait_ge(SV, SV_F0 + 1)
            sc.activation(jrow, x1x, AF.Square,
                          accum_out=rs[:, 1:2]).then_inc(SS, 1)
            sc.wait_ge(SV, SV_F0 + 6)
            sc.activation(rs[:, 6:7], rs[:, 5:6], AF.Sqrt, bias=cconst[0:1, 0:1]).then_inc(SS, 1)
            for g in range(4):
                gb = (g % 2) * 4
                for s in range(4):
                    t_ = g * 4 + s
                    sc.wait_ge(SV, F_CNT[("x1", t_)])
                    sc.wait_ge(SS, f_sumss(t_) - 1)
                    if g >= 2:
                        sc.wait_ge(SV, F_CNT[("svar", g - 2)])
                        sc.wait_ge(SV, F_CNT[("smean", g - 2)])
                    sc.activation(zst[:, 2 * C:3 * C], x1b[:, (t_ % 8) * C:(t_ % 8 + 1) * C],
                                  AF.Copy,
                                  accum_out=sums[:, t_ % 8:t_ % 8 + 1]).then_inc(SS, 1)
                    sc.wait_ge(SS, f_sumss(t_))
                    sc.activation(zst[:, 2 * C:3 * C], x1b[:, (t_ % 8) * C:(t_ % 8 + 1) * C],
                                  AF.Square,
                                  accum_out=sumsq[:, t_ % 8:t_ % 8 + 1]).then_inc(SS, 1)
                sc.wait_ge(SV, F_CNT[("varsub", g)])
                if g >= 2:
                    sc.wait_ge(SV, F_CNT[("recip", g - 2)])
                sc.activation(sstd[:, gb:gb + 4], svar[:, gb:gb + 4], AF.Sqrt,
                              bias=cconst[:, 0:1]).then_inc(SS, 1)

        @block.scalar
        def _(sc):
            sc.wait_ge(SD, sd_zero1)
            for s in range(8):
                sc.wait_ge(ZSs[s], 16)
            for ctn in range(CT):
                sc.dma_start(out=z1T[ctn][:, 0:1040],
                             in_=z1dram[0:1040, ctn * 128:(ctn + 1) * 128],
                             transpose=True).then_inc(SG_LO, 16)
        @block.sync
        def _(sy):
            sy.wait_ge(SD, sd_zero2)
            for s in range(8):
                sy.wait_ge(ZSs[s], 32)
            sy.wait_ge(SG_LO, 128)
            for ctn in range(CT):
                sy.wait_ge(SG_HI, 16 * ctn)
                sy.dma_start(out=z1T[ctn][:, 1040:2080],
                             in_=z1dram[1040:2080, ctn * 128:(ctn + 1) * 128],
                             transpose=True).then_inc(SG_HI, 16)

        if debug:
            @block.gpsimd
            def _(g):
                g.wait_ge(SG_HI, 128)
                g.dma_start(out=d_z1, in_=z1dram).then_inc(SW, 16)
                g.wait_ge(SW, SW_SMALL + 160)

        closes(x1f_cm)
        closes(out2_cm)
        xk_cm = [nc.sbuf_tensor(f"xk{i}", [128, 2048], BF16) for i in range(CT)]
        xk = opens(xk_cm)
        xr_cm = [nc.sbuf_tensor(f"xr{i}", [128, 2048], BF16) for i in range(CT)]
        xr = opens(xr_cm)
        dbuf = zst[:, 0:2048]

        @block.scalar
        def _(sc):
            sc.wait_ge(SV, F_CNT[("apply", 15)])
            for ct in range(CT):
                sc.wait_ge(SG_HI, szt_g_hi(ct))
                sc.activation(xr[ct][:, :], z1T[ct][:, 16:2064], AF.Identity,
                              bias=0.0, scale=mu[:, ct:ct + 1]).then_inc(SS, 1)
            for ct in range(CT):
                sc.activation(xk[ct][:, :], z1T[ct][:, 16:2064], AF.Identity,
                              bias=0.0, scale=mu[:, 8 + ct:9 + ct]).then_inc(SS, 1)

        @block.vector
        def _(v):
            v.wait_ge(SS, f_square(15))
            v.scalar_tensor_tensor(mu1m[:, :], mu[:, :], -1.0, onec[:, :],
                                   ALU.mult, ALU.add).then_inc(SV, 1)
            v.wait_ge(SV, SV_H0)
            for ct in range(CT):
                v.wait_ge(SS, h_tsr(ct))
                v.wait_ge(SV, SV_H0 + ct)
                v.scalar_tensor_tensor(xr[ct][:, :], z1T[ct][:, 15:2063],
                                       mu1m[:, ct:ct + 1],
                                       xr[ct][:, :], ALU.mult,
                                       ALU.add).then_inc(SV, 1)
            for ct in range(CT):
                v.wait_ge(SS, h_tsk(ct))
                v.wait_ge(SV, SV_H0 + 8 + ct)
                v.scalar_tensor_tensor(xk[ct][:, :], z1T[ct][:, 15:2063],
                                       mu1m[:, 8 + ct:9 + ct],
                                       xk[ct][:, :], ALU.mult,
                                       ALU.add).then_inc(SV, 1)

        closes(z1T_cm)
        r_sb_cm = [nc.sbuf_tensor("r_sb", [128, TOWN * C], BF16, side="right")]
        r_sb, = opens(r_sb_cm)
        cm_rps = [nc.psum_tensor(f"rps{i}", [128, C], F32) for i in range(3)]
        r_ps = opens(cm_rps)

        @block.tensor
        def _(t):
            t.wait_ge(SV, h_xr_all)
            t.wait_ge(SS, SS_F)
            t.wait_ge(SWB, SWB_ALL)
            for t_ in range(TOWN):
                if t_ >= 3:
                    t.wait_ge(SS, i_sig(t_ - 3))
                for ct in range(CT):
                    for hh in range(2):
                        ins = t.matmul(
                            r_ps[t_ % 3][:, hh * 512:(hh + 1) * 512],
                            xr[ct][:, t_ * 128:(t_ + 1) * 128],
                            wrT[:, ct * C + hh * 512:ct * C + (hh + 1) * 512],
                            start=(ct == 0), stop=(ct == CT - 1),
                            skip_group_check=True)
                ins.then_inc(SM, 1)

        @block.scalar
        def _(sc):
            sc.wait_ge(SV, SV_H)
            for t_ in range(TOWN):
                sc.wait_ge(SM, i_mm(t_))
                sc.activation(r_sb[:, t_ * C:(t_ + 1) * C], r_ps[t_ % 3][:, :],
                              AF.Sigmoid).then_inc(SS, 1)

        closes(cm_rps)
        closes(xr_cm)

        if debug:
            @block.gpsimd
            def _(g):
                g.wait_ge(SS, SS_I)
                g.dma_start(out=d_r, in_=r_sb[:, :]).then_inc(SW, 16)
                g.wait_ge(SW, SW_SMALL + 192)

        kaT_cm = [nc.sbuf_tensor(f"kaT{i}", [128, 2048], BF16, side="right") for i in range(CT)]
        kaT = opens(kaT_cm)
        cm_ktp = [nc.psum_tensor(f"ktp{i}", [128, 512], F32) for i in range(3)]
        kt_ps = opens(cm_ktp)

        @block.tensor
        def _(t):
            t.wait_ge(SS, SS_I)
            for dt in range(CT):
                for tc in range(4):
                    ix = dt * 4 + tc
                    if ix >= 3:
                        t.wait_ge(SS, j_relu(ix - 3))
                    for ct in range(CT):
                        ins = t.matmul(
                            kt_ps[ix % 3][:, :],
                            wkT[:, ct * C + dt * 128:ct * C + (dt + 1) * 128],
                            xk[ct][:, tc * 512:(tc + 1) * 512],
                            start=(ct == 0), stop=(ct == CT - 1),
                            skip_group_check=True)
                    ins.then_inc(SM, 1)

        @block.scalar
        def _(sc):
            for dt in range(CT):
                for tc in range(4):
                    ix = dt * 4 + tc
                    sc.wait_ge(SM, j_mm(ix))
                    if ix >= 4:
                        sc.wait_ge(SV, j_sq(ix - 4))
                    sc.activation(kaT[dt][:, tc * 512:(tc + 1) * 512],
                                  kt_ps[ix % 3][:, :], AF.Relu).then_inc(SS, 1)

        @block.vector
        def _(v):
            for dt in range(CT):
                for tc in range(4):
                    ix = dt * 4 + tc
                    v.wait_ge(SS, j_relu(ix))
                    v.tensor_tensor(kaT[dt][:, tc * 512:(tc + 1) * 512],
                                    kaT[dt][:, tc * 512:(tc + 1) * 512],
                                    kaT[dt][:, tc * 512:(tc + 1) * 512],
                                    ALU.mult).then_inc(SV, 1)

        closes(cm_ktp)
        closes(xk_cm)

        if debug:
            @block.gpsimd
            def _(g):
                g.wait_ge(SV, SV_J)
                for dt in range(CT):
                    g.dma_start(out=d_ka[:, dt * 2048:(dt + 1) * 2048],
                                in_=kaT[dt][:, :]).then_inc(SW, 16)
                g.wait_ge(SW, SW_SMALL + 192 + 128)

        k32_cm = [nc.sbuf_tensor("tmp32", [128, 2 * C], F32, side="right"),
                  nc.sbuf_tensor("x2b", [128, 2 * C], F32, side="right")]
        tmp32, x2b = opens(k32_cm)
        cm_ops = [nc.psum_tensor(f"ops{i}", [128, C], F32) for i in range(3)]
        o_ps = opens(cm_ops)

        @block.tensor
        def _(t):
            t.wait_ge(SV, SV_J)
            t.wait_ge(SS, SS_J)
            for t_ in range(TOWN):
                if t_ >= 3:
                    t.wait_ge(SV, k_tmp(t_ - 3))
                for dt in range(CT):
                    for hh in range(2):
                        ins = t.matmul(
                            o_ps[t_ % 3][:, hh * 512:(hh + 1) * 512],
                            kaT[dt][:, t_ * 128:(t_ + 1) * 128],
                            wvT[:, dt * C + hh * 512:dt * C + (hh + 1) * 512],
                            start=(dt == 0), stop=(dt == CT - 1),
                            skip_group_check=True)
                ins.then_inc(SM, 1)

        @block.gpsimd
        def _(g):
            def k_xadd(g, t_):
                g.wait_ge(FSs[t_ % 8], fs_k(t_))
                g.tensor_tensor(
                    xst[:, (t_ % 8) * C:(t_ % 8 + 1) * C],
                    xst[:, (t_ % 8) * C:(t_ % 8 + 1) * C],
                    bass.AP(out_tok, t_ * D, [[TOWN * D, 128], [0, 16], [1, D]]),
                    ALU.add).then_inc(SH, 1)

            for t_ in range(TOWN):
                if t_ >= 4:
                    g.wait_ge(SV, k_x2(t_ - 4))
                else:
                    g.wait_ge(SV, SV_J)
                if t_ >= 8:
                    g.wait_ge(FSs[t_ % 8], fs_k(t_ - 8))
                g.dma_start(out=xst[:, (t_ % 8) * C:(t_ % 8 + 1) * C],
                            in_=x_in[t_ * 128:(t_ + 1) * 128, :]).then_inc(FSs[t_ % 8], 16)
                if t_ >= 1:
                    k_xadd(g, t_ - 1)
            k_xadd(g, TOWN - 1)

        def k_ydma_emit(sy, t_):
            sy.wait_ge(SV, k_x2(t_))
            sy.dma_start(out=y_out[t_ * 128:(t_ + 1) * 128, :],
                         in_=x2b[:, (t_ % 2) * C:(t_ % 2 + 1) * C]).then_inc(YSs[t_ % 2], 16)

        @block.sync
        def _(sy):
            for t_ in range(TOWN):
                if t_ >= 2:
                    k_ydma_emit(sy, t_ - 2)
            k_ydma_emit(sy, TOWN - 2)
            k_ydma_emit(sy, TOWN - 1)
            sy.wait_ge(YSs[0], 16 * (TOWN // 2))
            sy.wait_ge(YSs[1], 16 * (TOWN // 2))

        @block.vector
        def _(v):
            for t_ in range(TOWN):
                s = t_ % 4
                v.wait_ge(SM, k_mm(t_))
                v.wait_ge(SS, i_sig(t_))
                if t_ >= 2:
                    v.wait_ge(SV, k_x2(t_ - 2))
                    v.wait_ge(YSs[t_ % 2], ys_k(t_ - 2))
                v.tensor_tensor(tmp32[:, (t_ % 2) * C:(t_ % 2 + 1) * C],
                                o_ps[t_ % 3][:, :],
                                r_sb[:, t_ * C:(t_ + 1) * C], ALU.mult).then_inc(SV, 1)
                v.wait_ge(SH, t_ + 1)
                v.wait_ge(SV, k_tmp(t_))
                v.tensor_tensor(x2b[:, (t_ % 2) * C:(t_ % 2 + 1) * C],
                                tmp32[:, (t_ % 2) * C:(t_ % 2 + 1) * C],
                                xst[:, (t_ % 8) * C:(t_ % 8 + 1) * C], ALU.add).then_inc(SV, 1)


        closes(cm_ops)
        closes(k32_cm)
        closes(kaT_cm)
        closes(r_sb_cm)
        closes(out_cm)

    return nc


def kernel(L0_attn_x, L0_attn_Wq, L0_attn_Wk, L0_attn_Wv,
           L0_ffn_W_r, L0_ffn_W_k, L0_ffn_W_v, L0_ffn_mu_r, L0_ffn_mu_k):
    x = np.ascontiguousarray(np.asarray(L0_attn_x, dtype=np.float32))
    Wq = np.asarray(L0_attn_Wq, dtype=np.float32)
    Wk = np.asarray(L0_attn_Wk, dtype=np.float32)
    Wv = np.asarray(L0_attn_Wv, dtype=np.float32)
    Wr = np.asarray(L0_ffn_W_r, dtype=np.float32)
    Wkf = np.asarray(L0_ffn_W_k, dtype=np.float32)
    Wvf = np.asarray(L0_ffn_W_v, dtype=np.float32)
    mur = np.asarray(L0_ffn_mu_r, dtype=np.float32)
    muk = np.asarray(L0_ffn_mu_k, dtype=np.float32)

    wqk = np.ascontiguousarray(np.concatenate([Wq, Wk], axis=1))
    wkq = np.ascontiguousarray(np.concatenate([Wk, Wq], axis=1))
    wvv = np.ascontiguousarray(np.concatenate([Wv, Wv], axis=1))
    wrT = np.ascontiguousarray(Wr.T)
    wkT = np.ascontiguousarray(Wkf.T)
    wvT = np.ascontiguousarray(Wvf.T)
    mu_sb = np.ascontiguousarray(
        np.concatenate([mur.reshape(8, 128).T, muk.reshape(8, 128).T], axis=1))
    ident = np.ascontiguousarray(np.eye(128, dtype=np.float32))

    if "nc" not in _PROGRAM_CACHE:
        _PROGRAM_CACHE["nc"] = build_program()
    nc = _PROGRAM_CACHE["nc"]

    in_maps = []
    for c in range(8):
        b, h = c // 2, c % 2
        own = x[b, h * 2048:(h + 1) * 2048]
        other = x[b, (1 - h) * 2048:(2 - h) * 2048]
        xp = np.ascontiguousarray(np.concatenate([own, other], axis=0))
        in_maps.append({
            "x_in": xp,
            "wqk_in": wqk, "wkq_in": wkq, "wvv_in": wvv,
            "wrT_in": wrT, "wkT_in": wkT, "wvT_in": wvT,
            "mu_in": mu_sb,
            "hmask_in": np.array([[float(h)]], dtype=np.float32),
            "ident_in": ident,
        })

    res = run_bass_kernel_spmd(nc, in_maps, core_ids=list(range(8)))

    y = np.empty((4, S, C), dtype=np.float32)
    for c in range(8):
        b, h = c // 2, c % 2
        y[b, h * 2048:(h + 1) * 2048] = np.asarray(res.results[c]["y_out"])
    return y
